# version 13
# baseline (speedup 1.0000x reference)
import contextlib
import numpy as np
import ml_dtypes

import concourse.bass as bass
import concourse.mybir as mybir
from concourse.bass_utils import run_bass_kernel_spmd

F32 = mybir.dt.float32
BF16 = mybir.dt.bfloat16
I32 = mybir.dt.int32
AF = mybir.ActivationFunctionType
ALU = mybir.AluOpType
AXX = mybir.AxisListType.X

S = 4096
D = 1024
H = 16
KVH = 4
HD = 64
E = 8
FF = 2048
SPLIT = 2432
EPS_LN = 1e-5
EPS_RMS = float(np.finfo(np.float32).eps)
NEG_BIG = -1.0e30
SCAL_VALS = [EPS_LN, EPS_RMS, -0.5, 1.0 / HD, 0.125, -0.125, 1.0, -1.0]

EVENS = [0, 1, 2, 3, 8, 9, 10, 11]
ODDS = [4, 5, 6, 7, 12, 13, 14, 15]
SLOT_HEADS = []
for _j in range(8):
    SLOT_HEADS += [EVENS[_j], ODDS[_j]]

ENGS = ("sync", "tensor", "vector", "scalar", "gpsimd")


def tiles(lo, hi, step=512):
    return [(t0, min(step, hi - t0)) for t0 in range(lo, hi, step)]


class Prog:
    def __init__(self, counters=None):
        self.q = {e: [] for e in ENGS}
        self.c = dict(counters or {})

    def now(self, s):
        return self.c.get(s, 0)

    DMA_SEMS = ("qd", "qg", "qg2", "qgs", "qg2s", "qx", "qs", "qa0", "qa1",
                "qw1", "qw2", "qm0", "qm1", "qm2", "qm3", "qn0", "qn1",
                "qn2", "qn3")

    def op(self, eng, fn, waits=(), inc=None, inc_n=1):
        ws = []
        for (sname, v) in waits:
            if v <= 0:
                continue
            assert v <= self.c.get(sname, 0), (
                f"future wait {sname}={v} (now {self.c.get(sname, 0)})")
            if sname in self.DMA_SEMS:
                v = self.c.get(sname, 0)
            ws.append((sname, v))
        agg = {}
        for s, v in ws:
            agg[s] = max(agg.get(s, 0), v)
        ws = list(agg.items())
        post = None
        if inc is not None:
            self.c[inc] = self.c.get(inc, 0) + inc_n
            post = self.c[inc]

        def emit(engine, sems, _fn=fn, _ws=tuple(ws), _inc=inc, _inc_n=inc_n):
            for (sname, v) in _ws:
                engine.wait_ge(sems[sname], v)
            inst = _fn(engine)
            if _inc is not None:
                assert inst is not None
                inst.then_inc(sems[_inc], _inc_n)

        self.q[eng].append(emit)
        return post

    def allc(self):
        return [(k, v) for k, v in self.c.items()]


def _host_consts():
    bf = ml_dtypes.bfloat16
    c = {}
    c["c_scal"] = np.tile(np.array([[1e-05, float(np.finfo(np.float32).eps), -0.5,
                                     1.0 / 64, 0.125, -0.125, 1.0, -1.0]],
                                    dtype=np.float32), (128, 1))
    c["c_identf"] = np.eye(128, dtype=np.float32)
    c["c_ones"] = np.ones((128, 128), dtype=bf)
    c["c_zeros"] = np.zeros((128, 128), dtype=bf)
    o2 = np.zeros((128, 2), dtype=bf)
    o2[0:64, 0] = 1.0
    o2[64:128, 1] = 1.0
    c["c_ones2blk"] = o2
    oz = np.zeros((128, 4), dtype=bf)
    oz[:, 0] = 1.0
    oz[:, 3] = 1.0
    c["c_onesLN"] = oz
    s2 = np.zeros((2, 128), dtype=bf)
    s2[0, 0:64] = 1.0
    s2[1, 64:128] = 1.0
    c["c_sel2"] = s2
    r = np.zeros((64, 64), dtype=np.float32)
    for i in range(32):
        r[i, i + 32] = 1.0
        r[i + 32, i] = -1.0
    r2 = np.zeros((128, 128), dtype=np.float32)
    r2[0:64, 0:64] = r
    r2[64:128, 64:128] = r
    c["c_r2t"] = np.ascontiguousarray(r2.T).astype(bf)
    trim = np.zeros((128, 4 * 512), dtype=bf)
    kk = np.arange(128)[:, None]
    qq = np.arange(512)[None, :]
    for pat in range(4):
        trim[:, pat * 512:(pat + 1) * 512] = (qq >= kk + 128 * pat).astype(bf)
    c["c_tri"] = trim
    inv = 1.0 / (10000.0 ** (np.arange(0, HD, 2, dtype=np.float32) / HD))
    pos = np.arange(S, dtype=np.float32)
    fr = pos[:, None] * inv[None, :]
    c["c_cos"] = np.ascontiguousarray(np.tile(np.cos(fr).T, (4, 1))).astype(bf)
    c["c_sin"] = np.ascontiguousarray(np.tile(np.sin(fr).T, (4, 1))).astype(bf)
    return c


BO = dict(
    slab=0, wps=8192, kbuf=16384, vbuf=24576, probs=32896, yts=36992,
    qbuf=41088, vtmp=42112, cstile=43152, xbf=44192, xsq=44704, qfin=45216, qtmp=45728,
    qsqt=46240, recdb=46752, mrbf=47776, cvb=48800, tri=51872, r2t=53920,
    identb=54048,
)
BO["xln"] = 54176
NB = 58272
BO["hsc"] = 0
BO["ges"] = 12288
BO["w1s"] = 15360
BO["w2s"] = 19456
BO["rts"] = 23552
BO["gsb"] = 23616
BO["sutm"] = 26688
FO = dict(
    xs=0, x3s=4096, uts=8192, dtmp=12288, rtmp1=12800, rtmp2=13312,
    rtmp3=13824, mrow=14336, rr=15872, recds=16896, rbs=16896, x1t=17920,
    cvf=18432, identf=21504,
)
NF = 21632
FO["rls"] = 12288
FO["ltt"] = 12800
FO["lmx"] = 12808
FO["gtm"] = 12816
FO["outs"] = 13312
FO["gsr"] = 13824
FO["g8r"] = 15872


def build_program(debug_taps=(), phase_limit=99):
    nc = bass.Bass("TRN2", target_bir_lowering=False)

    din = {}

    def dinp(name, shape, dt):
        din[name] = nc.dram_tensor(name, list(shape), dt, kind="ExternalInput").ap()
        return din[name]

    xT = dinp("xT", (D, S), F32)
    role_in = dinp("role", (1, 1), I32)
    wqT = dinp("wqT", (D, D), F32)
    wkT = dinp("wkT", (D, KVH * HD), F32)
    wvT = dinp("wvT", (D, KVH * HD), F32)
    wpT = dinp("wpT", (D, D), F32)
    cattn = dinp("cattn", (D, 3 * D), F32)
    cproj = dinp("cproj", (D, D), F32)
    router = dinp("router", (D, E), F32)
    w1 = dinp("w1", (D, E * FF), F32)
    w2 = dinp("w2", (E * FF, D), F32)
    gain_row = dinp("gain_row", (1, H), F32)
    gain_pairs = dinp("gain_pairs", (2, 8), F32)
    ln1w_row = dinp("ln1w_row", (1, D), F32)
    ln2w_row = dinp("ln2w_row", (1, D), F32)
    for name, shape, dt in [
        ("c_scal", (128, 8), F32),
        ("c_identf", (128, 128), F32), ("c_ones", (128, 128), BF16),
        ("c_zeros", (128, 128), BF16),
        ("c_ones2blk", (128, 2), BF16), ("c_onesLN", (128, 4), BF16),
        ("c_sel2", (2, 128), BF16), ("c_r2t", (128, 128), BF16),
        ("c_tri", (128, 4 * 512), BF16), ("c_cos", (128, S), BF16),
        ("c_sin", (128, S), BF16),
    ]:
        dinp(name, shape, dt)

    out_tm = nc.dram_tensor("out_tm", [SPLIT, D], F32, kind="ExternalOutput").ap()
    dbg = {}
    for name, shape in debug_taps:
        dbg[name] = nc.dram_tensor(name, list(shape), F32, kind="ExternalOutput").ap()

    def scrt(name, shape, dt):
        return nc.dram_tensor(name, list(shape), dt).ap()

    wqTb = scrt("wqTb", (D, D), BF16)
    wkTb = scrt("wkTb", (D, KVH * HD), BF16)
    wvTb = scrt("wvTb", (D, KVH * HD), BF16)
    wpTb = scrt("wpTb", (D, D), BF16)
    cattnb = scrt("cattnb", (D, 3 * D), BF16)
    cprojb = scrt("cprojb", (D, D), BF16)
    routerb = scrt("routerb", (D, E), BF16)
    w1b = scrt("w1b", (D, E * FF), BF16)
    w2b = scrt("w2b", (E * FF, D), BF16)
    qO_d = scrt("qO_d", (D, S), BF16)
    kO_d = scrt("kO_d", (KVH * HD, S), BF16)
    vO_d = scrt("vO_d", (S, KVH * 65), BF16)
    x1_d = scrt("x1_d", (D, S), F32)
    h2_d = scrt("h2_d", (D, S), F32)
    qI_d = scrt("qI_d", (D, S), BF16)
    kI_d = scrt("kI_d", (D, S), BF16)
    vI_d = scrt("vI_d", (S, H * 65), BF16)
    su_d = scrt("su_d", (D, S), BF16)
    uln_d = scrt("uln_d", (D, S), BF16)
    g_d = scrt("g_d", (E, S), F32)
    bnc_d = scrt("bnc_d", (2, 16), F32)

    est = contextlib.ExitStack()
    with est:
        sbx = {}

        def sbuf(name, shape, dt):
            sbx[name] = est.enter_context(nc.sbuf_tensor(name, list(shape), dt))
            return sbx[name]

        arb = sbuf("arb", (128, NB), BF16)
        arf = sbuf("arf", (128, NF), F32)

        def B_(name, rows=128):
            return arb[0:rows, BO[name]:]

        def F_(name, rows=128):
            return arf[0:rows, FO[name]:]

        slab = arb[:, BO["slab"]:BO["slab"] + 8192]
        wps = arb[:, BO["wps"]:BO["wps"] + 8192]
        kbuf = arb[:, BO["kbuf"]:BO["kbuf"] + 8192]
        vbuf = arb[:, BO["vbuf"]:BO["vbuf"] + 8320]
        probs = arb[:, BO["probs"]:BO["probs"] + 4096]
        yts = arb[:, BO["yts"]:BO["yts"] + 4096]
        qbuf = arb[:, BO["qbuf"]:BO["qbuf"] + 1024]
        vtmp = arb[:, BO["vtmp"]:BO["vtmp"] + 1040]
        cstile = arb[:, BO["cstile"]:BO["cstile"] + 1040]
        xbf = arb[:, BO["xbf"]:BO["xbf"] + 512]
        xsq = arb[:, BO["xsq"]:BO["xsq"] + 512]
        qfin = arb[:, BO["qfin"]:BO["qfin"] + 512]
        qtmp = arb[:, BO["qtmp"]:BO["qtmp"] + 512]
        qsqt = arb[:, BO["qsqt"]:BO["qsqt"] + 512]
        recdb = arb[:, BO["recdb"]:BO["recdb"] + 1024]
        mrbf = arb[0:1, BO["mrbf"]:BO["mrbf"] + 1024]
        cvb = arb[:, BO["cvb"]:BO["cvb"] + 3072]
        tri = arb[:, BO["tri"]:BO["tri"] + 2048]
        r2t = arb[:, BO["r2t"]:BO["r2t"] + 128]
        identb = arb[:, BO["identb"]:BO["identb"] + 128]
        hsc = arb[:, BO["hsc"]:BO["hsc"] + 12288]
        ges = arb[:, BO["ges"]:BO["ges"] + 3072]
        w1s = arb[:, BO["w1s"]:BO["w1s"] + 4096]
        w2s = arb[:, BO["w2s"]:BO["w2s"] + 4096]
        sutm = arb[:, BO["sutm"]:BO["sutm"] + 3072]
        rts = arb[:, BO["rts"]:BO["rts"] + 64]
        gsb = arb[0:1, BO["gsb"]:BO["gsb"] + 3072]

        xs = arf[:, FO["xs"]:FO["xs"] + 4096]
        x3s = arf[:, FO["x3s"]:FO["x3s"] + 4096]
        uts = arf[:, FO["uts"]:FO["uts"] + 4096]
        dtmp = arf[:, FO["dtmp"]:FO["dtmp"] + 512]
        rtmp1 = arf[:, FO["rtmp1"]:FO["rtmp1"] + 512]
        rtmp2 = arf[:, FO["rtmp2"]:FO["rtmp2"] + 512]
        rtmp3 = arf[:, FO["rtmp3"]:FO["rtmp3"] + 512]
        mrow = arf[0:1, FO["mrow"]:FO["mrow"] + 1536]
        rr = arf[0:2, FO["rr"]:FO["rr"] + 1024]
        recds = arf[:, FO["recds"]:FO["recds"] + 1024]
        rbs = arf[0:64, FO["rbs"]:FO["rbs"] + 1024]
        x1t = arf[:, FO["x1t"]:FO["x1t"] + 512]
        cvf = arf[:, FO["cvf"]:FO["cvf"] + 3072]
        identf = arf[:, FO["identf"]:FO["identf"] + 128]
        rls = arf[0:8, FO["rls"]:FO["rls"] + 512]
        ltt = arf[:, FO["ltt"]:FO["ltt"] + 8]
        lmx = arf[:, FO["lmx"]:FO["lmx"] + 8]
        gtm = arf[:, FO["gtm"]:FO["gtm"] + 8]
        outs = arf[:, FO["outs"]:FO["outs"] + 512]
        gsr = arf[0:1, FO["gsr"]:FO["gsr"] + 3072]
        g8r = arf[0:8, FO["g8r"]:FO["g8r"] + 128]

        def xln_a():
            return arb[:, BO["xln"]:BO["xln"] + 4096]

        scalvec = sbuf("scalvec", (128, 8), F32)
        for _i, _v in enumerate(SCAL_VALS):
            nc.const_aps.register(F32, float(_v), scalvec[:, _i:_i + 1])
        role_s = sbuf("role_s", (1, 1), I32)
        onesb = sbuf("onesb", (128, 128), BF16)
        zerosb = sbuf("zerosb", (128, 128), BF16)
        ones2blk = sbuf("ones2blk", (128, 2), BF16)
        onesLN = sbuf("onesLN", (128, 4), BF16)
        sel2 = sbuf("sel2", (2, 128), BF16)
        gainr = sbuf("gainr", (1, H), F32)
        gainp = sbuf("gainp", (2, 8), F32)
        ln1wb = sbuf("ln1wb", (1, D), BF16)
        ln2wb = sbuf("ln2wb", (1, D), BF16)

        cbO = sbuf("cbO", (128, H), F32)
        cbI = sbuf("cbI", (128, H), F32)
        cvoutb = sbuf("cvoutb", (1, H), BF16)
        qmaxs = sbuf("qmaxs", (2, 8), F32)
        kmaxs = sbuf("kmaxs", (2, 8), F32)
        bncs = sbuf("bncs", (1, 96), F32)
        cvalb = sbuf("cvalb", (1, H), BF16)

        pb = [est.enter_context(nc.psum_tensor(f"pb{i}", [128, 512], F32))
              for i in range(2)]
        pq = [est.enter_context(nc.psum_tensor(f"pq{i}", [128, 1024], F32))
              for i in range(3)]
        for _q in range(3):
            for _h in range(2):
                pb.append(pq[_q][:, _h * 512:(_h + 1) * 512])

        sems = {n: est.enter_context(nc.semaphore(n))
                for n in ["qd", "qg", "qg2", "qgs", "qg2s", "qx", "qs", "qa0",
                          "qa1", "qw1", "qw2", "qm0", "qm1", "qm2", "qm3",
                          "qn0", "qn1", "qn2", "qn3", "pe", "act", "dve",
                          "pool"]}

        block = est.enter_context(nc.Block())

        P0 = Prog()

        def dma(p, eng, out, in_, waits=(), sem="qd", **kw):
            return p.op(eng, lambda e: e.dma_start(out=out, in_=in_, **kw),
                        waits=waits, inc=sem, inc_n=16)

        dma(P0, "sync", scalvec[:, :], din["c_scal"])
        e_role = dma(P0, "sync", role_s[:, :], role_in)
        e_consts = e_role
        for dst, srcn in [(identf, "c_identf"), (onesb, "c_ones"),
                          (zerosb, "c_zeros"),
                          (ones2blk, "c_ones2blk"), (onesLN, "c_onesLN"),
                          (sel2, "c_sel2"), (r2t, "c_r2t"), (tri, "c_tri"),
                          (gainr, "gain_row"), (gainp, "gain_pairs")]:
            e_consts = dma(P0, "sync", dst[:, :], din[srcn])
        e_l1 = dma(P0, "sync", cvf[0:1, 0:D], din["ln1w_row"])
        e_l2 = dma(P0, "sync", cvf[1:2, 0:D] if False else cvf[0:1, 0:D], din["ln1w_row"])
        e_c1 = P0.op("vector", lambda e: e.tensor_copy(ln1wb[:, :], cvf[0:1, 0:D]),
                     waits=[("qd", e_l1)], inc="dve")
        e_l2 = dma(P0, "sync", cvf[0:1, 0:D], din["ln2w_row"],
                   waits=[("dve", e_c1)])
        e_c2 = P0.op("vector", lambda e: e.tensor_copy(ln2wb[:, :], cvf[0:1, 0:D]),
                     waits=[("qd", e_l2)], inc="dve")

        e_cv0 = P0.op("scalar",
                      lambda e: e.activation(bncs[0:1, 0:H], gainr[0:1, :], AF.Abs),
                      waits=[("qd", e_consts)], inc="act")
        e_cv = P0.op("vector",
                     lambda e: e.tensor_scalar_mul(cvoutb[0:1, 0:H],
                                                   bncs[0:1, 0:H], -8.0),
                     waits=[("act", e_cv0)], inc="dve")
        e_cbm = P0.op("tensor",
                      lambda e: e.matmul(pb[7][:, 0:H], onesb[0:1, 0:128],
                                         cvoutb[0:1, 0:H], start=True, stop=True),
                      waits=[("dve", e_cv)], inc="pe")
        e_cbO = P0.op("scalar", lambda e: e.copy(cbO[:, :], pb[7][:, 0:H]),
                      waits=[("pe", e_cbm)], inc="act")

        conv_done = {}
        cv_guard = [None, None, None]
        ji = 0
        for (src, dst) in [(wqT, wqTb), (wkT, wkTb), (wvT, wvTb), (wpT, wpTb),
                           (cattn, cattnb), (cproj, cprojb), (router, routerb),
                           (w1, w1b), (w2, w2b)]:
            big = dst.tensor.name in ("w1b", "w2b")
            sm = "qg2" if big else "qg"
            sms = "qg2s" if big else "qgs"
            rows, cols = src.tensor.shape
            for r0 in range(0, rows, 128):
                rh = min(128, rows - r0)
                for c0 in range(0, cols, 1024):
                    cw = min(1024, cols - c0)
                    h = ji % 3
                    w_ = [cv_guard[h]] if cv_guard[h] else []
                    e_ld = dma(P0, "gpsimd", cvf[0:rh, h * 1024:h * 1024 + cw],
                               src[r0:r0 + rh, c0:c0 + cw],
                               waits=w_ + ([("dve", e_c2)] if ji < 3 else []),
                               sem=sm)
                    e_cp = P0.op("gpsimd",
                                 lambda e, rh=rh, h=h, cw=cw: e.tensor_copy(
                                     cvb[0:rh, h * 1024:h * 1024 + cw],
                                     cvf[0:rh, h * 1024:h * 1024 + cw]),
                                 waits=[(sm, e_ld)], inc="pool")
                    e_st = dma(P0, "gpsimd", dst[r0:r0 + rh, c0:c0 + cw],
                               cvb[0:rh, h * 1024:h * 1024 + cw],
                               waits=[("pool", e_cp)], sem=sm)
                    cv_guard[h] = (sm, e_st)
                    ji += 1
            conv_done[dst.tensor.name] = (sm, P0.now(sm))

        shared_counts = dict(P0.c)

        def build_role(role):
            p = Prog(shared_counts)
            if role == 0:
                own = (0, SPLIT)
                full = (0, SPLIT)
            else:
                own = (SPLIT, S)
                full = (0, S)
            t_full = tiles(*full)
            t_own = tiles(*own)
            t_moe = tiles(own[0], own[1], 384)

            G = {}

            def guard(key):
                return list(G.get(key, []))

            def setg(key, semname, cnt):
                G[key] = [(semname, cnt)]

            def addg(key, semname, cnt):
                G.setdefault(key, []).append((semname, cnt))

            def ln_tile(src_sb, w, dst_sb, wrow_b, out_f32_sb=None, eps=EPS_LN,
                        src_ready=()):
                for c in range(8):
                    e_bf = p.op("vector",
                                lambda e, c=c, w=w: e.tensor_copy(
                                    xbf[:, 0:w], src_sb[:, c * 512:c * 512 + w]),
                                waits=list(src_ready) + guard(("xbf",)), inc="dve")
                    e_sq = p.op("scalar",
                                lambda e, c=c, w=w: e.activation(
                                    xsq[:, 0:w], src_sb[:, c * 512:c * 512 + w],
                                    AF.Square),
                                waits=list(src_ready) + guard(("xsq",)), inc="act")
                    e_m1 = p.op("tensor",
                                lambda e, c=c, w=w: e.matmul(
                                    pb[2][0:1, 0:w], onesb[:, 0:1], xbf[:, 0:w],
                                    start=(c == 0), stop=(c == 7)),
                                waits=[("dve", e_bf)] +
                                      (guard(("bank", 2)) if c == 0 else []),
                                inc="pe")
                    setg(("xbf",), "pe", e_m1)
                    e_m2 = p.op("tensor",
                                lambda e, c=c, w=w: e.matmul(
                                    pb[3][0:1, 0:w], onesb[:, 0:1], xsq[:, 0:w],
                                    start=(c == 0), stop=(c == 7)),
                                waits=[("act", e_sq)] +
                                      (guard(("bank", 3)) if c == 0 else []),
                                inc="pe")
                    setg(("xsq",), "pe", e_m2)
                e_mm = p.now("pe")
                e_m = p.op("vector",
                           lambda e, w=w: e.tensor_scalar_mul(mrow[0:1, 0:w],
                                                              pb[2][0:1, 0:w], 1.0 / D),
                           waits=[("pe", e_mm)] + guard(("mrow",)), inc="dve")
                e_ex = p.op("vector",
                            lambda e, w=w: e.tensor_scalar_mul(mrow[0:1, 512:512 + w],
                                                               pb[3][0:1, 0:w], 1.0 / D),
                            inc="dve")
                setg(("bank", 2), "dve", e_ex)
                setg(("bank", 3), "dve", e_ex)
                e_m2r = p.op("vector",
                             lambda e, w=w: e.tensor_mul(mrow[0:1, 1024:1024 + w],
                                                         mrow[0:1, 0:w], mrow[0:1, 0:w]),
                             waits=[("dve", e_m)], inc="dve")
                e_var = p.op("vector",
                             lambda e, w=w: e.tensor_sub(mrow[0:1, 512:512 + w],
                                                         mrow[0:1, 512:512 + w],
                                                         mrow[0:1, 1024:1024 + w]),
                             waits=[("dve", e_m2r)], inc="dve")
                e_lnv = p.op("scalar",
                             lambda e, w=w, eps=eps: e.activation(
                                 mrow[0:1, 1024:1024 + w], mrow[0:1, 512:512 + w],
                                 AF.Ln, bias=eps),
                             waits=[("dve", e_var)], inc="act")
                e_rst = p.op("scalar",
                             lambda e, w=w: e.activation(mrow[0:1, 512:512 + w],
                                                         mrow[0:1, 1024:1024 + w],
                                                         AF.Exp, scale=-0.5),
                             waits=[("act", e_lnv)], inc="act")
                e_mb = p.op("vector",
                            lambda e, w=w: e.tensor_copy(mrbf[0:1, 0:w], mrow[0:1, 0:w]),
                            waits=[("dve", e_var)] + guard(("mrbf",)), inc="dve")
                e_rb = p.op("vector",
                            lambda e, w=w: e.tensor_copy(mrbf[0:1, 512:512 + w],
                                                         mrow[0:1, 512:512 + w]),
                            waits=[("act", e_rst)], inc="dve")
                setg(("mrow",), "dve", e_rb)
                e_bm = p.op("tensor",
                            lambda e, w=w: e.matmul(pb[3][:, 0:w], onesb[0:1, 0:128],
                                                    mrbf[0:1, 0:w], start=True, stop=True),
                            waits=[("dve", e_mb)] + guard(("bank", 3)), inc="pe")
                last = None
                for c in range(8):
                    wr_ap = (onesb[0:1, 0:128] if wrow_b is None
                             else wrow_b[0:1, c * 128:(c + 1) * 128])
                    dslot = dtmp if c % 2 == 0 else outs
                    dkey = ("dtmp", c % 2)
                    e_br = p.op("tensor",
                                lambda e, c=c, w=w, wr_ap=wr_ap: e.matmul(
                                    pb[4][:, 0:w], wr_ap,
                                    mrbf[0:1, 512:512 + w], start=True, stop=True),
                                waits=[("dve", e_rb)] + guard(("bank", 4)), inc="pe")
                    e_d = p.op("vector",
                               lambda e, c=c, w=w, dslot=dslot: e.tensor_sub(
                                   dslot[:, 0:w], src_sb[:, c * 512:c * 512 + w],
                                   pb[3][:, 0:w]),
                               waits=[("pe", e_bm), ("pe", e_br)] + guard(dkey),
                               inc="dve")
                    e_a = p.op("vector",
                               lambda e, c=c, w=w, dslot=dslot: e.tensor_mul(
                                   dst_sb[:, c * 512:c * 512 + w], dslot[:, 0:w],
                                   pb[4][:, 0:w]),
                               waits=[("dve", e_d)], inc="dve")
                    if out_f32_sb is not None:
                        e_a = p.op("vector",
                                   lambda e, c=c, w=w, dslot=dslot: e.tensor_mul(
                                       out_f32_sb[:, c * 512:c * 512 + w],
                                       dslot[:, 0:w], pb[4][:, 0:w]),
                                   inc="dve")
                    setg(dkey, "dve", e_a)
                    setg(("bank", 4), "dve", e_a)
                    last = e_a
                setg(("bank", 3), "dve", last)
                setg(("mrbf",), "dve", last)
                return last

            def rope_norm(kind, cpair, t0, w, bank, dests, e_qmm):
                e_raw = p.op("scalar",
                             lambda e, w=w, bank=bank: e.copy(qtmp[:, 0:w],
                                                              pb[bank][:, 0:w]),
                             waits=[("pe", e_qmm)] + guard(("qtmp",)), inc="act")
                e_sq = p.op("scalar",
                            lambda e, w=w, bank=bank: e.activation(
                                qsqt[:, 0:w], pb[bank][:, 0:w], AF.Square),
                            waits=[("pe", e_qmm)] + guard(("qsqt",)), inc="act")
                setg(("bank", bank), "act", e_sq)
                e_ss = p.op("tensor",
                            lambda e, w=w: e.matmul(pb[5][0:2, 0:w], ones2blk[:, :],
                                                    qsqt[:, 0:w], start=True, stop=True),
                            waits=[("act", e_sq)] + guard(("bank", 5)), inc="pe")
                setg(("qsqt",), "pe", e_ss)
                e_l = p.op("scalar",
                           lambda e, w=w: e.activation(rr[0:2, 0:w], pb[5][0:2, 0:w],
                                                       AF.Ln, bias=EPS_RMS,
                                                       scale=1.0 / HD),
                           waits=[("pe", e_ss)] + guard(("rr",)), inc="act")
                setg(("bank", 5), "act", e_l)
                e_r = p.op("scalar",
                           lambda e, w=w: e.activation(rr[0:2, 512:512 + w],
                                                       rr[0:2, 0:w], AF.Exp, scale=-0.5),
                           waits=[("act", e_l)], inc="act")
                setg(("rr",), "act", e_r)
                if kind == "q":
                    e_g = p.op("vector",
                               lambda e, w=w, cpair=cpair: e.tensor_scalar_mul(
                                   xbf[0:2, 0:w], rr[0:2, 512:512 + w],
                                   gainp[:, cpair:cpair + 1]),
                               waits=[("act", e_r)] + guard(("xbf",)), inc="dve")
                else:
                    e_g = p.op("vector",
                               lambda e, w=w: e.tensor_copy(xbf[0:2, 0:w],
                                                            rr[0:2, 512:512 + w]),
                               waits=[("act", e_r)] + guard(("xbf",)), inc="dve")
                e_bc = p.op("tensor",
                            lambda e, w=w: e.matmul(pb[6][:, 0:w], sel2[:, :],
                                                    xbf[0:2, 0:w], start=True, stop=True),
                            waits=[("dve", e_g)] + guard(("bank", 6)), inc="pe")
                setg(("xbf",), "pe", e_bc)
                e_rot = p.op("tensor",
                             lambda e, w=w: e.matmul(pb[7][:, 0:w], r2t[:, :],
                                                     qtmp[:, 0:w], start=True, stop=True),
                             waits=[("act", e_raw)] + guard(("bank", 7)), inc="pe")
                e_t1 = p.op("vector",
                            lambda e, w=w: e.tensor_mul(
                                rtmp1[:, 0:w], qtmp[:, 0:w], cstile[:, 0:w]),
                            waits=[("act", e_raw), ("qx", 1)] + guard(("rtmp1",)), inc="dve")
                setg(("qtmp",), "dve", e_t1)
                e_t2 = p.op("vector",
                            lambda e, w=w: e.tensor_mul(
                                rtmp2[:, 0:w], pb[7][:, 0:w], cstile[:, 520:520 + w]),
                            waits=[("pe", e_rot)] + guard(("rtmp2",)), inc="dve")
                setg(("bank", 7), "dve", e_t2)
                e_t3 = p.op("vector",
                            lambda e, w=w: e.tensor_add(rtmp3[:, 0:w], rtmp1[:, 0:w],
                                                        rtmp2[:, 0:w]),
                            waits=[("dve", e_t1), ("dve", e_t2)] + guard(("rtmp3",)),
                            inc="dve")
                setg(("rtmp1",), "dve", e_t3)
                setg(("rtmp2",), "dve", e_t3)
                e_f = None
                for (r0, _, _) in dests:
                    e_f = p.op("vector",
                               lambda e, r0=r0, w=w: e.tensor_mul(
                                   qfin[r0:r0 + 64, 0:w], rtmp3[r0:r0 + 64, 0:w],
                                   pb[6][r0:r0 + 64, 0:w]),
                               waits=[("dve", e_t3), ("pe", e_bc)] + guard(("qfin",)),
                               inc="dve")
                setg(("rtmp3",), "dve", e_f)
                setg(("bank", 6), "dve", e_f)
                e_st = None
                for (r0, dram, dr0) in dests:
                    e_st = dma(p, "sync", dram[dr0:dr0 + 64, t0:t0 + w],
                               qfin[r0:r0 + 64, 0:w], waits=[("dve", e_f)],
                               sem="qs")
                setg(("qfin",), "qs", e_st)

            e_wk = dma(p, "sync", slab[:, 0:2048].rearrange("p (c n) -> p c n", n=256),
                       wkTb.rearrange("(c p) n -> p c n", p=128),
                       waits=[conv_done["wkTb"]], sem="qw1")
            e_wv = dma(p, "sync", slab[:, 2048:4096].rearrange("p (c n) -> p c n", n=256),
                       wvTb.rearrange("(c p) n -> p c n", p=128),
                       waits=[conv_done["wvTb"]], sem="qw1")

            def p2_tile(t0, w):
                e_cs = dma(p, "sync", cstile[:, 0:w], din["c_cos"][:, t0:t0 + w],
                           waits=guard(("cstile",)), sem="qx")
                e_cs = dma(p, "sync", cstile[:, 520:520 + w],
                           din["c_sin"][:, t0:t0 + w], sem="qx")
                e_x = dma(p, "sync",
                          xs.rearrange("p (c n) -> p c n", n=512)[:, :, 0:w],
                          xT.rearrange("(c p) n -> p c n", p=128)[:, :, t0:t0 + w],
                          waits=guard(("xs",)), sem="qx")
                ln_tile(xs, w, xln_a(), None, src_ready=[("qx", e_x)])
                e_ln = p.now("dve")
                setg(("xs",), "dve", e_ln)
                rope_pend = []

                def flush_rope():
                    while rope_pend:
                        args = rope_pend.pop(0)
                        rope_norm(*args)

                for half in range(2):
                    e_wq = dma(p, "sync",
                               slab[:, 4096:8192].rearrange("p (c n) -> p c n", n=512),
                               wqTb.rearrange("(c p) n -> p c n", p=128)[
                                   :, :, half * 512:half * 512 + 512],
                               waits=[conv_done["wqTb"]] + guard(("wqh",)),
                               sem="qw1")
                    for cc in range(4):
                        c = 4 * half + cc
                        bank = c % 2
                        for d_ in range(8):
                            p.op("tensor",
                                 lambda e, cc=cc, d_=d_, bank=bank, w=w: e.matmul(
                                     pb[bank][:, 0:w],
                                     slab[:, 4096 + d_ * 512 + 128 * cc:
                                          4096 + d_ * 512 + 128 * cc + 128],
                                     xln_a()[:, d_ * 512: d_ * 512 + w],
                                     start=(d_ == 0), stop=(d_ == 7)),
                                 waits=([("dve", e_ln), ("qw1", e_wq)] +
                                        guard(("bank", bank)) if d_ == 0 else []),
                                 inc=("pe" if d_ == 7 else None))
                        rope_pend.append(("q", c, t0, w, bank,
                                          [(0, qO_d, 128 * c), (64, qO_d, 128 * c + 64)],
                                          p.now("pe")))
                        if len(rope_pend) > 1:
                            rope_norm(*rope_pend.pop(0))
                    flush_rope()
                    setg(("wqh",), "pe", p.now("pe"))
                for ck in range(2):
                    bank = ck % 2
                    for d_ in range(8):
                        p.op("tensor",
                             lambda e, ck=ck, d_=d_, bank=bank, w=w: e.matmul(
                                 pb[bank][:, 0:w],
                                 slab[:, d_ * 256 + 128 * ck: d_ * 256 + 128 * ck + 128],
                                 xln_a()[:, d_ * 512: d_ * 512 + w],
                                 start=(d_ == 0), stop=(d_ == 7)),
                             waits=([("qw1", e_wk)] + guard(("bank", bank))
                                    if d_ == 0 else []),
                             inc=("pe" if d_ == 7 else None))
                    rope_pend.append(("k", ck, t0, w, bank,
                                      [(0, kO_d, 128 * ck), (64, kO_d, 128 * ck + 64)],
                                      p.now("pe")))
                    if len(rope_pend) > 1:
                        rope_norm(*rope_pend.pop(0))
                flush_rope()
                ntok = (w + 127) // 128
                for s_ in range(ntok):
                    tw = min(128, w - s_ * 128)
                    bank = s_ % 2
                    for d_ in range(8):
                        p.op("tensor",
                             lambda e, s_=s_, d_=d_, bank=bank, tw=tw: e.matmul(
                                 pb[bank][0:tw, 0:256],
                                 xln_a()[:, d_ * 512 + s_ * 128: d_ * 512 + s_ * 128 + tw],
                                 slab[:, 2048 + d_ * 256: 2048 + d_ * 256 + 256],
                                 start=(d_ == 0), stop=(d_ == 7)),
                             waits=([("qw1", e_wv)] + guard(("bank", bank))
                                    if d_ == 0 else []),
                             inc=("pe" if d_ == 7 else None))
                    e_vm = p.now("pe")
                    e_ms = p.op("vector",
                                lambda e, tw=tw: e.memset(vtmp[0:tw, 0:260], 1.0),
                                waits=guard(("vtmp",)), inc="dve")
                    e_vc = p.op("vector",
                                lambda e, tw=tw, bank=bank: e.tensor_copy(
                                    vtmp[0:tw, 0:260].rearrange(
                                        "p (g x) -> p g x", x=65)[:, :, 0:64],
                                    pb[bank][0:tw, 0:256].rearrange(
                                        "p (g x) -> p g x", x=64)),
                                waits=[("dve", e_ms), ("pe", e_vm)], inc="dve")
                    setg(("bank", bank), "dve", e_vc)
                    e_vs = dma(p, "sync", vO_d[t0 + s_ * 128: t0 + s_ * 128 + tw, :],
                               vtmp[0:tw, 0:260], waits=[("dve", e_vc)], sem="qs")
                    setg(("vtmp",), "qs", e_vs)
                setg(("cstile",), "dve", p.now("dve"))

            if phase_limit == 2:
                for (t0, w) in t_full:
                    p2_tile(t0, w)

            setg(("kbuf", 0), "dve", p.now("dve"))
            setg(("kbuf", 1), "dve", p.now("dve"))

            def attention_block(t_list, qsrc, vsrc, k_load, cb_sb, cb_col, resid_fn,
                                interleave=None):
                def emit_loads(ti, t0, w, j):
                    kext = ((t0 + w + 127) // 128) * 128
                    nkc = kext // 128
                    jb = j % 2
                    sm = f"qa{jb}"
                    e_k = k_load(j, jb, kext)
                    vcol = 65 * (EVENS[j] // 4) if vsrc is vO_d else 130 * j
                    e_v = dma(p, "sync",
                              vbuf.rearrange("p (kc cc) -> p kc cc", cc=130)[
                                  :, jb * 32: jb * 32 + nkc, :],
                              vsrc.rearrange("(kc pp) cc -> pp kc cc", pp=128)[
                                  :, 0:nkc, vcol:vcol + 130],
                              waits=[("qs", 1)] + guard(("vbuf", jb)), sem=sm)
                    e_q = dma(p, "sync", qbuf[:, jb * 512: jb * 512 + w],
                              qsrc[128 * j:128 * j + 128, t0:t0 + w],
                              waits=[("qs", 1)] + guard(("qbuf", jb)), sem=sm)
                    return (sm, e_k, e_v, e_q)

                pend = {}
                for ti, (t0, w) in enumerate(t_list):
                    if interleave is not None:
                        interleave(ti)
                    kext = ((t0 + w + 127) // 128) * 128
                    nkc = kext // 128
                    pend[0] = emit_loads(ti, t0, w, 0)
                    pend[1] = emit_loads(ti, t0, w, 1)
                    for j in range(8):
                        jb = j % 2
                        sm, e_k, e_v, e_q = pend.pop(j)
                        ca = cb_col(j, 0, ti)
                        cb_ = cb_col(j, 1, ti)
                        nkp = (nkc + 1) // 2
                        dep = {}
                        last_ya = last_yb = None

                        def quad_of(kp, hb):
                            return (2 * kp + hb) % 3

                        for kp in range(nkp + 1):
                            if kp < nkp:
                                ks = [2 * kp] + ([2 * kp + 1]
                                                 if 2 * kp + 1 < nkc else [])
                                qa, qb = quad_of(kp, 0), quad_of(kp, 1)
                                for k in ks:
                                    kh = k % 2
                                    e_sa = p.op("tensor",
                                                lambda e, k=k, kh=kh, qa=qa, jb=jb, w=w: e.matmul(
                                                    pq[qa][:, kh * 512: kh * 512 + w],
                                                    kbuf[0:64, jb * 4096 + 128 * k:
                                                         jb * 4096 + 128 * k + 128],
                                                    qbuf[0:64, jb * 512: jb * 512 + w],
                                                    start=True, stop=True),
                                                waits=[(sm, e_k), (sm, e_q)] +
                                                      (guard(("bank", 2 + 2 * qa)) +
                                                       guard(("bank", 3 + 2 * qa))
                                                       if k == ks[0] else []) +
                                                      (guard(("bank", 0)) if k == 0 else []),
                                                inc="pe")
                                    e_sb = p.op("tensor",
                                                lambda e, k=k, kh=kh, qb=qb, jb=jb, w=w: e.matmul(
                                                    pq[qb][:, kh * 512: kh * 512 + w],
                                                    kbuf[64:128, jb * 4096 + 128 * k:
                                                         jb * 4096 + 128 * k + 128],
                                                    qbuf[64:128, jb * 512: jb * 512 + w],
                                                    start=True, stop=True,
                                                    tile_position=(64, 0)),
                                                waits=(guard(("bank", 2 + 2 * qb)) +
                                                       guard(("bank", 3 + 2 * qb))
                                                       if k == ks[0] else []) +
                                                      (guard(("bank", 1)) if k == 0 else []),
                                                inc="pe")
                                e_s_last = p.now("pe")
                                kw = 512 * (len(ks) - 1) + w
                                ca = cb_col(j, 0, ti)
                                cb_ = cb_col(j, 1, ti)
                                pslot_a = kp % 2
                                pslot_b = 2 + kp % 2
                                e_ea = p.op("scalar",
                                            lambda e, qa=qa, kw=kw, pslot_a=pslot_a, ca=ca: e.activation(
                                                probs[:, pslot_a * 1024: pslot_a * 1024 + kw],
                                                pq[qa][:, 0:kw], AF.Exp, scale=0.125,
                                                bias=cb_sb[:, ca:ca + 1]),
                                            waits=[("pe", e_s_last)] +
                                                  guard(("probs", pslot_a)),
                                            inc="act")
                                setg(("bank", 2 + 2 * qa), "act", e_ea)
                                setg(("bank", 3 + 2 * qa), "act", e_ea)
                                e_eb = p.op("scalar",
                                            lambda e, qb=qb, kw=kw, pslot_b=pslot_b, cb_=cb_: e.activation(
                                                probs[:, pslot_b * 1024: pslot_b * 1024 + kw],
                                                pq[qb][:, 0:kw], AF.Exp, scale=0.125,
                                                bias=cb_sb[:, cb_:cb_ + 1]),
                                            waits=[("pe", e_s_last)] +
                                                  guard(("probs", pslot_b)),
                                            inc="act")
                                setg(("bank", 2 + 2 * qb), "act", e_eb)
                                setg(("bank", 3 + 2 * qb), "act", e_eb)
                                deps_a = ("act", e_ea)
                                deps_b = ("act", e_eb)
                                for k in ks:
                                    dd = t0 - 128 * k
                                    if -w < dd < 128:
                                        pi = (-dd) // 128
                                        kh = k % 2
                                        e_ma = p.op("vector",
                                                    lambda e, pslot_a=pslot_a, kh=kh, pi=pi, w=w: e.tensor_mul(
                                                        probs[:, pslot_a * 1024 + kh * 512:
                                                              pslot_a * 1024 + kh * 512 + w],
                                                        probs[:, pslot_a * 1024 + kh * 512:
                                                              pslot_a * 1024 + kh * 512 + w],
                                                        tri[:, pi * 512: pi * 512 + w]),
                                                    waits=[("act", e_ea)], inc="dve")
                                        e_mb = p.op("vector",
                                                    lambda e, pslot_b=pslot_b, kh=kh, pi=pi, w=w: e.tensor_mul(
                                                        probs[:, pslot_b * 1024 + kh * 512:
                                                              pslot_b * 1024 + kh * 512 + w],
                                                        probs[:, pslot_b * 1024 + kh * 512:
                                                              pslot_b * 1024 + kh * 512 + w],
                                                        tri[:, pi * 512: pi * 512 + w]),
                                                    waits=[("act", e_eb)], inc="dve")
                                        deps_a = ("dve", e_ma)
                                        deps_b = ("dve", e_mb)
                                dep[kp] = (ks, pslot_a, pslot_b, deps_a, deps_b)
                            kpp = kp - 1
                            if kpp >= 0:
                                ksp, psa, psb, da, db = dep.pop(kpp)
                                for k in ksp:
                                    kh = k % 2
                                    w_ya = [da, (sm, e_v)]
                                    if k == 0:
                                        w_ya += guard(("bank", 0))
                                    last_ya = p.op("tensor",
                                                   lambda e, k=k, kh=kh, psa=psa, jb=jb,
                                                          w=w, nkc=nkc: e.matmul(
                                                       pb[0][0:65, 0:w],
                                                       vbuf[:, jb * 4160 + k * 130:
                                                            jb * 4160 + k * 130 + 65],
                                                       probs[:, psa * 1024 + kh * 512:
                                                             psa * 1024 + kh * 512 + w],
                                                       start=(k == 0),
                                                       stop=(k == nkc - 1)),
                                                   waits=w_ya, inc="pe")
                                    w_yb = [db]
                                    if k == 0:
                                        w_yb += guard(("bank", 1))
                                    last_yb = p.op("tensor",
                                                   lambda e, k=k, kh=kh, psb=psb, jb=jb,
                                                          w=w, nkc=nkc: e.matmul(
                                                       pb[1][0:65, 0:w],
                                                       vbuf[:, jb * 4160 + k * 130 + 65:
                                                            jb * 4160 + k * 130 + 130],
                                                       probs[:, psb * 1024 + kh * 512:
                                                             psb * 1024 + kh * 512 + w],
                                                       start=(k == 0),
                                                       stop=(k == nkc - 1)),
                                                   waits=w_yb, inc="pe")
                                setg(("probs", psa), "pe", last_ya)
                                setg(("probs", psb), "pe", last_yb)
                        setg(("vbuf", jb), "pe", last_yb)
                        setg(("qbuf", jb), "pe", last_yb)
                        setg(("kbuf", jb), "pe", last_yb)
                        if j + 2 < 8:
                            pend[j + 2] = emit_loads(ti, t0, w, j + 2)
                        e_yn_last = None
                        for hb in range(2):
                            e_rc = p.op("scalar",
                                        lambda e, hb=hb, w=w: e.activation(
                                            recds[64:65, hb * 512:hb * 512 + w],
                                            pb[hb][64:65, 0:w], AF.Ln),
                                        waits=[("pe", last_ya), ("pe", last_yb)] +
                                              guard(("recds", hb)),
                                        inc="act")
                            e_rcb = p.op("scalar",
                                         lambda e, hb=hb, w=w: e.activation(
                                             recdb[64:65, hb * 512:hb * 512 + w],
                                             recds[64:65, hb * 512:hb * 512 + w],
                                             AF.Exp, scale=-1.0),
                                         waits=[("act", e_rc)] + guard(("recdb", hb)),
                                         inc="act")
                            setg(("recds", hb), "act", e_rcb)
                            e_rbm = p.op("tensor",
                                         lambda e, hb=hb, w=w: e.matmul(
                                             pb[6][0:64, 0:w], onesb[64:65, 0:64],
                                             recdb[64:65, hb * 512:hb * 512 + w],
                                             start=True, stop=True),
                                         waits=[("act", e_rcb)] + guard(("bank", 6)),
                                         inc="pe")
                            setg(("recdb", hb), "pe", e_rbm)
                            e_rbe = p.op("scalar",
                                         lambda e, hb=hb, w=w: e.copy(
                                             rbs[0:64, hb * 512:hb * 512 + w],
                                             pb[6][0:64, 0:w]),
                                         waits=[("pe", e_rbm)] + guard(("rbs", hb)),
                                         inc="act")
                            setg(("bank", 6), "act", e_rbe)
                            e_yn = p.op("vector",
                                        lambda e, hb=hb, j=j, w=w: e.tensor_mul(
                                            yts[:, j * 512:j * 512 + w][
                                                64 * hb:64 * hb + 64, :],
                                            pb[hb][0:64, 0:w],
                                            rbs[0:64, hb * 512:hb * 512 + w]),
                                        waits=[("act", e_rbe)] + guard(("yts", j)),
                                        inc="dve")
                            setg(("rbs", hb), "dve", e_yn)
                            setg(("bank", hb), "dve", e_yn)
                            e_yn_last = e_yn
                        setg(("yts", j), "dve", e_yn_last)
                    resid_fn(ti, t0, w)

            def k_load_outer(j, jb, kext):
                ga = EVENS[j] // 4
                gb = ODDS[j] // 4
                dma(p, "sync", kbuf[0:64, jb * 4096: jb * 4096 + kext],
                    kO_d[64 * ga:64 * ga + 64, 0:kext],
                    waits=[("qs", 1)] + guard(("kbuf", jb)), sem=f"qa{jb}")
                return dma(p, "sync", kbuf[64:128, jb * 4096: jb * 4096 + kext],
                           kO_d[64 * gb:64 * gb + 64, 0:kext], sem=f"qa{jb}")

            def outer_resid(ti, t0, w):
                e_xr = dma(p, "sync",
                           x3s.rearrange("p (c n) -> p c n", n=512)[:, :, 0:w],
                           xT.rearrange("(c p) n -> p c n", p=128)[:, :, t0:t0 + w],
                           waits=guard(("x3s",)), sem="qx")
                yts_done = p.now("dve")
                for c in range(8):
                    bank = 2 + c % 2
                    for jj in range(8):
                        p.op("tensor",
                             lambda e, c=c, jj=jj, bank=bank, w=w: e.matmul(
                                 pb[bank][:, 0:w],
                                 wps[:, jj * 1024 + 128 * c: jj * 1024 + 128 * c + 128],
                                 yts[:, jj * 512:jj * 512 + w],
                                 start=(jj == 0), stop=(jj == 7)),
                             waits=([("qw1", e_wp2), ("dve", yts_done)] +
                                    guard(("bank", bank)) if jj == 0 else []),
                             inc=("pe" if jj == 7 else None))
                    e_pm = p.now("pe")
                    e_r = p.op("vector",
                               lambda e, c=c, bank=bank, w=w: e.scalar_tensor_tensor(
                                   x1t[:, 0:w], pb[bank][:, 0:w], 1.0,
                                   x3s[:, c * 512:c * 512 + w], ALU.mult, ALU.add),
                               waits=[("pe", e_pm), ("qx", e_xr)] + guard(("x1t",)),
                               inc="dve")
                    setg(("bank", bank), "dve", e_r)
                    e_st = dma(p, "sync", x1_d[128 * c: 128 * c + 128, t0:t0 + w],
                               x1t[:, 0:w], waits=[("dve", e_r)], sem="qs")
                    setg(("x1t",), "qs", e_st)
                setg(("x3s",), "dve", p.now("dve"))
                for j in range(8):
                    setg(("yts", j), "pe", p.now("pe"))

            if phase_limit >= 3:
                e_wp2 = dma(p, "sync", wps.rearrange("p (c n) -> p c n", n=1024),
                            wpTb.rearrange("(c p) n -> p c n", p=128),
                            waits=[conv_done["wpTb"]], sem="qw1")
                attention_block(t_full, qO_d, vO_d, k_load_outer, cbO,
                                lambda j, hb, ti: 2 * j + hb, outer_resid,
                                interleave=lambda ti: p2_tile(*t_full[ti]))

            if phase_limit >= 4:
                p.op("vector", lambda e: e.memset(qmaxs[:, :], 0.0), inc="dve")
                p.op("vector", lambda e: e.memset(kmaxs[:, :], 0.0), inc="dve")
                for ti, (t0, w) in enumerate(t_full):
                    e_x1 = dma(p, "sync",
                               xs.rearrange("p (c n) -> p c n", n=512)[:, :, 0:w],
                               x1_d.rearrange("(c p) n -> p c n", p=128)[:, :, t0:t0 + w],
                               waits=[("qs", 1)] + guard(("xs",)), sem="qx")
                    uts_g = []
                    for c in range(8):
                        uts_g += guard(("uts", c))
                    ln_tile(xs, w, xln_a(), None, out_f32_sb=uts,
                            src_ready=[("qx", e_x1)] + uts_g)
                    e_h2 = p.now("dve")
                    setg(("xs",), "dve", e_h2)
                    lo = max(t0, own[0]) - t0
                    hi = min(t0 + w, own[1]) - t0
                    e_h2s = 0
                    if hi > lo:
                        for c in range(8):
                            e_h2s = dma(p, "sync",
                                        h2_d[128 * c:128 * c + 128, t0 + lo:t0 + hi],
                                        uts[:, c * 512 + lo: c * 512 + hi],
                                        waits=[("dve", e_h2)], sem="qs")
                    ln_tile(uts, w, xln_a(), ln1wb, src_ready=[("dve", e_h2)])
                    e_z = p.now("dve")
                    for c in range(8):
                        setg(("uts", c), "dve", e_z)
                        if e_h2s:
                            addg(("uts", c), "qs", e_h2s)
                    q_lo, q_hi = lo, hi
                    for b_ in range(6):
                        is_q = b_ < 2
                        is_k = 2 <= b_ < 4
                        if is_q and q_hi <= q_lo:
                            continue
                        hh = b_ % 2
                        e_sl = dma(p, "sync",
                                   slab.rearrange("p (h c n) -> p h c n", h=2, n=512)[
                                       :, hh, :, :],
                                   cattnb.rearrange("(c p) n -> p c n", p=128)[
                                       :, :, b_ * 512:(b_ + 1) * 512],
                                   waits=[conv_done["cattnb"]] +
                                         guard(("slab", hh)), sem="qw1")
                        if is_q or is_k:
                            o0, o1_ = (q_lo, q_hi) if is_q else (0, w)
                            ww = o1_ - o0
                            for sub in range(4):
                                bank = sub % 2
                                for d_ in range(8):
                                    p.op("tensor",
                                         lambda e, sub=sub, d_=d_, bank=bank, hh=hh,
                                                o0=o0, ww=ww: e.matmul(
                                             pb[bank][:, 0:ww],
                                             slab[:, hh * 4096 + d_ * 512 + 128 * sub:
                                                  hh * 4096 + d_ * 512 + 128 * sub + 128],
                                             xln_a()[:, d_ * 512 + o0: d_ * 512 + o0 + ww],
                                             start=(d_ == 0), stop=(d_ == 7)),
                                         waits=([("dve", e_z), ("qw1", e_sl)] +
                                                guard(("bank", bank)) if d_ == 0 else []),
                                         inc=("pe" if d_ == 7 else None))
                                e_mm = p.now("pe")
                                gd = b_ * 512 + 128 * sub
                                e_ev = p.op("scalar",
                                            lambda e, ww=ww, bank=bank: e.copy(
                                                qtmp[:, 0:ww], pb[bank][:, 0:ww]),
                                            waits=[("pe", e_mm)] + guard(("qtmp",)),
                                            inc="act")
                                e_sq2 = p.op("vector",
                                             lambda e, ww=ww: e.tensor_mul(
                                                 qsqt[:, 0:ww], qtmp[:, 0:ww],
                                                 qtmp[:, 0:ww]),
                                             waits=[("act", e_ev)] + guard(("qsqt",)),
                                             inc="dve")
                                setg(("bank", bank), "act", e_ev)
                                e_ss = p.op("tensor",
                                            lambda e, ww=ww: e.matmul(
                                                pb[5][0:2, 0:ww], ones2blk[:, :],
                                                qsqt[:, 0:ww], start=True, stop=True),
                                            waits=[("dve", e_sq2)] + guard(("bank", 5)),
                                            inc="pe")
                                setg(("qsqt",), "pe", e_ss)
                                ch = (gd % 1024) // 128
                                tgt = qmaxs if is_q else kmaxs
                                e_rm = p.op("vector",
                                            lambda e, tgt=tgt, ch=ch, ww=ww: e.reduce_max(
                                                rr[0:2, 512 + ch:512 + ch + 1]
                                                if False else rr[0:2, 0:1],
                                                pb[5][0:2, 0:ww], axis=AXX),
                                            waits=[("pe", e_ss)] + guard(("rr",)),
                                            inc="dve")
                                setg(("bank", 5), "dve", e_rm)
                                e_rx = p.op("vector",
                                            lambda e, tgt=tgt, ch=ch: e.tensor_max(
                                                tgt[0:2, ch:ch + 1],
                                                tgt[0:2, ch:ch + 1], rr[0:2, 0:1]),
                                            waits=[("dve", e_rm)] + guard(("bmax",)),
                                            inc="dve")
                                setg(("rr",), "dve", e_rx)
                                setg(("bmax",), "dve", e_rx)
                                dstd = qI_d if is_q else kI_d
                                e_qs = dma(p, "sync",
                                           dstd[gd % 1024: gd % 1024 + 128,
                                                t0 + o0:t0 + o0 + ww],
                                           qtmp[:, 0:ww], waits=[("act", e_ev)],
                                           sem="qs")
                                setg(("qtmp",), "qs", e_qs)
                        else:
                            vh = b_ - 4
                            ntok = (w + 127) // 128
                            for s_ in range(ntok):
                                tw = min(128, w - s_ * 128)
                                bank = s_ % 2
                                for d_ in range(8):
                                    p.op("tensor",
                                         lambda e, s_=s_, d_=d_, bank=bank, hh=hh,
                                                tw=tw: e.matmul(
                                             pb[bank][0:tw, 0:512],
                                             xln_a()[:, d_ * 512 + s_ * 128:
                                                     d_ * 512 + s_ * 128 + tw],
                                             slab[:, hh * 4096 + d_ * 512:
                                                  hh * 4096 + d_ * 512 + 512],
                                             start=(d_ == 0), stop=(d_ == 7)),
                                         waits=([("qw1", e_sl), ("dve", e_z)] +
                                                guard(("bank", bank)) if d_ == 0 else []),
                                         inc=("pe" if d_ == 7 else None))
                                e_vm = p.now("pe")
                                e_ms = p.op("vector",
                                            lambda e, tw=tw: e.memset(
                                                vtmp[0:tw, 0:520], 1.0),
                                            waits=guard(("vtmp",)), inc="dve")
                                e_vc = p.op("vector",
                                            lambda e, tw=tw, bank=bank: e.tensor_copy(
                                                vtmp[0:tw, 0:520].rearrange(
                                                    "p (g x) -> p g x", x=65)[:, :, 0:64],
                                                pb[bank][0:tw, 0:512].rearrange(
                                                    "p (g x) -> p g x", x=64)),
                                            waits=[("dve", e_ms), ("pe", e_vm)],
                                            inc="dve")
                                setg(("bank", bank), "dve", e_vc)
                                e_vst = dma(p, "sync",
                                            vI_d[t0 + s_ * 128: t0 + s_ * 128 + tw,
                                                 vh * 520: vh * 520 + 520],
                                            vtmp[0:tw, 0:520], waits=[("dve", e_vc)],
                                            sem="qs")
                                setg(("vtmp",), "qs", e_vst)
                        setg(("slab", hh), "pe", p.now("pe"))
                    setg(("xln",), "pe", p.now("pe"))

                e_b1 = dma(p, "sync", bnc_d[:, 0:8], qmaxs[:, :],
                           waits=[("dve", p.now("dve"))])
                e_b2 = dma(p, "sync", bnc_d[:, 8:16], kmaxs[:, :])
                e_b3 = dma(p, "sync", bncs[0:1, 0:32],
                           bnc_d.rearrange("r c -> (r c)")[None, :],
                           waits=[("qd", e_b2)])
                e_lq = p.op("scalar",
                            lambda e: e.activation(bncs[0:1, 32:64], bncs[0:1, 0:32],
                                                   AF.Ln),
                            waits=[("qd", e_b3)], inc="act")
                e_cs = None
                for h_ in range(H):
                    par, ch = h_ % 2, h_ // 2
                    e_cs = p.op("vector",
                                lambda e, h_=h_, par=par, ch=ch: e.tensor_scalar(
                                    bncs[0:1, 64 + h_:64 + h_ + 1],
                                    bncs[0:1, 32 + par * 16 + ch:32 + par * 16 + ch + 1],
                                    bncs[0:1, 32 + par * 16 + 8 + ch:
                                         32 + par * 16 + 8 + ch + 1],
                                    0.5, ALU.add, ALU.mult),
                                waits=[("act", e_lq)], inc="dve")
                e_ex2 = p.op("scalar",
                             lambda e: e.activation(bncs[0:1, 80:96], bncs[0:1, 64:80],
                                                    AF.Exp),
                             waits=[("dve", e_cs)], inc="act")
                e_cb2 = p.op("vector",
                             lambda e: e.tensor_scalar_mul(cvalb[0:1, :],
                                                           bncs[0:1, 80:96], -0.125),
                             waits=[("act", e_ex2)], inc="dve")
                e_cbm2 = p.op("tensor",
                              lambda e: e.matmul(pb[7][:, 0:H], onesb[0:1, 0:128],
                                                 cvalb[0:1, :], start=True, stop=True),
                              waits=[("dve", e_cb2)] + guard(("bank", 7)), inc="pe")
                e_cbI = p.op("scalar", lambda e: e.copy(cbI[:, :], pb[7][:, 0:H]),
                             waits=[("pe", e_cbm2)], inc="act")
                setg(("bank", 7), "act", e_cbI)

            if phase_limit >= 5:
                e_cp = dma(p, "sync", wps.rearrange("p (c n) -> p c n", n=1024),
                           cprojb.rearrange("(c p) n -> p c n", p=128),
                           waits=[conv_done["cprojb"], ("pe", p.now("pe"))],
                           sem="qw1")

                def k_load_inner(j, jb, kext):
                    return dma(p, "sync", kbuf[:, jb * 4096: jb * 4096 + kext],
                               kI_d[128 * j:128 * j + 128, 0:kext],
                               waits=[("qs", 1)] + guard(("kbuf", jb)),
                               sem=f"qa{jb}")

                def inner_resid(ti, t0, w):
                    e_h2r = dma(p, "sync",
                                x3s.rearrange("p (c n) -> p c n", n=512)[:, :, 0:w],
                                h2_d.rearrange("(c p) n -> p c n", p=128)[:, :, t0:t0 + w],
                                waits=[("qs", 1)] + guard(("x3s",)), sem="qx")
                    e_x1r = dma(p, "sync",
                                xs.rearrange("p (c n) -> p c n", n=512)[:, :, 0:w],
                                x1_d.rearrange("(c p) n -> p c n", p=128)[:, :, t0:t0 + w],
                                waits=[("qs", 1)] + guard(("xs",)), sem="qx")
                    yts_done = p.now("dve")
                    for c in range(8):
                        bank = 2 + c % 2
                        for jj in range(8):
                            p.op("tensor",
                                 lambda e, c=c, jj=jj, bank=bank, w=w: e.matmul(
                                     pb[bank][:, 0:w],
                                     wps[:, jj * 1024 + 128 * c: jj * 1024 + 128 * c + 128],
                                     yts[:, jj * 512:jj * 512 + w],
                                     start=(jj == 0), stop=(jj == 7)),
                                 waits=([("qw1", e_cp), ("dve", yts_done)] +
                                        guard(("bank", bank)) if jj == 0 else []),
                                 inc=("pe" if jj == 7 else None))
                        e_pm = p.now("pe")
                        e_u = p.op("vector",
                                   lambda e, c=c, bank=bank, w=w: e.scalar_tensor_tensor(
                                       uts[:, c * 512:c * 512 + w], pb[bank][:, 0:w],
                                       1.0, x3s[:, c * 512:c * 512 + w],
                                       ALU.mult, ALU.add),
                                   waits=[("pe", e_pm), ("qx", e_h2r)] +
                                         guard(("uts", c)),
                                   inc="dve")
                        e_su = p.op("vector",
                                    lambda e, c=c, w=w: e.tensor_add(
                                        cvb[:, 0:w], uts[:, c * 512:c * 512 + w],
                                        xs[:, c * 512:c * 512 + w]),
                                    waits=[("dve", e_u), ("qx", e_x1r)] +
                                          guard(("x1t",)),
                                    inc="dve")
                        setg(("bank", bank), "dve", e_u)
                        e_sust = dma(p, "sync", su_d[128 * c:128 * c + 128, t0:t0 + w],
                                     cvb[:, 0:w], waits=[("dve", e_su)], sem="qs")
                        setg(("x1t",), "qs", e_sust)
                        setg(("uts", c), "dve", e_su)
                    setg(("xs",), "dve", p.now("dve"))
                    for j in range(8):
                        setg(("yts", j), "pe", p.now("pe"))
                    ln_tile(uts, w, xln_a(), ln2wb,
                            src_ready=[("dve", p.now("dve"))] + guard(("xln",)))
                    e_ul = p.now("dve")
                    e_us = None
                    for c in range(8):
                        e_us = dma(p, "sync", uln_d[128 * c:128 * c + 128, t0:t0 + w],
                                   xln_a()[:, c * 512:c * 512 + w],
                                   waits=[("dve", e_ul)], sem="qs")
                    setg(("xln",), "qs", e_us)
                    setg(("x3s",), "dve", p.now("dve"))
                    for c in range(8):
                        setg(("uts", c), "dve", e_ul)

                attention_block(t_own, qI_d, vI_d, k_load_inner, cbI,
                                lambda j, hb, ti: 2 * j + hb, inner_resid)

            bwaits = [(sn, p.now(sn)) for sn in
                      ["qd", "qg", "qg2", "qx", "qs", "qa0", "qa1", "qw1",
                       "qw2", "pe", "act", "dve", "pool"]]
            for engn in ENGS:
                p.op(engn, lambda e: e.nop(), waits=bwaits)
            G.clear()

            if phase_limit >= 6:
                e_rt = dma(p, "sync", rts.rearrange("p (c n) -> p c n", n=8),
                           routerb.rearrange("(c p) n -> p c n", p=128),
                           waits=[conv_done["routerb"]], sem="qw1")
                for ti, (t0, w) in enumerate(t_own):
                    e_ur = dma(p, "sync",
                               xln_a().rearrange("p (c n) -> p c n", n=512)[:, :, 0:w],
                               uln_d.rearrange("(c p) n -> p c n", p=128)[:, :, t0:t0 + w],
                               waits=[("qs", 1)] + guard(("xln",)), sem="qx")
                    for d_ in range(8):
                        p.op("tensor",
                             lambda e, d_=d_, w=w: e.matmul(
                                 pb[0][0:E, 0:w], rts[:, d_ * 8:(d_ + 1) * 8],
                                 xln_a()[:, d_ * 512:d_ * 512 + w],
                                 start=(d_ == 0), stop=(d_ == 7)),
                             waits=([("qx", e_ur), ("qw1", e_rt)] + guard(("bank", 0))
                                    if d_ == 0 else []),
                             inc=("pe" if d_ == 7 else None))
                    e_rmm = p.now("pe")
                    e_rl = p.op("scalar",
                                lambda e, w=w: e.copy(rls[0:E, 0:w], pb[0][0:E, 0:w]),
                                waits=[("pe", e_rmm)] + guard(("rls",)), inc="act")
                    setg(("bank", 0), "act", e_rl)
                    ntok = (w + 127) // 128
                    e_gst = None
                    for s_ in range(ntok):
                        tw = min(128, w - s_ * 128)
                        e_tp = p.op("tensor",
                                    lambda e, s_=s_, tw=tw: e.transpose(
                                        pb[1][0:tw, 0:E],
                                        rls[0:E, s_ * 128: s_ * 128 + tw],
                                        identf[0:E, 0:E]),
                                    waits=[("act", e_rl)] + guard(("bank", 1)),
                                    inc="pe")
                        e_lt = p.op("vector",
                                    lambda e, tw=tw: e.tensor_copy(ltt[0:tw, :],
                                                                   pb[1][0:tw, 0:E]),
                                    waits=[("pe", e_tp)] + guard(("ltt",)), inc="dve")
                        setg(("bank", 1), "dve", e_lt)
                        e_m1 = p.op("vector",
                                    lambda e, tw=tw: e.reduce_max(lmx[0:tw, 0:1],
                                                                  ltt[0:tw, :], axis=AXX),
                                    waits=[("dve", e_lt)], inc="dve")
                        e_k1 = p.op("vector",
                                    lambda e, tw=tw: e.tensor_scalar(
                                        gtm[0:tw, :], ltt[0:tw, :], lmx[0:tw, 0:1],
                                        None, ALU.is_equal),
                                    waits=[("dve", e_m1)] + guard(("gtm",)), inc="dve")
                        e_lm = p.op("vector",
                                    lambda e, tw=tw: e.scalar_tensor_tensor(
                                        ltt[0:tw, :], gtm[0:tw, :], NEG_BIG,
                                        ltt[0:tw, :], ALU.mult, ALU.add),
                                    waits=[("dve", e_k1)], inc="dve")
                        e_m2 = p.op("vector",
                                    lambda e, tw=tw: e.reduce_max(lmx[0:tw, 1:2],
                                                                  ltt[0:tw, :], axis=AXX),
                                    waits=[("dve", e_lm)], inc="dve")
                        e_df = p.op("vector",
                                    lambda e, tw=tw: e.tensor_sub(lmx[0:tw, 2:3],
                                                                  lmx[0:tw, 0:1],
                                                                  lmx[0:tw, 1:2]),
                                    waits=[("dve", e_m2)], inc="dve")
                        e_s1 = p.op("scalar",
                                    lambda e, tw=tw: e.activation(lmx[0:tw, 3:4],
                                                                  lmx[0:tw, 2:3],
                                                                  AF.Sigmoid),
                                    waits=[("dve", e_df)], inc="act")
                        e_s2 = p.op("vector",
                                    lambda e, tw=tw: e.tensor_scalar(
                                        lmx[0:tw, 4:5], lmx[0:tw, 3:4], -1.0, 1.0,
                                        ALU.mult, ALU.add),
                                    waits=[("act", e_s1)], inc="dve")
                        e_k2 = p.op("vector",
                                    lambda e, tw=tw: e.tensor_scalar(
                                        ltt[0:tw, :], ltt[0:tw, :], lmx[0:tw, 1:2],
                                        None, ALU.is_equal),
                                    waits=[("dve", e_s2)], inc="dve")
                        e_g1 = p.op("vector",
                                    lambda e, tw=tw: e.tensor_scalar_mul(
                                        gtm[0:tw, :], gtm[0:tw, :], lmx[0:tw, 3:4]),
                                    waits=[("dve", e_k2)], inc="dve")
                        e_g2 = p.op("vector",
                                    lambda e, tw=tw: e.tensor_scalar_mul(
                                        ltt[0:tw, :], ltt[0:tw, :], lmx[0:tw, 4:5]),
                                    waits=[("dve", e_g1)], inc="dve")
                        e_gf = p.op("vector",
                                    lambda e, tw=tw: e.tensor_add(gtm[0:tw, :],
                                                                  gtm[0:tw, :],
                                                                  ltt[0:tw, :]),
                                    waits=[("dve", e_g2)], inc="dve")
                        setg(("ltt",), "dve", e_gf)
                        e_gt = p.op("tensor",
                                    lambda e, tw=tw: e.transpose(
                                        pb[2][0:E, 0:tw], gtm[0:tw, :],
                                        identf[0:tw, 0:tw]),
                                    waits=[("dve", e_gf)] + guard(("bank", 2)),
                                    inc="pe")
                        e_g8 = p.op("vector",
                                    lambda e, tw=tw: e.tensor_copy(g8r[0:E, 0:tw],
                                                                   pb[2][0:E, 0:tw]),
                                    waits=[("pe", e_gt)] + guard(("g8r",)), inc="dve")
                        setg(("bank", 2), "dve", e_g8)
                        setg(("gtm",), "pe", e_gt)
                        e_gst = dma(p, "sync",
                                    g_d[:, t0 + s_ * 128: t0 + s_ * 128 + tw],
                                    g8r[0:E, 0:tw], waits=[("dve", e_g8)], sem="qs")
                        setg(("g8r",), "qs", e_gst)
                    setg(("rls",), "pe", p.now("pe"))
                    setg(("xln",), "pe", p.now("pe"))

            if phase_limit >= 7:
                for ti, (t0, w) in enumerate(t_moe):
                    ntc = (w + 127) // 128
                    e_ur = dma(p, "sync",
                               xln_a().rearrange("p (c n) -> p c n", n=512)[:, :, 0:w],
                               uln_d.rearrange("(c p) n -> p c n", p=128)[:, :, t0:t0 + w],
                               waits=guard(("xln",)) + [("qs", 1)], sem="qx")
                    e_gr = None
                    for ee in range(E):
                        e_gr = dma(p, "sync", gsr[0:1, ee * 384: ee * 384 + w],
                                   g_d[ee:ee + 1, t0:t0 + w],
                                   waits=(guard(("gsr",)) + [("qs", 1)]
                                          if ee == 0 else []), sem="qx")
                    e_su_r = None
                    for tc in range(ntc):
                        tw = min(128, w - tc * 128)
                        e_su_r = p.op("sync",
                                      lambda e, tc=tc, tw=tw, t0=t0: e.dma_start_transpose(
                                          sutm[0:tw, tc * 1024:(tc + 1) * 1024],
                                          su_d[:, t0 + tc * 128: t0 + tc * 128 + tw]),
                                      waits=[("qs", 1)] + guard(("sutm",)),
                                      inc="qx", inc_n=16)
                    e_gb = p.op("vector",
                                lambda e: e.tensor_copy(gsb[0:1, :], gsr[0:1, :]),
                                waits=[("qx", e_gr)] + guard(("gsb",)), inc="dve")
                    setg(("gsr",), "dve", e_gb)
                    e_ge_all = None
                    for ee in range(E):
                        e_gm = p.op("tensor",
                                    lambda e, ee=ee, w=w: e.matmul(
                                        pb[6][:, 0:w], onesb[0:1, 0:128],
                                        gsb[0:1, ee * 384: ee * 384 + w],
                                        start=True, stop=True),
                                    waits=[("dve", e_gb)] + guard(("bank", 6)),
                                    inc="pe")
                        e_ge = p.op("scalar",
                                    lambda e, ee=ee, w=w: e.copy(
                                        ges[:, ee * 384:ee * 384 + w], pb[6][:, 0:w]),
                                    waits=[("pe", e_gm)] + guard(("ges", ee)),
                                    inc="act")
                        setg(("bank", 6), "act", e_ge)
                        e_ge_all = e_ge
                    setg(("gsb",), "act", e_ge_all)
                    for bq in range(6):
                        p.op("tensor",
                             lambda e, bq=bq: e.matmul(
                                 pb[bq][:, 0:512], zerosb[:, :], xln_a()[:, 0:512],
                                 start=True, stop=False),
                             waits=[("qx", e_ur)] + guard(("bank", bq)),
                             inc="pe")
                    last_w2 = None
                    hs_ev = {}
                    w1_ev = {}
                    w2_ev = {}

                    def slab_load(sl):
                        hh = sl % 4
                        ee, f_ = sl // 16, sl % 16
                        w1_ev[sl] = dma(p, "sync",
                                        w1s[:, hh * 1024:(hh + 1) * 1024].rearrange(
                                            "p (c n) -> p c n", n=128),
                                        w1b.rearrange("(c p) n -> p c n", p=128)[
                                            :, :, ee * FF + f_ * 128:
                                            ee * FF + f_ * 128 + 128],
                                        waits=[conv_done["w1b"]] +
                                              guard(("w1s", hh)), sem=f"qm{hh}")
                        w2_ev[sl] = dma(p, "sync",
                                        w2s[:, hh * 1024:(hh + 1) * 1024],
                                        w2b[sl * 128:(sl + 1) * 128, :],
                                        waits=[conv_done["w2b"]] +
                                              guard(("w2s", hh)), sem=f"qn{hh}")

                    for pre in range(3):
                        slab_load(pre)
                    for sl in range(129):
                        if sl < 128:
                            ee, f_ = sl // 16, sl % 16
                            hh = sl % 4
                            e_w1 = w1_ev.pop(sl)
                            bank = 6 + sl % 2
                            for d_ in range(8):
                                p.op("tensor",
                                     lambda e, d_=d_, hh=hh, bank=bank, w=w: e.matmul(
                                         pb[bank][:, 0:w],
                                         w1s[:, hh * 1024 + d_ * 128:
                                             hh * 1024 + d_ * 128 + 128],
                                         xln_a()[:, d_ * 512: d_ * 512 + w],
                                         start=(d_ == 0), stop=(d_ == 7)),
                                     waits=([(f"qm{hh}", e_w1), ("qx", e_ur)] +
                                            guard(("bank", bank)) if d_ == 0 else []),
                                     inc=("pe" if d_ == 7 else None))
                            e_hm = p.now("pe")
                            setg(("w1s", hh), "pe", e_hm)
                            e_gl = p.op("scalar",
                                        lambda e, bank=bank, w=w: e.activation(
                                            qtmp[:, 0:w], pb[bank][:, 0:w], AF.Gelu),
                                        waits=[("pe", e_hm), ("act", e_ge_all)] +
                                              guard(("qtmp",)),
                                        inc="act")
                            setg(("bank", bank), "act", e_gl)
                            hslot = sl % 32
                            e_hs = p.op("vector",
                                        lambda e, hslot=hslot, ee=ee, w=w: e.tensor_mul(
                                            hsc[:, hslot * 384: hslot * 384 + w],
                                            qtmp[:, 0:w],
                                            ges[:, ee * 384: ee * 384 + w]),
                                        waits=[("act", e_gl)] + guard(("hsc", hslot)),
                                        inc="dve")
                            setg(("qtmp",), "dve", e_hs)
                            hs_ev[sl] = e_hs
                        sp = sl - 1
                        if sp >= 0:
                            hhp = sp % 4
                            hslotp = sp % 32
                            e_hsp = hs_ev.pop(sp)
                            e_w2p = w2_ev.pop(sp)
                            for tc in range(ntc):
                                tw = min(128, w - tc * 128)
                                for dh in range(2):
                                    bk = 2 * tc + dh
                                    ws_ = []
                                    if tc == 0 and dh == 0:
                                        ws_ = [(f"qn{hhp}", e_w2p), ("dve", e_hsp)]
                                    last_w2 = p.op("tensor",
                                                   lambda e, sp=sp, tc=tc, dh=dh, bk=bk,
                                                          hhp=hhp, hslotp=hslotp,
                                                          tw=tw: e.matmul(
                                                       pb[bk][0:tw, 0:512],
                                                       hsc[:, hslotp * 384 + 128 * tc:
                                                           hslotp * 384 + 128 * tc + tw],
                                                       w2s[:, hhp * 1024 + 512 * dh:
                                                           hhp * 1024 + 512 * dh + 512],
                                                       start=False,
                                                       stop=(sp == 127)),
                                                   waits=ws_,
                                                   inc=("pe" if (tc == ntc - 1 and
                                                                 dh == 1) else None))
                            setg(("w2s", hhp), "pe", p.now("pe"))
                            setg(("hsc", hslotp), "pe", p.now("pe"))
                        if sl + 3 < 128:
                            slab_load(sl + 3)
                    e_w2d = p.now("pe")
                    for ee in range(E):
                        setg(("ges", ee), "pe", e_w2d)
                    for tc in range(ntc):
                        tw = min(128, w - tc * 128)
                        for dh in range(2):
                            bk = 2 * tc + dh
                            e_o = p.op("vector",
                                       lambda e, tc=tc, dh=dh, bk=bk, tw=tw: e.scalar_tensor_tensor(
                                           outs[0:tw, 0:512], pb[bk][0:tw, 0:512],
                                           1.0,
                                           sutm[0:tw, tc * 1024 + 512 * dh:
                                                tc * 1024 + 512 * dh + 512],
                                           ALU.mult, ALU.add),
                                       waits=[("pe", e_w2d), ("qx", e_su_r)] +
                                             guard(("outs",)),
                                       inc="dve")
                            oc0 = t0 - own[0]
                            e_os = dma(p, "sync",
                                       out_tm[oc0 + tc * 128: oc0 + tc * 128 + tw,
                                              512 * dh: 512 * dh + 512],
                                       outs[0:tw, 0:512], waits=[("dve", e_o)],
                                       sem="qs")
                            setg(("outs",), "qs", e_os)
                    setg(("sutm",), "dve", p.now("dve"))
                    for bq in range(6):
                        setg(("bank", bq), "dve", p.now("dve"))
                    setg(("xln",), "pe", p.now("pe"))

            for name, shape in debug_taps:
                src = {"dbg_x1": x1_d, "dbg_h2": h2_d, "dbg_su": su_d,
                       "dbg_uln": uln_d, "dbg_g": g_d, "dbg_qO": qO_d,
                       "dbg_kO": kO_d, "dbg_qI": qI_d, "dbg_kI": kI_d,
                       "dbg_vO": vO_d, "dbg_vI": vI_d}.get(name)
                if src is None:
                    continue
                rows, cols = shape
                if cols == S:
                    cols = full[1]
                if rows == S:
                    rows = full[1]
                isbf = src.tensor.dtype == BF16
                for r0 in range(0, rows, 128):
                    rh = min(128, rows - r0)
                    for c0 in range(0, cols, 512):
                        cw = min(512, cols - c0)
                        if isbf:
                            e_l = dma(p, "sync", cvb[0:rh, 0:cw],
                                      src[r0:r0 + rh, c0:c0 + cw],
                                      waits=[("qs", 1)] + guard(("cv",)))
                            e_c = p.op("vector",
                                       lambda e, rh=rh, cw=cw: e.tensor_copy(
                                           cvf[0:rh, 0:cw], cvb[0:rh, 0:cw]),
                                       waits=[("qd", e_l)], inc="dve")
                            e_s = dma(p, "sync", dbg[name][r0:r0 + rh, c0:c0 + cw],
                                      cvf[0:rh, 0:cw], waits=[("dve", e_c)])
                        else:
                            e_l = dma(p, "sync", cvf[0:rh, 0:cw],
                                      src[r0:r0 + rh, c0:c0 + cw],
                                      waits=[("qs", 1)] + guard(("cv",)))
                            e_s = dma(p, "sync", dbg[name][r0:r0 + rh, c0:c0 + cw],
                                      cvf[0:rh, 0:cw], waits=[("qd", e_l)])
                        setg(("cv",), "qd", e_s)

            p.op("sync", lambda e: e.nop(),
                 waits=[(sn, p.now(sn)) for sn in Prog.DMA_SEMS])
            return p

        PA = build_role(0)
        PB = build_role(1)

        def emit_engine(name):
            def body(eng):
                for f in P0.q[name]:
                    f(eng, sems)
                eng.wait_ge(sems["qd"], e_role)
                with eng.register(f"role_{name}") as reg:
                    eng.reg_load(reg, role_s[0:1, 0:1])
                    with eng.If_eq(reg, 0):
                        for f in PA.q[name]:
                            f(eng, sems)
                    with eng.Else():
                        for f in PB.q[name]:
                            f(eng, sems)
            return body

        for name in ENGS:
            getattr(block, name)(emit_engine(name))

    return nc



_BUILD_CACHE = {}


def _get_nc(key="main", **kw):
    if key not in _BUILD_CACHE:
        _BUILD_CACHE[key] = build_program(**kw)
    return _BUILD_CACHE[key]


def _perm_slot():
    perm = np.zeros(D, dtype=np.int64)
    for j in range(8):
        for b_, h_ in ((0, EVENS[j]), (1, ODDS[j])):
            for d_ in range(HD):
                perm[128 * j + 64 * b_ + d_] = 64 * h_ + d_
    return perm


def make_in_maps(inputs, consts=None):
    x = np.ascontiguousarray(np.asarray(inputs["L0_attn_x"], dtype=np.float32))
    perm = _perm_slot()
    wqT = np.ascontiguousarray(np.asarray(inputs["L0_attn_Wq"], np.float32).T[:, perm])
    wkT = np.ascontiguousarray(np.asarray(inputs["L0_attn_Wk"], np.float32).T)
    wvT = np.ascontiguousarray(np.asarray(inputs["L0_attn_Wv"], np.float32).T)
    wpT = np.ascontiguousarray(np.asarray(inputs["L0_attn_Wproj"], np.float32).T[perm, :])
    cattn = np.ascontiguousarray(np.asarray(inputs["L0_ffn_c_attn_w"], np.float32))
    cproj = np.ascontiguousarray(np.asarray(inputs["L0_ffn_c_proj_w"], np.float32))
    router = np.ascontiguousarray(np.asarray(inputs["L0_ffn_router_w"], np.float32))
    w1 = np.ascontiguousarray(np.asarray(inputs["L0_ffn_w1"], np.float32))
    w2 = np.ascontiguousarray(np.asarray(inputs["L0_ffn_w2"], np.float32))
    gains = np.asarray(inputs["L0_attn_q_gain"], np.float32)
    gain_row = np.ascontiguousarray(gains[np.array(SLOT_HEADS)][None, :])
    gain_pairs = np.ascontiguousarray(
        np.stack([gains[np.array(EVENS)], gains[np.array(ODDS)]], axis=0))
    ln1w = np.ascontiguousarray(np.asarray(inputs["L0_ffn_ln1_w"], np.float32)[None, :])
    ln2w = np.ascontiguousarray(np.asarray(inputs["L0_ffn_ln2_w"], np.float32)[None, :])
    if consts is None:
        consts = _host_consts()
    in_maps = []
    for core in range(8):
        b_ = core // 2
        m = dict(
            xT=np.ascontiguousarray(x[b_].T),
            role=np.array([[core % 2]], dtype=np.int32),
            wqT=wqT, wkT=wkT, wvT=wvT, wpT=wpT, cattn=cattn, cproj=cproj,
            router=router, w1=w1, w2=w2, gain_row=gain_row,
            gain_pairs=gain_pairs, ln1w_row=ln1w, ln2w_row=ln2w,
        )
        m.update(consts)
        in_maps.append(m)
    return in_maps


def kernel(**inputs):
    nc = _get_nc("main")
    in_maps = make_in_maps(inputs)
    res = run_bass_kernel_spmd(nc, in_maps, core_ids=list(range(8)))
    out = np.empty((4, S, D), dtype=np.float32)
    for core in range(8):
        b_ = core // 2
        o = res.results[core]["out_tm"]
        if core % 2 == 0:
            out[b_, 0:SPLIT, :] = o
        else:
            out[b_, SPLIT:S, :] = o[0:S - SPLIT, :]
    return out


# version 14
# speedup vs baseline: 1.0544x; 1.0544x over previous
import contextlib
import numpy as np
import ml_dtypes

import concourse.bass as bass
import concourse.mybir as mybir
from concourse.bass_utils import run_bass_kernel_spmd

F32 = mybir.dt.float32
BF16 = mybir.dt.bfloat16
I32 = mybir.dt.int32
AF = mybir.ActivationFunctionType
ALU = mybir.AluOpType
AXX = mybir.AxisListType.X

S = 4096
D = 1024
H = 16
KVH = 4
HD = 64
E = 8
FF = 2048
SPLIT = 2304
EPS_LN = 1e-5
EPS_RMS = float(np.finfo(np.float32).eps)
NEG_BIG = -1.0e30
SCAL_VALS = [EPS_LN, EPS_RMS, -0.5, 1.0 / HD, 0.125, -0.125, 1.0, -1.0]

EVENS = [0, 1, 2, 3, 8, 9, 10, 11]
ODDS = [4, 5, 6, 7, 12, 13, 14, 15]
SLOT_HEADS = []
for _j in range(8):
    SLOT_HEADS += [EVENS[_j], ODDS[_j]]

ENGS = ("sync", "tensor", "vector", "scalar", "gpsimd")


def tiles(lo, hi, step=512):
    return [(t0, min(step, hi - t0)) for t0 in range(lo, hi, step)]


class Prog:
    def __init__(self, counters=None):
        self.q = {e: [] for e in ENGS}
        self.c = dict(counters or {})

    def now(self, s):
        return self.c.get(s, 0)

    DMA_SEMS = ("qd", "qg", "qg2", "qgs", "qg2s", "qx", "qs", "qa0", "qa1",
                "qw1", "qw2", "qm0", "qm1", "qm2", "qm3", "qn0", "qn1",
                "qn2", "qn3")

    def op(self, eng, fn, waits=(), inc=None, inc_n=1):
        ws = []
        for (sname, v) in waits:
            if v <= 0:
                continue
            assert v <= self.c.get(sname, 0), (
                f"future wait {sname}={v} (now {self.c.get(sname, 0)})")
            if sname in self.DMA_SEMS:
                v = self.c.get(sname, 0)
            ws.append((sname, v))
        agg = {}
        for s, v in ws:
            agg[s] = max(agg.get(s, 0), v)
        ws = list(agg.items())
        post = None
        if inc is not None:
            self.c[inc] = self.c.get(inc, 0) + inc_n
            post = self.c[inc]

        def emit(engine, sems, _fn=fn, _ws=tuple(ws), _inc=inc, _inc_n=inc_n):
            for (sname, v) in _ws:
                engine.wait_ge(sems[sname], v)
            inst = _fn(engine)
            if _inc is not None:
                assert inst is not None
                inst.then_inc(sems[_inc], _inc_n)

        self.q[eng].append(emit)
        return post

    def allc(self):
        return [(k, v) for k, v in self.c.items()]


def _host_consts():
    bf = ml_dtypes.bfloat16
    c = {}
    c["c_scal"] = np.tile(np.array([[1e-05, float(np.finfo(np.float32).eps), -0.5,
                                     1.0 / 64, 0.125, -0.125, 1.0, -1.0]],
                                    dtype=np.float32), (128, 1))
    c["c_identf"] = np.eye(128, dtype=np.float32)
    c["c_ones"] = np.ones((128, 128), dtype=bf)
    c["c_zeros"] = np.zeros((128, 128), dtype=bf)
    o2 = np.zeros((128, 2), dtype=bf)
    o2[0:64, 0] = 1.0
    o2[64:128, 1] = 1.0
    c["c_ones2blk"] = o2
    oz = np.zeros((128, 4), dtype=bf)
    oz[:, 0] = 1.0
    oz[:, 3] = 1.0
    c["c_onesLN"] = oz
    s2 = np.zeros((2, 128), dtype=bf)
    s2[0, 0:64] = 1.0
    s2[1, 64:128] = 1.0
    c["c_sel2"] = s2
    r = np.zeros((64, 64), dtype=np.float32)
    for i in range(32):
        r[i, i + 32] = 1.0
        r[i + 32, i] = -1.0
    r2 = np.zeros((128, 128), dtype=np.float32)
    r2[0:64, 0:64] = r
    r2[64:128, 64:128] = r
    c["c_r2t"] = np.ascontiguousarray(r2.T).astype(bf)
    trim = np.zeros((128, 4 * 512), dtype=bf)
    kk = np.arange(128)[:, None]
    qq = np.arange(512)[None, :]
    for pat in range(4):
        trim[:, pat * 512:(pat + 1) * 512] = (qq >= kk + 128 * pat).astype(bf)
    c["c_tri"] = trim
    inv = 1.0 / (10000.0 ** (np.arange(0, HD, 2, dtype=np.float32) / HD))
    pos = np.arange(S, dtype=np.float32)
    fr = pos[:, None] * inv[None, :]
    c["c_cos"] = np.ascontiguousarray(np.tile(np.cos(fr).T, (4, 1))).astype(bf)
    c["c_sin"] = np.ascontiguousarray(np.tile(np.sin(fr).T, (4, 1))).astype(bf)
    return c


BO = dict(
    slab=0, wps=8192, kbuf=16384, vbuf=24576, probs=32896, yts=36992,
    qbuf=41088, vtmp=42112, cstile=43152, xbf=44192, xsq=44704, qfin=45216, qtmp=45728,
    qsqt=46240, recdb=46752, mrbf=47776, cvb=48800, tri=51872, r2t=53920,
    identb=54048,
)
BO["xln"] = 54176
NB = 58272
BO["hsc"] = 0
BO["ges"] = 12288
BO["w1s"] = 15360
BO["w2s"] = 19456
BO["rts"] = 23552
BO["gsb"] = 23616
BO["sutm"] = 26688
FO = dict(
    xs=0, x3s=4096, uts=8192, dtmp=12288, rtmp1=12800, rtmp2=13312,
    rtmp3=13824, mrow=14336, rr=15872, recds=16896, rbs=16896, x1t=17920,
    cvf=18432, identf=21504,
)
NF = 21632
FO["rls"] = 12288
FO["ltt"] = 12800
FO["lmx"] = 12808
FO["gtm"] = 12816
FO["outs"] = 13312
FO["gsr"] = 13824
FO["g8r"] = 15872


def build_program(debug_taps=(), phase_limit=99):
    nc = bass.Bass("TRN2", target_bir_lowering=False)

    din = {}

    def dinp(name, shape, dt):
        din[name] = nc.dram_tensor(name, list(shape), dt, kind="ExternalInput").ap()
        return din[name]

    xT = dinp("xT", (D, S), F32)
    role_in = dinp("role", (1, 1), I32)
    wqT = dinp("wqT", (D, D), F32)
    wkT = dinp("wkT", (D, KVH * HD), F32)
    wvT = dinp("wvT", (D, KVH * HD), F32)
    wpT = dinp("wpT", (D, D), F32)
    cattn = dinp("cattn", (D, 3 * D), F32)
    cproj = dinp("cproj", (D, D), F32)
    router = dinp("router", (D, E), F32)
    w1 = dinp("w1", (D, E * FF), F32)
    w2 = dinp("w2", (E * FF, D), F32)
    gain_row = dinp("gain_row", (1, H), F32)
    gain_pairs = dinp("gain_pairs", (2, 8), F32)
    ln1w_row = dinp("ln1w_row", (1, D), F32)
    ln2w_row = dinp("ln2w_row", (1, D), F32)
    for name, shape, dt in [
        ("c_scal", (128, 8), F32),
        ("c_identf", (128, 128), F32), ("c_ones", (128, 128), BF16),
        ("c_zeros", (128, 128), BF16),
        ("c_ones2blk", (128, 2), BF16), ("c_onesLN", (128, 4), BF16),
        ("c_sel2", (2, 128), BF16), ("c_r2t", (128, 128), BF16),
        ("c_tri", (128, 4 * 512), BF16), ("c_cos", (128, S), BF16),
        ("c_sin", (128, S), BF16),
    ]:
        dinp(name, shape, dt)

    out_tm = nc.dram_tensor("out_tm", [SPLIT, D], F32, kind="ExternalOutput").ap()
    dbg = {}
    for name, shape in debug_taps:
        dbg[name] = nc.dram_tensor(name, list(shape), F32, kind="ExternalOutput").ap()

    def scrt(name, shape, dt):
        return nc.dram_tensor(name, list(shape), dt).ap()

    wqTb = scrt("wqTb", (D, D), BF16)
    wkTb = scrt("wkTb", (D, KVH * HD), BF16)
    wvTb = scrt("wvTb", (D, KVH * HD), BF16)
    wpTb = scrt("wpTb", (D, D), BF16)
    cattnb = scrt("cattnb", (D, 3 * D), BF16)
    cprojb = scrt("cprojb", (D, D), BF16)
    routerb = scrt("routerb", (D, E), BF16)
    w1b = scrt("w1b", (D, E * FF), BF16)
    w2b = scrt("w2b", (E * FF, D), BF16)
    qO_d = scrt("qO_d", (D, S), BF16)
    kO_d = scrt("kO_d", (KVH * HD, S), BF16)
    vO_d = scrt("vO_d", (S, KVH * 65), BF16)
    x1_d = scrt("x1_d", (D, S), F32)
    h2_d = scrt("h2_d", (D, S), F32)
    qI_d = scrt("qI_d", (D, S), BF16)
    kI_d = scrt("kI_d", (D, S), BF16)
    vI_d = scrt("vI_d", (S, H * 65), BF16)
    su_d = scrt("su_d", (D, S), BF16)
    uln_d = scrt("uln_d", (D, S), BF16)
    g_d = scrt("g_d", (E, S), F32)
    bnc_d = scrt("bnc_d", (2, 16), F32)

    est = contextlib.ExitStack()
    with est:
        sbx = {}

        def sbuf(name, shape, dt):
            sbx[name] = est.enter_context(nc.sbuf_tensor(name, list(shape), dt))
            return sbx[name]

        arb = sbuf("arb", (128, NB), BF16)
        arf = sbuf("arf", (128, NF), F32)

        def B_(name, rows=128):
            return arb[0:rows, BO[name]:]

        def F_(name, rows=128):
            return arf[0:rows, FO[name]:]

        slab = arb[:, BO["slab"]:BO["slab"] + 8192]
        wps = arb[:, BO["wps"]:BO["wps"] + 8192]
        kbuf = arb[:, BO["kbuf"]:BO["kbuf"] + 8192]
        vbuf = arb[:, BO["vbuf"]:BO["vbuf"] + 8320]
        probs = arb[:, BO["probs"]:BO["probs"] + 4096]
        yts = arb[:, BO["yts"]:BO["yts"] + 4096]
        qbuf = arb[:, BO["qbuf"]:BO["qbuf"] + 1024]
        vtmp = arb[:, BO["vtmp"]:BO["vtmp"] + 1040]
        cstile = arb[:, BO["cstile"]:BO["cstile"] + 1040]
        xbf = arb[:, BO["xbf"]:BO["xbf"] + 512]
        xsq = arb[:, BO["xsq"]:BO["xsq"] + 512]
        qfin = arb[:, BO["qfin"]:BO["qfin"] + 512]
        qtmp = arb[:, BO["qtmp"]:BO["qtmp"] + 512]
        qsqt = arb[:, BO["qsqt"]:BO["qsqt"] + 512]
        recdb = arb[:, BO["recdb"]:BO["recdb"] + 1024]
        mrbf = arb[0:1, BO["mrbf"]:BO["mrbf"] + 1024]
        cvb = arb[:, BO["cvb"]:BO["cvb"] + 3072]
        tri = arb[:, BO["tri"]:BO["tri"] + 2048]
        r2t = arb[:, BO["r2t"]:BO["r2t"] + 128]
        identb = arb[:, BO["identb"]:BO["identb"] + 128]
        hsc = arb[:, BO["hsc"]:BO["hsc"] + 12288]
        ges = arb[:, BO["ges"]:BO["ges"] + 3072]
        w1s = arb[:, BO["w1s"]:BO["w1s"] + 4096]
        w2s = arb[:, BO["w2s"]:BO["w2s"] + 4096]
        sutm = arb[:, BO["sutm"]:BO["sutm"] + 3072]
        rts = arb[:, BO["rts"]:BO["rts"] + 64]
        gsb = arb[0:1, BO["gsb"]:BO["gsb"] + 3072]

        xs = arf[:, FO["xs"]:FO["xs"] + 4096]
        x3s = arf[:, FO["x3s"]:FO["x3s"] + 4096]
        uts = arf[:, FO["uts"]:FO["uts"] + 4096]
        dtmp = arf[:, FO["dtmp"]:FO["dtmp"] + 512]
        rtmp1 = arf[:, FO["rtmp1"]:FO["rtmp1"] + 512]
        rtmp2 = arf[:, FO["rtmp2"]:FO["rtmp2"] + 512]
        rtmp3 = arf[:, FO["rtmp3"]:FO["rtmp3"] + 512]
        mrow = arf[0:1, FO["mrow"]:FO["mrow"] + 1536]
        rr = arf[0:2, FO["rr"]:FO["rr"] + 1024]
        recds = arf[:, FO["recds"]:FO["recds"] + 1024]
        rbs = arf[0:64, FO["rbs"]:FO["rbs"] + 1024]
        x1t = arf[:, FO["x1t"]:FO["x1t"] + 512]
        cvf = arf[:, FO["cvf"]:FO["cvf"] + 3072]
        identf = arf[:, FO["identf"]:FO["identf"] + 128]
        rls = arf[0:8, FO["rls"]:FO["rls"] + 512]
        ltt = arf[:, FO["ltt"]:FO["ltt"] + 8]
        lmx = arf[:, FO["lmx"]:FO["lmx"] + 8]
        gtm = arf[:, FO["gtm"]:FO["gtm"] + 8]
        outs = arf[:, FO["outs"]:FO["outs"] + 512]
        gsr = arf[0:1, FO["gsr"]:FO["gsr"] + 3072]
        g8r = arf[0:8, FO["g8r"]:FO["g8r"] + 128]

        def xln_a():
            return arb[:, BO["xln"]:BO["xln"] + 4096]

        scalvec = sbuf("scalvec", (128, 8), F32)
        for _i, _v in enumerate(SCAL_VALS):
            nc.const_aps.register(F32, float(_v), scalvec[:, _i:_i + 1])
        role_s = sbuf("role_s", (1, 1), I32)
        onesb = sbuf("onesb", (128, 128), BF16)
        zerosb = sbuf("zerosb", (128, 128), BF16)
        ones2blk = sbuf("ones2blk", (128, 2), BF16)
        onesLN = sbuf("onesLN", (128, 4), BF16)
        sel2 = sbuf("sel2", (2, 128), BF16)
        gainr = sbuf("gainr", (1, H), F32)
        gainp = sbuf("gainp", (2, 8), F32)
        ln1wb = sbuf("ln1wb", (1, D), BF16)
        ln2wb = sbuf("ln2wb", (1, D), BF16)

        cbO = sbuf("cbO", (128, H), F32)
        cbI = sbuf("cbI", (128, H), F32)
        cvoutb = sbuf("cvoutb", (1, H), BF16)
        qmaxs = sbuf("qmaxs", (2, 8), F32)
        kmaxs = sbuf("kmaxs", (2, 8), F32)
        bncs = sbuf("bncs", (1, 96), F32)
        cvalb = sbuf("cvalb", (1, H), BF16)

        pb = [est.enter_context(nc.psum_tensor(f"pb{i}", [128, 512], F32))
              for i in range(2)]
        pq = [est.enter_context(nc.psum_tensor(f"pq{i}", [128, 1024], F32))
              for i in range(3)]
        for _q in range(3):
            for _h in range(2):
                pb.append(pq[_q][:, _h * 512:(_h + 1) * 512])

        sems = {n: est.enter_context(nc.semaphore(n))
                for n in ["qd", "qg", "qg2", "qgs", "qg2s", "qx", "qs", "qa0",
                          "qa1", "qw1", "qw2", "qm0", "qm1", "qm2", "qm3",
                          "qn0", "qn1", "qn2", "qn3", "pe", "act", "dve",
                          "pool"]}

        block = est.enter_context(nc.Block())

        P0 = Prog()

        def dma(p, eng, out, in_, waits=(), sem="qd", **kw):
            return p.op(eng, lambda e: e.dma_start(out=out, in_=in_, **kw),
                        waits=waits, inc=sem, inc_n=16)

        dma(P0, "sync", scalvec[:, :], din["c_scal"])
        e_role = dma(P0, "sync", role_s[:, :], role_in)
        e_consts = e_role
        for dst, srcn in [(identf, "c_identf"), (onesb, "c_ones"),
                          (zerosb, "c_zeros"),
                          (ones2blk, "c_ones2blk"), (onesLN, "c_onesLN"),
                          (sel2, "c_sel2"), (r2t, "c_r2t"), (tri, "c_tri"),
                          (gainr, "gain_row"), (gainp, "gain_pairs")]:
            e_consts = dma(P0, "sync", dst[:, :], din[srcn])
        e_l1 = dma(P0, "sync", cvf[0:1, 0:D], din["ln1w_row"])
        e_l2 = dma(P0, "sync", cvf[1:2, 0:D] if False else cvf[0:1, 0:D], din["ln1w_row"])
        e_c1 = P0.op("vector", lambda e: e.tensor_copy(ln1wb[:, :], cvf[0:1, 0:D]),
                     waits=[("qd", e_l1)], inc="dve")
        e_l2 = dma(P0, "sync", cvf[0:1, 0:D], din["ln2w_row"],
                   waits=[("dve", e_c1)])
        e_c2 = P0.op("vector", lambda e: e.tensor_copy(ln2wb[:, :], cvf[0:1, 0:D]),
                     waits=[("qd", e_l2)], inc="dve")

        e_cv0 = P0.op("scalar",
                      lambda e: e.activation(bncs[0:1, 0:H], gainr[0:1, :], AF.Abs),
                      waits=[("qd", e_consts)], inc="act")
        e_cv = P0.op("vector",
                     lambda e: e.tensor_scalar_mul(cvoutb[0:1, 0:H],
                                                   bncs[0:1, 0:H], -8.0),
                     waits=[("act", e_cv0)], inc="dve")
        e_cbm = P0.op("tensor",
                      lambda e: e.matmul(pb[7][:, 0:H], onesb[0:1, 0:128],
                                         cvoutb[0:1, 0:H], start=True, stop=True),
                      waits=[("dve", e_cv)], inc="pe")
        e_cbO = P0.op("scalar", lambda e: e.copy(cbO[:, :], pb[7][:, 0:H]),
                      waits=[("pe", e_cbm)], inc="act")

        conv_done = {}
        cv_guard = [None, None, None]
        ji = 0
        for (src, dst) in [(wqT, wqTb), (wkT, wkTb), (wvT, wvTb), (wpT, wpTb),
                           (cattn, cattnb), (cproj, cprojb), (router, routerb),
                           (w1, w1b), (w2, w2b)]:
            big = dst.tensor.name in ("w1b", "w2b")
            sm = "qg2" if big else "qg"
            sms = "qg2s" if big else "qgs"
            rows, cols = src.tensor.shape
            for r0 in range(0, rows, 128):
                rh = min(128, rows - r0)
                for c0 in range(0, cols, 1024):
                    cw = min(1024, cols - c0)
                    h = ji % 3
                    w_ = [cv_guard[h]] if cv_guard[h] else []
                    e_ld = dma(P0, "gpsimd", cvf[0:rh, h * 1024:h * 1024 + cw],
                               src[r0:r0 + rh, c0:c0 + cw],
                               waits=w_ + ([("dve", e_c2)] if ji < 3 else []),
                               sem=sm)
                    e_cp = P0.op("gpsimd",
                                 lambda e, rh=rh, h=h, cw=cw: e.tensor_copy(
                                     cvb[0:rh, h * 1024:h * 1024 + cw],
                                     cvf[0:rh, h * 1024:h * 1024 + cw]),
                                 waits=[(sm, e_ld)], inc="pool")
                    e_st = dma(P0, "gpsimd", dst[r0:r0 + rh, c0:c0 + cw],
                               cvb[0:rh, h * 1024:h * 1024 + cw],
                               waits=[("pool", e_cp)], sem=sm)
                    cv_guard[h] = (sm, e_st)
                    ji += 1
            conv_done[dst.tensor.name] = (sm, P0.now(sm))

        shared_counts = dict(P0.c)

        def build_role(role):
            p = Prog(shared_counts)
            if role == 0:
                own = (0, SPLIT)
                full = (0, SPLIT)
            else:
                own = (SPLIT, S)
                full = (0, S)
            t_full = tiles(*full)
            t_own = tiles(*own)
            t_moe = tiles(own[0], own[1], 384)

            G = {}

            def guard(key):
                return list(G.get(key, []))

            def setg(key, semname, cnt):
                G[key] = [(semname, cnt)]

            def addg(key, semname, cnt):
                G.setdefault(key, []).append((semname, cnt))

            def ln_tile(src_sb, w, dst_sb, wrow_b, out_f32_sb=None, eps=EPS_LN,
                        src_ready=()):
                for c in range(8):
                    e_bf = p.op("vector",
                                lambda e, c=c, w=w: e.tensor_copy(
                                    xbf[:, 0:w], src_sb[:, c * 512:c * 512 + w]),
                                waits=list(src_ready) + guard(("xbf",)), inc="dve")
                    e_sq = p.op("scalar",
                                lambda e, c=c, w=w: e.activation(
                                    xsq[:, 0:w], src_sb[:, c * 512:c * 512 + w],
                                    AF.Square),
                                waits=list(src_ready) + guard(("xsq",)), inc="act")
                    e_m1 = p.op("tensor",
                                lambda e, c=c, w=w: e.matmul(
                                    pb[2][0:1, 0:w], onesb[:, 0:1], xbf[:, 0:w],
                                    start=(c == 0), stop=(c == 7)),
                                waits=[("dve", e_bf)] +
                                      (guard(("bank", 2)) if c == 0 else []),
                                inc="pe")
                    setg(("xbf",), "pe", e_m1)
                    e_m2 = p.op("tensor",
                                lambda e, c=c, w=w: e.matmul(
                                    pb[3][0:1, 0:w], onesb[:, 0:1], xsq[:, 0:w],
                                    start=(c == 0), stop=(c == 7)),
                                waits=[("act", e_sq)] +
                                      (guard(("bank", 3)) if c == 0 else []),
                                inc="pe")
                    setg(("xsq",), "pe", e_m2)
                e_mm = p.now("pe")
                e_m = p.op("vector",
                           lambda e, w=w: e.tensor_scalar_mul(mrow[0:1, 0:w],
                                                              pb[2][0:1, 0:w], 1.0 / D),
                           waits=[("pe", e_mm)] + guard(("mrow",)), inc="dve")
                e_ex = p.op("vector",
                            lambda e, w=w: e.tensor_scalar_mul(mrow[0:1, 512:512 + w],
                                                               pb[3][0:1, 0:w], 1.0 / D),
                            inc="dve")
                setg(("bank", 2), "dve", e_ex)
                setg(("bank", 3), "dve", e_ex)
                e_m2r = p.op("vector",
                             lambda e, w=w: e.tensor_mul(mrow[0:1, 1024:1024 + w],
                                                         mrow[0:1, 0:w], mrow[0:1, 0:w]),
                             waits=[("dve", e_m)], inc="dve")
                e_var = p.op("vector",
                             lambda e, w=w: e.tensor_sub(mrow[0:1, 512:512 + w],
                                                         mrow[0:1, 512:512 + w],
                                                         mrow[0:1, 1024:1024 + w]),
                             waits=[("dve", e_m2r)], inc="dve")
                e_lnv = p.op("scalar",
                             lambda e, w=w, eps=eps: e.activation(
                                 mrow[0:1, 1024:1024 + w], mrow[0:1, 512:512 + w],
                                 AF.Ln, bias=eps),
                             waits=[("dve", e_var)], inc="act")
                e_rst = p.op("scalar",
                             lambda e, w=w: e.activation(mrow[0:1, 512:512 + w],
                                                         mrow[0:1, 1024:1024 + w],
                                                         AF.Exp, scale=-0.5),
                             waits=[("act", e_lnv)], inc="act")
                e_mb = p.op("vector",
                            lambda e, w=w: e.tensor_copy(mrbf[0:1, 0:w], mrow[0:1, 0:w]),
                            waits=[("dve", e_var)] + guard(("mrbf",)), inc="dve")
                e_rb = p.op("vector",
                            lambda e, w=w: e.tensor_copy(mrbf[0:1, 512:512 + w],
                                                         mrow[0:1, 512:512 + w]),
                            waits=[("act", e_rst)], inc="dve")
                setg(("mrow",), "dve", e_rb)
                e_bm = p.op("tensor",
                            lambda e, w=w: e.matmul(pb[3][:, 0:w], onesb[0:1, 0:128],
                                                    mrbf[0:1, 0:w], start=True, stop=True),
                            waits=[("dve", e_mb)] + guard(("bank", 3)), inc="pe")
                last = None
                for c in range(8):
                    wr_ap = (onesb[0:1, 0:128] if wrow_b is None
                             else wrow_b[0:1, c * 128:(c + 1) * 128])
                    dslot = dtmp if c % 2 == 0 else outs
                    dkey = ("dtmp", c % 2)
                    e_br = p.op("tensor",
                                lambda e, c=c, w=w, wr_ap=wr_ap: e.matmul(
                                    pb[4][:, 0:w], wr_ap,
                                    mrbf[0:1, 512:512 + w], start=True, stop=True),
                                waits=[("dve", e_rb)] + guard(("bank", 4)), inc="pe")
                    e_d = p.op("vector",
                               lambda e, c=c, w=w, dslot=dslot: e.tensor_sub(
                                   dslot[:, 0:w], src_sb[:, c * 512:c * 512 + w],
                                   pb[3][:, 0:w]),
                               waits=[("pe", e_bm), ("pe", e_br)] + guard(dkey),
                               inc="dve")
                    e_a = p.op("vector",
                               lambda e, c=c, w=w, dslot=dslot: e.tensor_mul(
                                   dst_sb[:, c * 512:c * 512 + w], dslot[:, 0:w],
                                   pb[4][:, 0:w]),
                               waits=[("dve", e_d)], inc="dve")
                    if out_f32_sb is not None:
                        e_a = p.op("vector",
                                   lambda e, c=c, w=w, dslot=dslot: e.tensor_mul(
                                       out_f32_sb[:, c * 512:c * 512 + w],
                                       dslot[:, 0:w], pb[4][:, 0:w]),
                                   inc="dve")
                    setg(dkey, "dve", e_a)
                    setg(("bank", 4), "dve", e_a)
                    last = e_a
                setg(("bank", 3), "dve", last)
                setg(("mrbf",), "dve", last)
                return last

            def rope_norm(kind, cpair, t0, w, bank, dests, e_qmm):
                e_raw = p.op("scalar",
                             lambda e, w=w, bank=bank: e.copy(qtmp[:, 0:w],
                                                              pb[bank][:, 0:w]),
                             waits=[("pe", e_qmm)] + guard(("qtmp",)), inc="act")
                e_sq = p.op("scalar",
                            lambda e, w=w, bank=bank: e.activation(
                                qsqt[:, 0:w], pb[bank][:, 0:w], AF.Square),
                            waits=[("pe", e_qmm)] + guard(("qsqt",)), inc="act")
                setg(("bank", bank), "act", e_sq)
                e_ss = p.op("tensor",
                            lambda e, w=w: e.matmul(pb[5][0:2, 0:w], ones2blk[:, :],
                                                    qsqt[:, 0:w], start=True, stop=True),
                            waits=[("act", e_sq)] + guard(("bank", 5)), inc="pe")
                setg(("qsqt",), "pe", e_ss)
                e_l = p.op("scalar",
                           lambda e, w=w: e.activation(rr[0:2, 0:w], pb[5][0:2, 0:w],
                                                       AF.Ln, bias=EPS_RMS,
                                                       scale=1.0 / HD),
                           waits=[("pe", e_ss)] + guard(("rr",)), inc="act")
                setg(("bank", 5), "act", e_l)
                e_r = p.op("scalar",
                           lambda e, w=w: e.activation(rr[0:2, 512:512 + w],
                                                       rr[0:2, 0:w], AF.Exp, scale=-0.5),
                           waits=[("act", e_l)], inc="act")
                setg(("rr",), "act", e_r)
                if kind == "q":
                    e_g = p.op("vector",
                               lambda e, w=w, cpair=cpair: e.tensor_scalar_mul(
                                   xbf[0:2, 0:w], rr[0:2, 512:512 + w],
                                   gainp[:, cpair:cpair + 1]),
                               waits=[("act", e_r)] + guard(("xbf",)), inc="dve")
                else:
                    e_g = p.op("vector",
                               lambda e, w=w: e.tensor_copy(xbf[0:2, 0:w],
                                                            rr[0:2, 512:512 + w]),
                               waits=[("act", e_r)] + guard(("xbf",)), inc="dve")
                e_bc = p.op("tensor",
                            lambda e, w=w: e.matmul(pb[6][:, 0:w], sel2[:, :],
                                                    xbf[0:2, 0:w], start=True, stop=True),
                            waits=[("dve", e_g)] + guard(("bank", 6)), inc="pe")
                setg(("xbf",), "pe", e_bc)
                e_rot = p.op("tensor",
                             lambda e, w=w: e.matmul(pb[7][:, 0:w], r2t[:, :],
                                                     qtmp[:, 0:w], start=True, stop=True),
                             waits=[("act", e_raw)] + guard(("bank", 7)), inc="pe")
                e_t1 = p.op("vector",
                            lambda e, w=w: e.tensor_mul(
                                rtmp1[:, 0:w], qtmp[:, 0:w], cstile[:, 0:w]),
                            waits=[("act", e_raw), ("qx", 1)] + guard(("rtmp1",)), inc="dve")
                setg(("qtmp",), "dve", e_t1)
                e_t2 = p.op("vector",
                            lambda e, w=w: e.tensor_mul(
                                rtmp2[:, 0:w], pb[7][:, 0:w], cstile[:, 520:520 + w]),
                            waits=[("pe", e_rot)] + guard(("rtmp2",)), inc="dve")
                setg(("bank", 7), "dve", e_t2)
                e_t3 = p.op("vector",
                            lambda e, w=w: e.tensor_add(rtmp3[:, 0:w], rtmp1[:, 0:w],
                                                        rtmp2[:, 0:w]),
                            waits=[("dve", e_t1), ("dve", e_t2)] + guard(("rtmp3",)),
                            inc="dve")
                setg(("rtmp1",), "dve", e_t3)
                setg(("rtmp2",), "dve", e_t3)
                e_f = None
                for (r0, _, _) in dests:
                    e_f = p.op("vector",
                               lambda e, r0=r0, w=w: e.tensor_mul(
                                   qfin[r0:r0 + 64, 0:w], rtmp3[r0:r0 + 64, 0:w],
                                   pb[6][r0:r0 + 64, 0:w]),
                               waits=[("dve", e_t3), ("pe", e_bc)] + guard(("qfin",)),
                               inc="dve")
                setg(("rtmp3",), "dve", e_f)
                setg(("bank", 6), "dve", e_f)
                e_st = None
                for (r0, dram, dr0) in dests:
                    e_st = dma(p, "sync", dram[dr0:dr0 + 64, t0:t0 + w],
                               qfin[r0:r0 + 64, 0:w], waits=[("dve", e_f)],
                               sem="qs")
                setg(("qfin",), "qs", e_st)

            e_wk = dma(p, "sync", slab[:, 0:2048].rearrange("p (c n) -> p c n", n=256),
                       wkTb.rearrange("(c p) n -> p c n", p=128),
                       waits=[conv_done["wkTb"]], sem="qw1")
            e_wv = dma(p, "sync", slab[:, 2048:4096].rearrange("p (c n) -> p c n", n=256),
                       wvTb.rearrange("(c p) n -> p c n", p=128),
                       waits=[conv_done["wvTb"]], sem="qw1")

            def p2_tile(t0, w):
                e_cs = dma(p, "sync", cstile[:, 0:w], din["c_cos"][:, t0:t0 + w],
                           waits=guard(("cstile",)), sem="qx")
                e_cs = dma(p, "sync", cstile[:, 520:520 + w],
                           din["c_sin"][:, t0:t0 + w], sem="qx")
                e_x = dma(p, "sync",
                          xs.rearrange("p (c n) -> p c n", n=512)[:, :, 0:w],
                          xT.rearrange("(c p) n -> p c n", p=128)[:, :, t0:t0 + w],
                          waits=guard(("xs",)), sem="qx")
                ln_tile(xs, w, xln_a(), None, src_ready=[("qx", e_x)])
                e_ln = p.now("dve")
                setg(("xs",), "dve", e_ln)
                rope_pend = []

                def flush_rope():
                    while rope_pend:
                        args = rope_pend.pop(0)
                        rope_norm(*args)

                for half in range(2):
                    e_wq = dma(p, "sync",
                               slab[:, 4096:8192].rearrange("p (c n) -> p c n", n=512),
                               wqTb.rearrange("(c p) n -> p c n", p=128)[
                                   :, :, half * 512:half * 512 + 512],
                               waits=[conv_done["wqTb"]] + guard(("wqh",)),
                               sem="qw1")
                    for cc in range(4):
                        c = 4 * half + cc
                        bank = c % 2
                        for d_ in range(8):
                            p.op("tensor",
                                 lambda e, cc=cc, d_=d_, bank=bank, w=w: e.matmul(
                                     pb[bank][:, 0:w],
                                     slab[:, 4096 + d_ * 512 + 128 * cc:
                                          4096 + d_ * 512 + 128 * cc + 128],
                                     xln_a()[:, d_ * 512: d_ * 512 + w],
                                     start=(d_ == 0), stop=(d_ == 7)),
                                 waits=([("dve", e_ln), ("qw1", e_wq)] +
                                        guard(("bank", bank)) if d_ == 0 else []),
                                 inc=("pe" if d_ == 7 else None))
                        rope_pend.append(("q", c, t0, w, bank,
                                          [(0, qO_d, 128 * c), (64, qO_d, 128 * c + 64)],
                                          p.now("pe")))
                        if len(rope_pend) > 1:
                            rope_norm(*rope_pend.pop(0))
                    flush_rope()
                    setg(("wqh",), "pe", p.now("pe"))
                for ck in range(2):
                    bank = ck % 2
                    for d_ in range(8):
                        p.op("tensor",
                             lambda e, ck=ck, d_=d_, bank=bank, w=w: e.matmul(
                                 pb[bank][:, 0:w],
                                 slab[:, d_ * 256 + 128 * ck: d_ * 256 + 128 * ck + 128],
                                 xln_a()[:, d_ * 512: d_ * 512 + w],
                                 start=(d_ == 0), stop=(d_ == 7)),
                             waits=([("qw1", e_wk)] + guard(("bank", bank))
                                    if d_ == 0 else []),
                             inc=("pe" if d_ == 7 else None))
                    rope_pend.append(("k", ck, t0, w, bank,
                                      [(0, kO_d, 128 * ck), (64, kO_d, 128 * ck + 64)],
                                      p.now("pe")))
                    if len(rope_pend) > 1:
                        rope_norm(*rope_pend.pop(0))
                flush_rope()
                ntok = (w + 127) // 128
                for s_ in range(ntok):
                    tw = min(128, w - s_ * 128)
                    bank = s_ % 2
                    for d_ in range(8):
                        p.op("tensor",
                             lambda e, s_=s_, d_=d_, bank=bank, tw=tw: e.matmul(
                                 pb[bank][0:tw, 0:256],
                                 xln_a()[:, d_ * 512 + s_ * 128: d_ * 512 + s_ * 128 + tw],
                                 slab[:, 2048 + d_ * 256: 2048 + d_ * 256 + 256],
                                 start=(d_ == 0), stop=(d_ == 7)),
                             waits=([("qw1", e_wv)] + guard(("bank", bank))
                                    if d_ == 0 else []),
                             inc=("pe" if d_ == 7 else None))
                    e_vm = p.now("pe")
                    e_ms = p.op("vector",
                                lambda e, tw=tw: e.memset(vtmp[0:tw, 0:260], 1.0),
                                waits=guard(("vtmp",)), inc="dve")
                    e_vc = p.op("vector",
                                lambda e, tw=tw, bank=bank: e.tensor_copy(
                                    vtmp[0:tw, 0:260].rearrange(
                                        "p (g x) -> p g x", x=65)[:, :, 0:64],
                                    pb[bank][0:tw, 0:256].rearrange(
                                        "p (g x) -> p g x", x=64)),
                                waits=[("dve", e_ms), ("pe", e_vm)], inc="dve")
                    setg(("bank", bank), "dve", e_vc)
                    e_vs = dma(p, "sync", vO_d[t0 + s_ * 128: t0 + s_ * 128 + tw, :],
                               vtmp[0:tw, 0:260], waits=[("dve", e_vc)], sem="qs")
                    setg(("vtmp",), "qs", e_vs)
                setg(("cstile",), "dve", p.now("dve"))

            if phase_limit == 2:
                for (t0, w) in t_full:
                    p2_tile(t0, w)

            setg(("kbuf", 0), "dve", p.now("dve"))
            setg(("kbuf", 1), "dve", p.now("dve"))

            def attention_block(t_list, qsrc, vsrc, k_load, cb_sb, cb_col, resid_fn,
                                interleave=None):
                def emit_loads(ti, t0, w, j):
                    kext = ((t0 + w + 127) // 128) * 128
                    nkc = kext // 128
                    jb = j % 2
                    sm = f"qa{jb}"
                    e_k = k_load(j, jb, kext)
                    vcol = 65 * (EVENS[j] // 4) if vsrc is vO_d else 130 * j
                    e_v = dma(p, "sync",
                              vbuf.rearrange("p (kc cc) -> p kc cc", cc=130)[
                                  :, jb * 32: jb * 32 + nkc, :],
                              vsrc.rearrange("(kc pp) cc -> pp kc cc", pp=128)[
                                  :, 0:nkc, vcol:vcol + 130],
                              waits=[("qs", 1)] + guard(("vbuf", jb)), sem=sm)
                    e_q = dma(p, "sync", qbuf[:, jb * 512: jb * 512 + w],
                              qsrc[128 * j:128 * j + 128, t0:t0 + w],
                              waits=[("qs", 1)] + guard(("qbuf", jb)), sem=sm)
                    return (sm, e_k, e_v, e_q)

                pend = {}
                for ti, (t0, w) in enumerate(t_list):
                    if interleave is not None:
                        interleave(ti)
                    kext = ((t0 + w + 127) // 128) * 128
                    nkc = kext // 128
                    pend[0] = emit_loads(ti, t0, w, 0)
                    pend[1] = emit_loads(ti, t0, w, 1)
                    for j in range(8):
                        jb = j % 2
                        sm, e_k, e_v, e_q = pend.pop(j)
                        ca = cb_col(j, 0, ti)
                        cb_ = cb_col(j, 1, ti)
                        nkp = (nkc + 1) // 2
                        dep = {}
                        last_ya = last_yb = None

                        def quad_of(kp, hb):
                            return (2 * kp + hb) % 3

                        for kp in range(nkp + 1):
                            if kp < nkp:
                                ks = [2 * kp] + ([2 * kp + 1]
                                                 if 2 * kp + 1 < nkc else [])
                                qa, qb = quad_of(kp, 0), quad_of(kp, 1)
                                for k in ks:
                                    kh = k % 2
                                    e_sa = p.op("tensor",
                                                lambda e, k=k, kh=kh, qa=qa, jb=jb, w=w: e.matmul(
                                                    pq[qa][:, kh * 512: kh * 512 + w],
                                                    kbuf[0:64, jb * 4096 + 128 * k:
                                                         jb * 4096 + 128 * k + 128],
                                                    qbuf[0:64, jb * 512: jb * 512 + w],
                                                    start=True, stop=True),
                                                waits=[(sm, e_k), (sm, e_q)] +
                                                      (guard(("bank", 2 + 2 * qa)) +
                                                       guard(("bank", 3 + 2 * qa))
                                                       if k == ks[0] else []) +
                                                      (guard(("bank", 0)) if k == 0 else []),
                                                inc="pe")
                                    e_sb = p.op("tensor",
                                                lambda e, k=k, kh=kh, qb=qb, jb=jb, w=w: e.matmul(
                                                    pq[qb][:, kh * 512: kh * 512 + w],
                                                    kbuf[64:128, jb * 4096 + 128 * k:
                                                         jb * 4096 + 128 * k + 128],
                                                    qbuf[64:128, jb * 512: jb * 512 + w],
                                                    start=True, stop=True,
                                                    tile_position=(64, 0)),
                                                waits=(guard(("bank", 2 + 2 * qb)) +
                                                       guard(("bank", 3 + 2 * qb))
                                                       if k == ks[0] else []) +
                                                      (guard(("bank", 1)) if k == 0 else []),
                                                inc="pe")
                                e_s_last = p.now("pe")
                                kw = 512 * (len(ks) - 1) + w
                                ca = cb_col(j, 0, ti)
                                cb_ = cb_col(j, 1, ti)
                                pslot_a = kp % 2
                                pslot_b = 2 + kp % 2
                                e_ea = p.op("scalar",
                                            lambda e, qa=qa, kw=kw, pslot_a=pslot_a, ca=ca: e.activation(
                                                probs[:, pslot_a * 1024: pslot_a * 1024 + kw],
                                                pq[qa][:, 0:kw], AF.Exp, scale=0.125,
                                                bias=cb_sb[:, ca:ca + 1]),
                                            waits=[("pe", e_s_last)] +
                                                  guard(("probs", pslot_a)),
                                            inc="act")
                                setg(("bank", 2 + 2 * qa), "act", e_ea)
                                setg(("bank", 3 + 2 * qa), "act", e_ea)
                                e_eb = p.op("scalar",
                                            lambda e, qb=qb, kw=kw, pslot_b=pslot_b, cb_=cb_: e.activation(
                                                probs[:, pslot_b * 1024: pslot_b * 1024 + kw],
                                                pq[qb][:, 0:kw], AF.Exp, scale=0.125,
                                                bias=cb_sb[:, cb_:cb_ + 1]),
                                            waits=[("pe", e_s_last)] +
                                                  guard(("probs", pslot_b)),
                                            inc="act")
                                setg(("bank", 2 + 2 * qb), "act", e_eb)
                                setg(("bank", 3 + 2 * qb), "act", e_eb)
                                deps_a = ("act", e_ea)
                                deps_b = ("act", e_eb)
                                for k in ks:
                                    dd = t0 - 128 * k
                                    if -w < dd < 128:
                                        pi = (-dd) // 128
                                        kh = k % 2
                                        e_ma = p.op("vector",
                                                    lambda e, pslot_a=pslot_a, kh=kh, pi=pi, w=w: e.tensor_mul(
                                                        probs[:, pslot_a * 1024 + kh * 512:
                                                              pslot_a * 1024 + kh * 512 + w],
                                                        probs[:, pslot_a * 1024 + kh * 512:
                                                              pslot_a * 1024 + kh * 512 + w],
                                                        tri[:, pi * 512: pi * 512 + w]),
                                                    waits=[("act", e_ea)], inc="dve")
                                        e_mb = p.op("vector",
                                                    lambda e, pslot_b=pslot_b, kh=kh, pi=pi, w=w: e.tensor_mul(
                                                        probs[:, pslot_b * 1024 + kh * 512:
                                                              pslot_b * 1024 + kh * 512 + w],
                                                        probs[:, pslot_b * 1024 + kh * 512:
                                                              pslot_b * 1024 + kh * 512 + w],
                                                        tri[:, pi * 512: pi * 512 + w]),
                                                    waits=[("act", e_eb)], inc="dve")
                                        deps_a = ("dve", e_ma)
                                        deps_b = ("dve", e_mb)
                                dep[kp] = (ks, pslot_a, pslot_b, deps_a, deps_b)
                            kpp = kp - 1
                            if kpp >= 0:
                                ksp, psa, psb, da, db = dep.pop(kpp)
                                for k in ksp:
                                    kh = k % 2
                                    w_ya = [da, (sm, e_v)]
                                    if k == 0:
                                        w_ya += guard(("bank", 0))
                                    last_ya = p.op("tensor",
                                                   lambda e, k=k, kh=kh, psa=psa, jb=jb,
                                                          w=w, nkc=nkc: e.matmul(
                                                       pb[0][0:65, 0:w],
                                                       vbuf[:, jb * 4160 + k * 130:
                                                            jb * 4160 + k * 130 + 65],
                                                       probs[:, psa * 1024 + kh * 512:
                                                             psa * 1024 + kh * 512 + w],
                                                       start=(k == 0),
                                                       stop=(k == nkc - 1)),
                                                   waits=w_ya, inc="pe")
                                    w_yb = [db]
                                    if k == 0:
                                        w_yb += guard(("bank", 1))
                                    last_yb = p.op("tensor",
                                                   lambda e, k=k, kh=kh, psb=psb, jb=jb,
                                                          w=w, nkc=nkc: e.matmul(
                                                       pb[1][0:65, 0:w],
                                                       vbuf[:, jb * 4160 + k * 130 + 65:
                                                            jb * 4160 + k * 130 + 130],
                                                       probs[:, psb * 1024 + kh * 512:
                                                             psb * 1024 + kh * 512 + w],
                                                       start=(k == 0),
                                                       stop=(k == nkc - 1)),
                                                   waits=w_yb, inc="pe")
                                setg(("probs", psa), "pe", last_ya)
                                setg(("probs", psb), "pe", last_yb)
                        setg(("vbuf", jb), "pe", last_yb)
                        setg(("qbuf", jb), "pe", last_yb)
                        setg(("kbuf", jb), "pe", last_yb)
                        if j + 2 < 8:
                            pend[j + 2] = emit_loads(ti, t0, w, j + 2)
                        e_yn_last = None
                        for hb in range(2):
                            e_rc = p.op("scalar",
                                        lambda e, hb=hb, w=w: e.activation(
                                            recds[64:65, hb * 512:hb * 512 + w],
                                            pb[hb][64:65, 0:w], AF.Ln),
                                        waits=[("pe", last_ya), ("pe", last_yb)] +
                                              guard(("recds", hb)),
                                        inc="act")
                            e_rcb = p.op("scalar",
                                         lambda e, hb=hb, w=w: e.activation(
                                             recdb[64:65, hb * 512:hb * 512 + w],
                                             recds[64:65, hb * 512:hb * 512 + w],
                                             AF.Exp, scale=-1.0),
                                         waits=[("act", e_rc)] + guard(("recdb", hb)),
                                         inc="act")
                            setg(("recds", hb), "act", e_rcb)
                            e_rbm = p.op("tensor",
                                         lambda e, hb=hb, w=w: e.matmul(
                                             pb[6][0:64, 0:w], onesb[64:65, 0:64],
                                             recdb[64:65, hb * 512:hb * 512 + w],
                                             start=True, stop=True),
                                         waits=[("act", e_rcb)] + guard(("bank", 6)),
                                         inc="pe")
                            setg(("recdb", hb), "pe", e_rbm)
                            e_rbe = p.op("scalar",
                                         lambda e, hb=hb, w=w: e.copy(
                                             rbs[0:64, hb * 512:hb * 512 + w],
                                             pb[6][0:64, 0:w]),
                                         waits=[("pe", e_rbm)] + guard(("rbs", hb)),
                                         inc="act")
                            setg(("bank", 6), "act", e_rbe)
                            e_yn = p.op("vector",
                                        lambda e, hb=hb, j=j, w=w: e.tensor_mul(
                                            yts[:, j * 512:j * 512 + w][
                                                64 * hb:64 * hb + 64, :],
                                            pb[hb][0:64, 0:w],
                                            rbs[0:64, hb * 512:hb * 512 + w]),
                                        waits=[("act", e_rbe)] + guard(("yts", j)),
                                        inc="dve")
                            setg(("rbs", hb), "dve", e_yn)
                            setg(("bank", hb), "dve", e_yn)
                            e_yn_last = e_yn
                        setg(("yts", j), "dve", e_yn_last)
                    resid_fn(ti, t0, w)

            def k_load_outer(j, jb, kext):
                ga = EVENS[j] // 4
                gb = ODDS[j] // 4
                dma(p, "sync", kbuf[0:64, jb * 4096: jb * 4096 + kext],
                    kO_d[64 * ga:64 * ga + 64, 0:kext],
                    waits=[("qs", 1)] + guard(("kbuf", jb)), sem=f"qa{jb}")
                return dma(p, "sync", kbuf[64:128, jb * 4096: jb * 4096 + kext],
                           kO_d[64 * gb:64 * gb + 64, 0:kext], sem=f"qa{jb}")

            def outer_resid(ti, t0, w):
                e_xr = dma(p, "sync",
                           x3s.rearrange("p (c n) -> p c n", n=512)[:, :, 0:w],
                           xT.rearrange("(c p) n -> p c n", p=128)[:, :, t0:t0 + w],
                           waits=guard(("x3s",)), sem="qx")
                yts_done = p.now("dve")
                for c in range(8):
                    bank = 2 + c % 2
                    for jj in range(8):
                        p.op("tensor",
                             lambda e, c=c, jj=jj, bank=bank, w=w: e.matmul(
                                 pb[bank][:, 0:w],
                                 wps[:, jj * 1024 + 128 * c: jj * 1024 + 128 * c + 128],
                                 yts[:, jj * 512:jj * 512 + w],
                                 start=(jj == 0), stop=(jj == 7)),
                             waits=([("qw1", e_wp2), ("dve", yts_done)] +
                                    guard(("bank", bank)) if jj == 0 else []),
                             inc=("pe" if jj == 7 else None))
                    e_pm = p.now("pe")
                    e_r = p.op("vector",
                               lambda e, c=c, bank=bank, w=w: e.scalar_tensor_tensor(
                                   x1t[:, 0:w], pb[bank][:, 0:w], 1.0,
                                   x3s[:, c * 512:c * 512 + w], ALU.mult, ALU.add),
                               waits=[("pe", e_pm), ("qx", e_xr)] + guard(("x1t",)),
                               inc="dve")
                    setg(("bank", bank), "dve", e_r)
                    e_st = dma(p, "sync", x1_d[128 * c: 128 * c + 128, t0:t0 + w],
                               x1t[:, 0:w], waits=[("dve", e_r)], sem="qs")
                    setg(("x1t",), "qs", e_st)
                setg(("x3s",), "dve", p.now("dve"))
                for j in range(8):
                    setg(("yts", j), "pe", p.now("pe"))

            if phase_limit >= 3:
                e_wp2 = dma(p, "sync", wps.rearrange("p (c n) -> p c n", n=1024),
                            wpTb.rearrange("(c p) n -> p c n", p=128),
                            waits=[conv_done["wpTb"]], sem="qw1")
                attention_block(t_full, qO_d, vO_d, k_load_outer, cbO,
                                lambda j, hb, ti: 2 * j + hb, outer_resid,
                                interleave=lambda ti: p2_tile(*t_full[ti]))

            if phase_limit >= 4:
                p.op("vector", lambda e: e.memset(qmaxs[:, :], 0.0), inc="dve")
                p.op("vector", lambda e: e.memset(kmaxs[:, :], 0.0), inc="dve")
                for ti, (t0, w) in enumerate(t_full):
                    e_x1 = dma(p, "sync",
                               xs.rearrange("p (c n) -> p c n", n=512)[:, :, 0:w],
                               x1_d.rearrange("(c p) n -> p c n", p=128)[:, :, t0:t0 + w],
                               waits=[("qs", 1)] + guard(("xs",)), sem="qx")
                    uts_g = []
                    for c in range(8):
                        uts_g += guard(("uts", c))
                    ln_tile(xs, w, xln_a(), None, out_f32_sb=uts,
                            src_ready=[("qx", e_x1)] + uts_g)
                    e_h2 = p.now("dve")
                    setg(("xs",), "dve", e_h2)
                    lo = max(t0, own[0]) - t0
                    hi = min(t0 + w, own[1]) - t0
                    e_h2s = 0
                    if hi > lo:
                        for c in range(8):
                            e_h2s = dma(p, "sync",
                                        h2_d[128 * c:128 * c + 128, t0 + lo:t0 + hi],
                                        uts[:, c * 512 + lo: c * 512 + hi],
                                        waits=[("dve", e_h2)], sem="qs")
                    ln_tile(uts, w, xln_a(), ln1wb, src_ready=[("dve", e_h2)])
                    e_z = p.now("dve")
                    for c in range(8):
                        setg(("uts", c), "dve", e_z)
                        if e_h2s:
                            addg(("uts", c), "qs", e_h2s)
                    q_lo, q_hi = lo, hi
                    for b_ in range(6):
                        is_q = b_ < 2
                        is_k = 2 <= b_ < 4
                        if is_q and q_hi <= q_lo:
                            continue
                        hh = b_ % 2
                        e_sl = dma(p, "sync",
                                   slab.rearrange("p (h c n) -> p h c n", h=2, n=512)[
                                       :, hh, :, :],
                                   cattnb.rearrange("(c p) n -> p c n", p=128)[
                                       :, :, b_ * 512:(b_ + 1) * 512],
                                   waits=[conv_done["cattnb"]] +
                                         guard(("slab", hh)), sem="qw1")
                        if is_q or is_k:
                            o0, o1_ = (q_lo, q_hi) if is_q else (0, w)
                            ww = o1_ - o0
                            for sub in range(4):
                                bank = sub % 2
                                for d_ in range(8):
                                    p.op("tensor",
                                         lambda e, sub=sub, d_=d_, bank=bank, hh=hh,
                                                o0=o0, ww=ww: e.matmul(
                                             pb[bank][:, 0:ww],
                                             slab[:, hh * 4096 + d_ * 512 + 128 * sub:
                                                  hh * 4096 + d_ * 512 + 128 * sub + 128],
                                             xln_a()[:, d_ * 512 + o0: d_ * 512 + o0 + ww],
                                             start=(d_ == 0), stop=(d_ == 7)),
                                         waits=([("dve", e_z), ("qw1", e_sl)] +
                                                guard(("bank", bank)) if d_ == 0 else []),
                                         inc=("pe" if d_ == 7 else None))
                                e_mm = p.now("pe")
                                gd = b_ * 512 + 128 * sub
                                e_ev = p.op("scalar",
                                            lambda e, ww=ww, bank=bank: e.copy(
                                                qtmp[:, 0:ww], pb[bank][:, 0:ww]),
                                            waits=[("pe", e_mm)] + guard(("qtmp",)),
                                            inc="act")
                                e_sq2 = p.op("vector",
                                             lambda e, ww=ww: e.tensor_mul(
                                                 qsqt[:, 0:ww], qtmp[:, 0:ww],
                                                 qtmp[:, 0:ww]),
                                             waits=[("act", e_ev)] + guard(("qsqt",)),
                                             inc="dve")
                                setg(("bank", bank), "act", e_ev)
                                e_ss = p.op("tensor",
                                            lambda e, ww=ww: e.matmul(
                                                pb[5][0:2, 0:ww], ones2blk[:, :],
                                                qsqt[:, 0:ww], start=True, stop=True),
                                            waits=[("dve", e_sq2)] + guard(("bank", 5)),
                                            inc="pe")
                                setg(("qsqt",), "pe", e_ss)
                                ch = (gd % 1024) // 128
                                tgt = qmaxs if is_q else kmaxs
                                e_rm = p.op("vector",
                                            lambda e, tgt=tgt, ch=ch, ww=ww: e.reduce_max(
                                                rr[0:2, 512 + ch:512 + ch + 1]
                                                if False else rr[0:2, 0:1],
                                                pb[5][0:2, 0:ww], axis=AXX),
                                            waits=[("pe", e_ss)] + guard(("rr",)),
                                            inc="dve")
                                setg(("bank", 5), "dve", e_rm)
                                e_rx = p.op("vector",
                                            lambda e, tgt=tgt, ch=ch: e.tensor_max(
                                                tgt[0:2, ch:ch + 1],
                                                tgt[0:2, ch:ch + 1], rr[0:2, 0:1]),
                                            waits=[("dve", e_rm)] + guard(("bmax",)),
                                            inc="dve")
                                setg(("rr",), "dve", e_rx)
                                setg(("bmax",), "dve", e_rx)
                                dstd = qI_d if is_q else kI_d
                                e_qs = dma(p, "sync",
                                           dstd[gd % 1024: gd % 1024 + 128,
                                                t0 + o0:t0 + o0 + ww],
                                           qtmp[:, 0:ww], waits=[("act", e_ev)],
                                           sem="qs")
                                setg(("qtmp",), "qs", e_qs)
                        else:
                            vh = b_ - 4
                            ntok = (w + 127) // 128
                            for s_ in range(ntok):
                                tw = min(128, w - s_ * 128)
                                bank = s_ % 2
                                for d_ in range(8):
                                    p.op("tensor",
                                         lambda e, s_=s_, d_=d_, bank=bank, hh=hh,
                                                tw=tw: e.matmul(
                                             pb[bank][0:tw, 0:512],
                                             xln_a()[:, d_ * 512 + s_ * 128:
                                                     d_ * 512 + s_ * 128 + tw],
                                             slab[:, hh * 4096 + d_ * 512:
                                                  hh * 4096 + d_ * 512 + 512],
                                             start=(d_ == 0), stop=(d_ == 7)),
                                         waits=([("qw1", e_sl), ("dve", e_z)] +
                                                guard(("bank", bank)) if d_ == 0 else []),
                                         inc=("pe" if d_ == 7 else None))
                                e_vm = p.now("pe")
                                e_ms = p.op("vector",
                                            lambda e, tw=tw: e.memset(
                                                vtmp[0:tw, 0:520], 1.0),
                                            waits=guard(("vtmp",)), inc="dve")
                                e_vc = p.op("vector",
                                            lambda e, tw=tw, bank=bank: e.tensor_copy(
                                                vtmp[0:tw, 0:520].rearrange(
                                                    "p (g x) -> p g x", x=65)[:, :, 0:64],
                                                pb[bank][0:tw, 0:512].rearrange(
                                                    "p (g x) -> p g x", x=64)),
                                            waits=[("dve", e_ms), ("pe", e_vm)],
                                            inc="dve")
                                setg(("bank", bank), "dve", e_vc)
                                e_vst = dma(p, "sync",
                                            vI_d[t0 + s_ * 128: t0 + s_ * 128 + tw,
                                                 vh * 520: vh * 520 + 520],
                                            vtmp[0:tw, 0:520], waits=[("dve", e_vc)],
                                            sem="qs")
                                setg(("vtmp",), "qs", e_vst)
                        setg(("slab", hh), "pe", p.now("pe"))
                    setg(("xln",), "pe", p.now("pe"))

                e_b1 = dma(p, "sync", bnc_d[:, 0:8], qmaxs[:, :],
                           waits=[("dve", p.now("dve"))])
                e_b2 = dma(p, "sync", bnc_d[:, 8:16], kmaxs[:, :])
                e_b3 = dma(p, "sync", bncs[0:1, 0:32],
                           bnc_d.rearrange("r c -> (r c)")[None, :],
                           waits=[("qd", e_b2)])
                e_lq = p.op("scalar",
                            lambda e: e.activation(bncs[0:1, 32:64], bncs[0:1, 0:32],
                                                   AF.Ln),
                            waits=[("qd", e_b3)], inc="act")
                e_cs = None
                for h_ in range(H):
                    par, ch = h_ % 2, h_ // 2
                    e_cs = p.op("vector",
                                lambda e, h_=h_, par=par, ch=ch: e.tensor_scalar(
                                    bncs[0:1, 64 + h_:64 + h_ + 1],
                                    bncs[0:1, 32 + par * 16 + ch:32 + par * 16 + ch + 1],
                                    bncs[0:1, 32 + par * 16 + 8 + ch:
                                         32 + par * 16 + 8 + ch + 1],
                                    0.5, ALU.add, ALU.mult),
                                waits=[("act", e_lq)], inc="dve")
                e_ex2 = p.op("scalar",
                             lambda e: e.activation(bncs[0:1, 80:96], bncs[0:1, 64:80],
                                                    AF.Exp),
                             waits=[("dve", e_cs)], inc="act")
                e_cb2 = p.op("vector",
                             lambda e: e.tensor_scalar_mul(cvalb[0:1, :],
                                                           bncs[0:1, 80:96], -0.125),
                             waits=[("act", e_ex2)], inc="dve")
                e_cbm2 = p.op("tensor",
                              lambda e: e.matmul(pb[7][:, 0:H], onesb[0:1, 0:128],
                                                 cvalb[0:1, :], start=True, stop=True),
                              waits=[("dve", e_cb2)] + guard(("bank", 7)), inc="pe")
                e_cbI = p.op("scalar", lambda e: e.copy(cbI[:, :], pb[7][:, 0:H]),
                             waits=[("pe", e_cbm2)], inc="act")
                setg(("bank", 7), "act", e_cbI)

            if phase_limit >= 5:
                e_cp = dma(p, "sync", wps.rearrange("p (c n) -> p c n", n=1024),
                           cprojb.rearrange("(c p) n -> p c n", p=128),
                           waits=[conv_done["cprojb"], ("pe", p.now("pe"))],
                           sem="qw1")

                def k_load_inner(j, jb, kext):
                    return dma(p, "sync", kbuf[:, jb * 4096: jb * 4096 + kext],
                               kI_d[128 * j:128 * j + 128, 0:kext],
                               waits=[("qs", 1)] + guard(("kbuf", jb)),
                               sem=f"qa{jb}")

                def inner_resid(ti, t0, w):
                    e_h2r = dma(p, "sync",
                                x3s.rearrange("p (c n) -> p c n", n=512)[:, :, 0:w],
                                h2_d.rearrange("(c p) n -> p c n", p=128)[:, :, t0:t0 + w],
                                waits=[("qs", 1)] + guard(("x3s",)), sem="qx")
                    e_x1r = dma(p, "sync",
                                xs.rearrange("p (c n) -> p c n", n=512)[:, :, 0:w],
                                x1_d.rearrange("(c p) n -> p c n", p=128)[:, :, t0:t0 + w],
                                waits=[("qs", 1)] + guard(("xs",)), sem="qx")
                    yts_done = p.now("dve")
                    for c in range(8):
                        bank = 2 + c % 2
                        for jj in range(8):
                            p.op("tensor",
                                 lambda e, c=c, jj=jj, bank=bank, w=w: e.matmul(
                                     pb[bank][:, 0:w],
                                     wps[:, jj * 1024 + 128 * c: jj * 1024 + 128 * c + 128],
                                     yts[:, jj * 512:jj * 512 + w],
                                     start=(jj == 0), stop=(jj == 7)),
                                 waits=([("qw1", e_cp), ("dve", yts_done)] +
                                        guard(("bank", bank)) if jj == 0 else []),
                                 inc=("pe" if jj == 7 else None))
                        e_pm = p.now("pe")
                        e_u = p.op("vector",
                                   lambda e, c=c, bank=bank, w=w: e.scalar_tensor_tensor(
                                       uts[:, c * 512:c * 512 + w], pb[bank][:, 0:w],
                                       1.0, x3s[:, c * 512:c * 512 + w],
                                       ALU.mult, ALU.add),
                                   waits=[("pe", e_pm), ("qx", e_h2r)] +
                                         guard(("uts", c)),
                                   inc="dve")
                        e_su = p.op("vector",
                                    lambda e, c=c, w=w: e.tensor_add(
                                        cvb[:, 0:w], uts[:, c * 512:c * 512 + w],
                                        xs[:, c * 512:c * 512 + w]),
                                    waits=[("dve", e_u), ("qx", e_x1r)] +
                                          guard(("x1t",)),
                                    inc="dve")
                        setg(("bank", bank), "dve", e_u)
                        e_sust = dma(p, "sync", su_d[128 * c:128 * c + 128, t0:t0 + w],
                                     cvb[:, 0:w], waits=[("dve", e_su)], sem="qs")
                        setg(("x1t",), "qs", e_sust)
                        setg(("uts", c), "dve", e_su)
                    setg(("xs",), "dve", p.now("dve"))
                    for j in range(8):
                        setg(("yts", j), "pe", p.now("pe"))
                    ln_tile(uts, w, xln_a(), ln2wb,
                            src_ready=[("dve", p.now("dve"))] + guard(("xln",)))
                    e_ul = p.now("dve")
                    e_us = None
                    for c in range(8):
                        e_us = dma(p, "sync", uln_d[128 * c:128 * c + 128, t0:t0 + w],
                                   xln_a()[:, c * 512:c * 512 + w],
                                   waits=[("dve", e_ul)], sem="qs")
                    setg(("xln",), "qs", e_us)
                    setg(("x3s",), "dve", p.now("dve"))
                    for c in range(8):
                        setg(("uts", c), "dve", e_ul)

                attention_block(t_own, qI_d, vI_d, k_load_inner, cbI,
                                lambda j, hb, ti: 2 * j + hb, inner_resid)

            bwaits = [(sn, p.now(sn)) for sn in
                      ["qd", "qg", "qg2", "qx", "qs", "qa0", "qa1", "qw1",
                       "qw2", "pe", "act", "dve", "pool"]]
            for engn in ENGS:
                p.op(engn, lambda e: e.nop(), waits=bwaits)
            G.clear()

            if phase_limit >= 6:
                e_rt = dma(p, "sync", rts.rearrange("p (c n) -> p c n", n=8),
                           routerb.rearrange("(c p) n -> p c n", p=128),
                           waits=[conv_done["routerb"]], sem="qw1")
                for ti, (t0, w) in enumerate(t_own):
                    e_ur = dma(p, "sync",
                               xln_a().rearrange("p (c n) -> p c n", n=512)[:, :, 0:w],
                               uln_d.rearrange("(c p) n -> p c n", p=128)[:, :, t0:t0 + w],
                               waits=[("qs", 1)] + guard(("xln",)), sem="qx")
                    for d_ in range(8):
                        p.op("tensor",
                             lambda e, d_=d_, w=w: e.matmul(
                                 pb[0][0:E, 0:w], rts[:, d_ * 8:(d_ + 1) * 8],
                                 xln_a()[:, d_ * 512:d_ * 512 + w],
                                 start=(d_ == 0), stop=(d_ == 7)),
                             waits=([("qx", e_ur), ("qw1", e_rt)] + guard(("bank", 0))
                                    if d_ == 0 else []),
                             inc=("pe" if d_ == 7 else None))
                    e_rmm = p.now("pe")
                    e_rl = p.op("scalar",
                                lambda e, w=w: e.copy(rls[0:E, 0:w], pb[0][0:E, 0:w]),
                                waits=[("pe", e_rmm)] + guard(("rls",)), inc="act")
                    setg(("bank", 0), "act", e_rl)
                    ntok = (w + 127) // 128
                    e_gst = None
                    for s_ in range(ntok):
                        tw = min(128, w - s_ * 128)
                        e_tp = p.op("tensor",
                                    lambda e, s_=s_, tw=tw: e.transpose(
                                        pb[1][0:tw, 0:E],
                                        rls[0:E, s_ * 128: s_ * 128 + tw],
                                        identf[0:E, 0:E]),
                                    waits=[("act", e_rl)] + guard(("bank", 1)),
                                    inc="pe")
                        e_lt = p.op("vector",
                                    lambda e, tw=tw: e.tensor_copy(ltt[0:tw, :],
                                                                   pb[1][0:tw, 0:E]),
                                    waits=[("pe", e_tp)] + guard(("ltt",)), inc="dve")
                        setg(("bank", 1), "dve", e_lt)
                        e_m1 = p.op("vector",
                                    lambda e, tw=tw: e.reduce_max(lmx[0:tw, 0:1],
                                                                  ltt[0:tw, :], axis=AXX),
                                    waits=[("dve", e_lt)], inc="dve")
                        e_k1 = p.op("vector",
                                    lambda e, tw=tw: e.tensor_scalar(
                                        gtm[0:tw, :], ltt[0:tw, :], lmx[0:tw, 0:1],
                                        None, ALU.is_equal),
                                    waits=[("dve", e_m1)] + guard(("gtm",)), inc="dve")
                        e_lm = p.op("vector",
                                    lambda e, tw=tw: e.scalar_tensor_tensor(
                                        ltt[0:tw, :], gtm[0:tw, :], NEG_BIG,
                                        ltt[0:tw, :], ALU.mult, ALU.add),
                                    waits=[("dve", e_k1)], inc="dve")
                        e_m2 = p.op("vector",
                                    lambda e, tw=tw: e.reduce_max(lmx[0:tw, 1:2],
                                                                  ltt[0:tw, :], axis=AXX),
                                    waits=[("dve", e_lm)], inc="dve")
                        e_df = p.op("vector",
                                    lambda e, tw=tw: e.tensor_sub(lmx[0:tw, 2:3],
                                                                  lmx[0:tw, 0:1],
                                                                  lmx[0:tw, 1:2]),
                                    waits=[("dve", e_m2)], inc="dve")
                        e_s1 = p.op("scalar",
                                    lambda e, tw=tw: e.activation(lmx[0:tw, 3:4],
                                                                  lmx[0:tw, 2:3],
                                                                  AF.Sigmoid),
                                    waits=[("dve", e_df)], inc="act")
                        e_s2 = p.op("vector",
                                    lambda e, tw=tw: e.tensor_scalar(
                                        lmx[0:tw, 4:5], lmx[0:tw, 3:4], -1.0, 1.0,
                                        ALU.mult, ALU.add),
                                    waits=[("act", e_s1)], inc="dve")
                        e_k2 = p.op("vector",
                                    lambda e, tw=tw: e.tensor_scalar(
                                        ltt[0:tw, :], ltt[0:tw, :], lmx[0:tw, 1:2],
                                        None, ALU.is_equal),
                                    waits=[("dve", e_s2)], inc="dve")
                        e_g1 = p.op("vector",
                                    lambda e, tw=tw: e.tensor_scalar_mul(
                                        gtm[0:tw, :], gtm[0:tw, :], lmx[0:tw, 3:4]),
                                    waits=[("dve", e_k2)], inc="dve")
                        e_g2 = p.op("vector",
                                    lambda e, tw=tw: e.tensor_scalar_mul(
                                        ltt[0:tw, :], ltt[0:tw, :], lmx[0:tw, 4:5]),
                                    waits=[("dve", e_g1)], inc="dve")
                        e_gf = p.op("vector",
                                    lambda e, tw=tw: e.tensor_add(gtm[0:tw, :],
                                                                  gtm[0:tw, :],
                                                                  ltt[0:tw, :]),
                                    waits=[("dve", e_g2)], inc="dve")
                        setg(("ltt",), "dve", e_gf)
                        e_gt = p.op("tensor",
                                    lambda e, tw=tw: e.transpose(
                                        pb[2][0:E, 0:tw], gtm[0:tw, :],
                                        identf[0:tw, 0:tw]),
                                    waits=[("dve", e_gf)] + guard(("bank", 2)),
                                    inc="pe")
                        e_g8 = p.op("vector",
                                    lambda e, tw=tw: e.tensor_copy(g8r[0:E, 0:tw],
                                                                   pb[2][0:E, 0:tw]),
                                    waits=[("pe", e_gt)] + guard(("g8r",)), inc="dve")
                        setg(("bank", 2), "dve", e_g8)
                        setg(("gtm",), "pe", e_gt)
                        e_gst = dma(p, "sync",
                                    g_d[:, t0 + s_ * 128: t0 + s_ * 128 + tw],
                                    g8r[0:E, 0:tw], waits=[("dve", e_g8)], sem="qs")
                        setg(("g8r",), "qs", e_gst)
                    setg(("rls",), "pe", p.now("pe"))
                    setg(("xln",), "pe", p.now("pe"))

            if phase_limit >= 7:
                for ti, (t0, w) in enumerate(t_moe):
                    ntc = (w + 127) // 128
                    e_ur = dma(p, "sync",
                               xln_a().rearrange("p (c n) -> p c n", n=512)[:, :, 0:w],
                               uln_d.rearrange("(c p) n -> p c n", p=128)[:, :, t0:t0 + w],
                               waits=guard(("xln",)) + [("qs", 1)], sem="qx")
                    e_gr = None
                    for ee in range(E):
                        e_gr = dma(p, "sync", gsr[0:1, ee * 384: ee * 384 + w],
                                   g_d[ee:ee + 1, t0:t0 + w],
                                   waits=(guard(("gsr",)) + [("qs", 1)]
                                          if ee == 0 else []), sem="qx")
                    e_su_r = None
                    for tc in range(ntc):
                        tw = min(128, w - tc * 128)
                        e_su_r = p.op("sync",
                                      lambda e, tc=tc, tw=tw, t0=t0: e.dma_start_transpose(
                                          sutm[0:tw, tc * 1024:(tc + 1) * 1024],
                                          su_d[:, t0 + tc * 128: t0 + tc * 128 + tw]),
                                      waits=[("qs", 1)] + guard(("sutm",)),
                                      inc="qx", inc_n=16)
                    e_gb = p.op("vector",
                                lambda e: e.tensor_copy(gsb[0:1, :], gsr[0:1, :]),
                                waits=[("qx", e_gr)] + guard(("gsb",)), inc="dve")
                    setg(("gsr",), "dve", e_gb)
                    e_ge_all = None
                    for ee in range(E):
                        e_gm = p.op("tensor",
                                    lambda e, ee=ee, w=w: e.matmul(
                                        pb[6][:, 0:w], onesb[0:1, 0:128],
                                        gsb[0:1, ee * 384: ee * 384 + w],
                                        start=True, stop=True),
                                    waits=[("dve", e_gb)] + guard(("bank", 6)),
                                    inc="pe")
                        e_ge = p.op("scalar",
                                    lambda e, ee=ee, w=w: e.copy(
                                        ges[:, ee * 384:ee * 384 + w], pb[6][:, 0:w]),
                                    waits=[("pe", e_gm)] + guard(("ges", ee)),
                                    inc="act")
                        setg(("bank", 6), "act", e_ge)
                        e_ge_all = e_ge
                    setg(("gsb",), "act", e_ge_all)
                    for bq in range(6):
                        p.op("tensor",
                             lambda e, bq=bq: e.matmul(
                                 pb[bq][:, 0:512], zerosb[:, :], xln_a()[:, 0:512],
                                 start=True, stop=False),
                             waits=[("qx", e_ur)] + guard(("bank", bq)),
                             inc="pe")
                    last_w2 = None
                    hs_ev = {}
                    w1_ev = {}
                    w2_ev = {}

                    def slab_load(sl):
                        hh = sl % 4
                        ee, f_ = sl // 16, sl % 16
                        w1_ev[sl] = dma(p, "sync",
                                        w1s[:, hh * 1024:(hh + 1) * 1024].rearrange(
                                            "p (c n) -> p c n", n=128),
                                        w1b.rearrange("(c p) n -> p c n", p=128)[
                                            :, :, ee * FF + f_ * 128:
                                            ee * FF + f_ * 128 + 128],
                                        waits=[conv_done["w1b"]] +
                                              guard(("w1s", hh)), sem=f"qm{hh}")
                        w2_ev[sl] = dma(p, "sync",
                                        w2s[:, hh * 1024:(hh + 1) * 1024],
                                        w2b[sl * 128:(sl + 1) * 128, :],
                                        waits=[conv_done["w2b"]] +
                                              guard(("w2s", hh)), sem=f"qn{hh}")

                    for pre in range(3):
                        slab_load(pre)
                    for sl in range(129):
                        if sl < 128:
                            ee, f_ = sl // 16, sl % 16
                            hh = sl % 4
                            e_w1 = w1_ev.pop(sl)
                            bank = 6 + sl % 2
                            for d_ in range(8):
                                p.op("tensor",
                                     lambda e, d_=d_, hh=hh, bank=bank, w=w: e.matmul(
                                         pb[bank][:, 0:w],
                                         w1s[:, hh * 1024 + d_ * 128:
                                             hh * 1024 + d_ * 128 + 128],
                                         xln_a()[:, d_ * 512: d_ * 512 + w],
                                         start=(d_ == 0), stop=(d_ == 7)),
                                     waits=([(f"qm{hh}", e_w1), ("qx", e_ur)] +
                                            guard(("bank", bank)) if d_ == 0 else []),
                                     inc=("pe" if d_ == 7 else None))
                            e_hm = p.now("pe")
                            setg(("w1s", hh), "pe", e_hm)
                            e_gl = p.op("scalar",
                                        lambda e, bank=bank, w=w: e.activation(
                                            qtmp[:, 0:w], pb[bank][:, 0:w], AF.Gelu),
                                        waits=[("pe", e_hm), ("act", e_ge_all)] +
                                              guard(("qtmp",)),
                                        inc="act")
                            setg(("bank", bank), "act", e_gl)
                            hslot = sl % 32
                            e_hs = p.op("vector",
                                        lambda e, hslot=hslot, ee=ee, w=w: e.tensor_mul(
                                            hsc[:, hslot * 384: hslot * 384 + w],
                                            qtmp[:, 0:w],
                                            ges[:, ee * 384: ee * 384 + w]),
                                        waits=[("act", e_gl)] + guard(("hsc", hslot)),
                                        inc="dve")
                            setg(("qtmp",), "dve", e_hs)
                            hs_ev[sl] = e_hs
                        sp = sl - 1
                        if sp >= 0:
                            hhp = sp % 4
                            hslotp = sp % 32
                            e_hsp = hs_ev.pop(sp)
                            e_w2p = w2_ev.pop(sp)
                            for tc in range(ntc):
                                tw = min(128, w - tc * 128)
                                for dh in range(2):
                                    bk = 2 * tc + dh
                                    ws_ = []
                                    if tc == 0 and dh == 0:
                                        ws_ = [(f"qn{hhp}", e_w2p), ("dve", e_hsp)]
                                    last_w2 = p.op("tensor",
                                                   lambda e, sp=sp, tc=tc, dh=dh, bk=bk,
                                                          hhp=hhp, hslotp=hslotp,
                                                          tw=tw: e.matmul(
                                                       pb[bk][0:tw, 0:512],
                                                       hsc[:, hslotp * 384 + 128 * tc:
                                                           hslotp * 384 + 128 * tc + tw],
                                                       w2s[:, hhp * 1024 + 512 * dh:
                                                           hhp * 1024 + 512 * dh + 512],
                                                       start=False,
                                                       stop=(sp == 127)),
                                                   waits=ws_,
                                                   inc=("pe" if (tc == ntc - 1 and
                                                                 dh == 1) else None))
                            setg(("w2s", hhp), "pe", p.now("pe"))
                            setg(("hsc", hslotp), "pe", p.now("pe"))
                        if sl + 3 < 128:
                            slab_load(sl + 3)
                    e_w2d = p.now("pe")
                    for ee in range(E):
                        setg(("ges", ee), "pe", e_w2d)
                    for tc in range(ntc):
                        tw = min(128, w - tc * 128)
                        for dh in range(2):
                            bk = 2 * tc + dh
                            e_o = p.op("vector",
                                       lambda e, tc=tc, dh=dh, bk=bk, tw=tw: e.scalar_tensor_tensor(
                                           outs[0:tw, 0:512], pb[bk][0:tw, 0:512],
                                           1.0,
                                           sutm[0:tw, tc * 1024 + 512 * dh:
                                                tc * 1024 + 512 * dh + 512],
                                           ALU.mult, ALU.add),
                                       waits=[("pe", e_w2d), ("qx", e_su_r)] +
                                             guard(("outs",)),
                                       inc="dve")
                            oc0 = t0 - own[0]
                            e_os = dma(p, "sync",
                                       out_tm[oc0 + tc * 128: oc0 + tc * 128 + tw,
                                              512 * dh: 512 * dh + 512],
                                       outs[0:tw, 0:512], waits=[("dve", e_o)],
                                       sem="qs")
                            setg(("outs",), "qs", e_os)
                    setg(("sutm",), "dve", p.now("dve"))
                    for bq in range(6):
                        setg(("bank", bq), "dve", p.now("dve"))
                    setg(("xln",), "pe", p.now("pe"))

            for name, shape in debug_taps:
                src = {"dbg_x1": x1_d, "dbg_h2": h2_d, "dbg_su": su_d,
                       "dbg_uln": uln_d, "dbg_g": g_d, "dbg_qO": qO_d,
                       "dbg_kO": kO_d, "dbg_qI": qI_d, "dbg_kI": kI_d,
                       "dbg_vO": vO_d, "dbg_vI": vI_d}.get(name)
                if src is None:
                    continue
                rows, cols = shape
                if cols == S:
                    cols = full[1]
                if rows == S:
                    rows = full[1]
                isbf = src.tensor.dtype == BF16
                for r0 in range(0, rows, 128):
                    rh = min(128, rows - r0)
                    for c0 in range(0, cols, 512):
                        cw = min(512, cols - c0)
                        if isbf:
                            e_l = dma(p, "sync", cvb[0:rh, 0:cw],
                                      src[r0:r0 + rh, c0:c0 + cw],
                                      waits=[("qs", 1)] + guard(("cv",)))
                            e_c = p.op("vector",
                                       lambda e, rh=rh, cw=cw: e.tensor_copy(
                                           cvf[0:rh, 0:cw], cvb[0:rh, 0:cw]),
                                       waits=[("qd", e_l)], inc="dve")
                            e_s = dma(p, "sync", dbg[name][r0:r0 + rh, c0:c0 + cw],
                                      cvf[0:rh, 0:cw], waits=[("dve", e_c)])
                        else:
                            e_l = dma(p, "sync", cvf[0:rh, 0:cw],
                                      src[r0:r0 + rh, c0:c0 + cw],
                                      waits=[("qs", 1)] + guard(("cv",)))
                            e_s = dma(p, "sync", dbg[name][r0:r0 + rh, c0:c0 + cw],
                                      cvf[0:rh, 0:cw], waits=[("qd", e_l)])
                        setg(("cv",), "qd", e_s)

            p.op("sync", lambda e: e.nop(),
                 waits=[(sn, p.now(sn)) for sn in Prog.DMA_SEMS])
            return p

        PA = build_role(0)
        PB = build_role(1)

        def emit_engine(name):
            def body(eng):
                for f in P0.q[name]:
                    f(eng, sems)
                eng.wait_ge(sems["qd"], e_role)
                with eng.register(f"role_{name}") as reg:
                    eng.reg_load(reg, role_s[0:1, 0:1])
                    with eng.If_eq(reg, 0):
                        for f in PA.q[name]:
                            f(eng, sems)
                    with eng.Else():
                        for f in PB.q[name]:
                            f(eng, sems)
            return body

        for name in ENGS:
            getattr(block, name)(emit_engine(name))

    return nc



_BUILD_CACHE = {}


def _get_nc(key="main", **kw):
    if key not in _BUILD_CACHE:
        _BUILD_CACHE[key] = build_program(**kw)
    return _BUILD_CACHE[key]


def _perm_slot():
    perm = np.zeros(D, dtype=np.int64)
    for j in range(8):
        for b_, h_ in ((0, EVENS[j]), (1, ODDS[j])):
            for d_ in range(HD):
                perm[128 * j + 64 * b_ + d_] = 64 * h_ + d_
    return perm


def make_in_maps(inputs, consts=None):
    x = np.ascontiguousarray(np.asarray(inputs["L0_attn_x"], dtype=np.float32))
    perm = _perm_slot()
    wqT = np.ascontiguousarray(np.asarray(inputs["L0_attn_Wq"], np.float32).T[:, perm])
    wkT = np.ascontiguousarray(np.asarray(inputs["L0_attn_Wk"], np.float32).T)
    wvT = np.ascontiguousarray(np.asarray(inputs["L0_attn_Wv"], np.float32).T)
    wpT = np.ascontiguousarray(np.asarray(inputs["L0_attn_Wproj"], np.float32).T[perm, :])
    cattn = np.ascontiguousarray(np.asarray(inputs["L0_ffn_c_attn_w"], np.float32))
    cproj = np.ascontiguousarray(np.asarray(inputs["L0_ffn_c_proj_w"], np.float32))
    router = np.ascontiguousarray(np.asarray(inputs["L0_ffn_router_w"], np.float32))
    w1 = np.ascontiguousarray(np.asarray(inputs["L0_ffn_w1"], np.float32))
    w2 = np.ascontiguousarray(np.asarray(inputs["L0_ffn_w2"], np.float32))
    gains = np.asarray(inputs["L0_attn_q_gain"], np.float32)
    gain_row = np.ascontiguousarray(gains[np.array(SLOT_HEADS)][None, :])
    gain_pairs = np.ascontiguousarray(
        np.stack([gains[np.array(EVENS)], gains[np.array(ODDS)]], axis=0))
    ln1w = np.ascontiguousarray(np.asarray(inputs["L0_ffn_ln1_w"], np.float32)[None, :])
    ln2w = np.ascontiguousarray(np.asarray(inputs["L0_ffn_ln2_w"], np.float32)[None, :])
    if consts is None:
        consts = _host_consts()
    in_maps = []
    for core in range(8):
        b_ = core // 2
        m = dict(
            xT=np.ascontiguousarray(x[b_].T),
            role=np.array([[core % 2]], dtype=np.int32),
            wqT=wqT, wkT=wkT, wvT=wvT, wpT=wpT, cattn=cattn, cproj=cproj,
            router=router, w1=w1, w2=w2, gain_row=gain_row,
            gain_pairs=gain_pairs, ln1w_row=ln1w, ln2w_row=ln2w,
        )
        m.update(consts)
        in_maps.append(m)
    return in_maps


def kernel(**inputs):
    nc = _get_nc("main")
    in_maps = make_in_maps(inputs)
    res = run_bass_kernel_spmd(nc, in_maps, core_ids=list(range(8)))
    out = np.empty((4, S, D), dtype=np.float32)
    for core in range(8):
        b_ = core // 2
        o = res.results[core]["out_tm"]
        if core % 2 == 0:
            out[b_, 0:SPLIT, :] = o
        else:
            out[b_, SPLIT:S, :] = o[0:S - SPLIT, :]
    return out


# version 17
# speedup vs baseline: 1.0609x; 1.0061x over previous
import contextlib
import numpy as np
import ml_dtypes

import concourse.bass as bass
import concourse.mybir as mybir
from concourse.bass_utils import run_bass_kernel_spmd

F32 = mybir.dt.float32
BF16 = mybir.dt.bfloat16
I32 = mybir.dt.int32
AF = mybir.ActivationFunctionType
ALU = mybir.AluOpType
AXX = mybir.AxisListType.X

S = 4096
D = 1024
H = 16
KVH = 4
HD = 64
E = 8
FF = 2048
SPLIT = 2304
EPS_LN = 1e-5
EPS_RMS = float(np.finfo(np.float32).eps)
NEG_BIG = -1.0e30
SCAL_VALS = [EPS_LN, EPS_RMS, -0.5, 1.0 / HD, 0.125, -0.125, 1.0, -1.0]

EVENS = [0, 1, 2, 3, 8, 9, 10, 11]
ODDS = [4, 5, 6, 7, 12, 13, 14, 15]
SLOT_HEADS = []
for _j in range(8):
    SLOT_HEADS += [EVENS[_j], ODDS[_j]]

ENGS = ("sync", "tensor", "vector", "scalar", "gpsimd")


def tiles(lo, hi, step=512):
    return [(t0, min(step, hi - t0)) for t0 in range(lo, hi, step)]


class Prog:
    def __init__(self, counters=None):
        self.q = {e: [] for e in ENGS}
        self.c = dict(counters or {})

    def now(self, s):
        return self.c.get(s, 0)

    DMA_SEMS = ("qd", "qg", "qg2", "qgs", "qg2s", "qx", "qs", "qa0", "qa1",
                "qw1", "qw2", "qm0", "qm1", "qm2", "qm3", "qn0", "qn1",
                "qn2", "qn3")

    def op(self, eng, fn, waits=(), inc=None, inc_n=1):
        ws = []
        for (sname, v) in waits:
            if v <= 0:
                continue
            assert v <= self.c.get(sname, 0), (
                f"future wait {sname}={v} (now {self.c.get(sname, 0)})")
            if sname in self.DMA_SEMS:
                v = self.c.get(sname, 0)
            ws.append((sname, v))
        agg = {}
        for s, v in ws:
            agg[s] = max(agg.get(s, 0), v)
        ws = list(agg.items())
        post = None
        if inc is not None:
            self.c[inc] = self.c.get(inc, 0) + inc_n
            post = self.c[inc]

        def emit(engine, sems, _fn=fn, _ws=tuple(ws), _inc=inc, _inc_n=inc_n):
            for (sname, v) in _ws:
                engine.wait_ge(sems[sname], v)
            inst = _fn(engine)
            if _inc is not None:
                assert inst is not None
                inst.then_inc(sems[_inc], _inc_n)

        self.q[eng].append(emit)
        return post

    def allc(self):
        return [(k, v) for k, v in self.c.items()]


def _host_consts():
    bf = ml_dtypes.bfloat16
    c = {}
    c["c_scal"] = np.tile(np.array([[1e-05, float(np.finfo(np.float32).eps), -0.5,
                                     1.0 / 64, 0.125, -0.125, 1.0, -1.0]],
                                    dtype=np.float32), (128, 1))
    c["c_identf"] = np.eye(128, dtype=np.float32)
    c["c_ones"] = np.ones((128, 128), dtype=bf)
    c["c_zeros"] = np.zeros((128, 128), dtype=bf)
    o2 = np.zeros((128, 2), dtype=bf)
    o2[0:64, 0] = 1.0
    o2[64:128, 1] = 1.0
    c["c_ones2blk"] = o2
    oz = np.zeros((128, 4), dtype=bf)
    oz[:, 0] = 1.0
    oz[:, 3] = 1.0
    c["c_onesLN"] = oz
    s2 = np.zeros((2, 128), dtype=bf)
    s2[0, 0:64] = 1.0
    s2[1, 64:128] = 1.0
    c["c_sel2"] = s2
    r = np.zeros((64, 64), dtype=np.float32)
    for i in range(32):
        r[i, i + 32] = 1.0
        r[i + 32, i] = -1.0
    r2 = np.zeros((128, 128), dtype=np.float32)
    r2[0:64, 0:64] = r
    r2[64:128, 64:128] = r
    c["c_r2t"] = np.ascontiguousarray(r2.T).astype(bf)
    trim = np.zeros((128, 4 * 512), dtype=bf)
    kk = np.arange(128)[:, None]
    qq = np.arange(512)[None, :]
    for pat in range(4):
        trim[:, pat * 512:(pat + 1) * 512] = (qq >= kk + 128 * pat).astype(bf)
    c["c_tri"] = trim
    inv = 1.0 / (10000.0 ** (np.arange(0, HD, 2, dtype=np.float32) / HD))
    pos = np.arange(S, dtype=np.float32)
    fr = pos[:, None] * inv[None, :]
    c["c_cos"] = np.ascontiguousarray(np.tile(np.cos(fr).T, (4, 1))).astype(bf)
    c["c_sin"] = np.ascontiguousarray(np.tile(np.sin(fr).T, (4, 1))).astype(bf)
    return c


BO = dict(
    slab=0, wps=8192, kbuf=16384, vbuf=24576, probs=32896, yts=36992,
    qbuf=41088, vtmp=42112, cstile=43152, xbf=44192, xsq=44704, qfin=45216, qtmp=45728,
    qsqt=46240, recdb=46752, mrbf=47776, cvb=48800, tri=51872, r2t=53920,
    identb=54048,
)
BO["xln"] = 54176
NB = 58272
BO["hsc"] = 0
BO["ges"] = 12288
BO["w1s"] = 15360
BO["w2s"] = 19456
BO["rts"] = 23552
BO["gsb"] = 23616
BO["sutm"] = 26688
FO = dict(
    xs=0, x3s=4096, uts=8192, dtmp=12288, rtmp1=12800, rtmp2=13312,
    rtmp3=13824, mrow=14336, rr=15872, recds=16896, rbs=16896, x1t=17920,
    cvf=18432, identf=21504,
)
NF = 21632
FO["rls"] = 12288
FO["ltt"] = 12800
FO["lmx"] = 12808
FO["gtm"] = 12816
FO["outs"] = 13312
FO["gsr"] = 13824
FO["g8r"] = 15872


def build_program(debug_taps=(), phase_limit=99):
    nc = bass.Bass("TRN2", target_bir_lowering=False)

    din = {}

    def dinp(name, shape, dt):
        din[name] = nc.dram_tensor(name, list(shape), dt, kind="ExternalInput").ap()
        return din[name]

    xT = dinp("xT", (D, S), F32)
    role_in = dinp("role", (1, 1), I32)
    wqT = dinp("wqT", (D, D), F32)
    wkT = dinp("wkT", (D, KVH * HD), F32)
    wvT = dinp("wvT", (D, KVH * HD), F32)
    wpT = dinp("wpT", (D, D), F32)
    cattn = dinp("cattn", (D, 3 * D), F32)
    cproj = dinp("cproj", (D, D), F32)
    router = dinp("router", (D, E), F32)
    w1 = dinp("w1", (D, E * FF), F32)
    w2 = dinp("w2", (E * FF, D), F32)
    gain_row = dinp("gain_row", (1, H), F32)
    gain_pairs = dinp("gain_pairs", (2, 8), F32)
    ln1w_row = dinp("ln1w_row", (1, D), F32)
    ln2w_row = dinp("ln2w_row", (1, D), F32)
    for name, shape, dt in [
        ("c_scal", (128, 8), F32),
        ("c_identf", (128, 128), F32), ("c_ones", (128, 128), BF16),
        ("c_zeros", (128, 128), BF16),
        ("c_ones2blk", (128, 2), BF16), ("c_onesLN", (128, 4), BF16),
        ("c_sel2", (2, 128), BF16), ("c_r2t", (128, 128), BF16),
        ("c_tri", (128, 4 * 512), BF16), ("c_cos", (128, S), BF16),
        ("c_sin", (128, S), BF16),
    ]:
        dinp(name, shape, dt)

    out_tm = nc.dram_tensor("out_tm", [SPLIT, D], F32, kind="ExternalOutput").ap()
    dbg = {}
    for name, shape in debug_taps:
        dbg[name] = nc.dram_tensor(name, list(shape), F32, kind="ExternalOutput").ap()

    def scrt(name, shape, dt):
        return nc.dram_tensor(name, list(shape), dt).ap()

    wqTb = scrt("wqTb", (D, D), BF16)
    wkTb = scrt("wkTb", (D, KVH * HD), BF16)
    wvTb = scrt("wvTb", (D, KVH * HD), BF16)
    wpTb = scrt("wpTb", (D, D), BF16)
    cattnb = scrt("cattnb", (D, 3 * D), BF16)
    cprojb = scrt("cprojb", (D, D), BF16)
    routerb = scrt("routerb", (D, E), BF16)
    w1b = scrt("w1b", (D, E * FF), BF16)
    w2b = scrt("w2b", (E * FF, D), BF16)
    qO_d = scrt("qO_d", (D, S), BF16)
    kO_d = scrt("kO_d", (KVH * HD, S), BF16)
    vO_d = scrt("vO_d", (2 * 128, 32 * 130), BF16)
    x1_d = scrt("x1_d", (D, S), F32)
    h2_d = scrt("h2_d", (D, S), F32)
    qI_d = scrt("qI_d", (D, S), BF16)
    kI_d = scrt("kI_d", (D, S), BF16)
    vI_d = scrt("vI_d", (8 * 128, 32 * 130), BF16)
    su_d = scrt("su_d", (D, S), BF16)
    uln_d = scrt("uln_d", (D, S), BF16)
    g_d = scrt("g_d", (E, S), F32)
    bnc_d = scrt("bnc_d", (2, 16), F32)

    est = contextlib.ExitStack()
    with est:
        sbx = {}

        def sbuf(name, shape, dt):
            sbx[name] = est.enter_context(nc.sbuf_tensor(name, list(shape), dt))
            return sbx[name]

        arb = sbuf("arb", (128, NB), BF16)
        arf = sbuf("arf", (128, NF), F32)

        def B_(name, rows=128):
            return arb[0:rows, BO[name]:]

        def F_(name, rows=128):
            return arf[0:rows, FO[name]:]

        slab = arb[:, BO["slab"]:BO["slab"] + 8192]
        wps = arb[:, BO["wps"]:BO["wps"] + 8192]
        kbuf = arb[:, BO["kbuf"]:BO["kbuf"] + 8192]
        vbuf = arb[:, BO["vbuf"]:BO["vbuf"] + 8320]
        probs = arb[:, BO["probs"]:BO["probs"] + 4096]
        yts = arb[:, BO["yts"]:BO["yts"] + 4096]
        qbuf = arb[:, BO["qbuf"]:BO["qbuf"] + 1024]
        vtmp = arb[:, BO["vtmp"]:BO["vtmp"] + 1040]
        cstile = arb[:, BO["cstile"]:BO["cstile"] + 1040]
        xbf = arb[:, BO["xbf"]:BO["xbf"] + 512]
        xsq = arb[:, BO["xsq"]:BO["xsq"] + 512]
        qfin = arb[:, BO["qfin"]:BO["qfin"] + 512]
        qtmp = arb[:, BO["qtmp"]:BO["qtmp"] + 512]
        qsqt = arb[:, BO["qsqt"]:BO["qsqt"] + 512]
        recdb = arb[:, BO["recdb"]:BO["recdb"] + 1024]
        mrbf = arb[0:1, BO["mrbf"]:BO["mrbf"] + 1024]
        cvb = arb[:, BO["cvb"]:BO["cvb"] + 3072]
        tri = arb[:, BO["tri"]:BO["tri"] + 2048]
        r2t = arb[:, BO["r2t"]:BO["r2t"] + 128]
        identb = arb[:, BO["identb"]:BO["identb"] + 128]
        hsc = arb[:, BO["hsc"]:BO["hsc"] + 12288]
        ges = arb[:, BO["ges"]:BO["ges"] + 3072]
        w1s = arb[:, BO["w1s"]:BO["w1s"] + 4096]
        w2s = arb[:, BO["w2s"]:BO["w2s"] + 4096]
        sutm = arb[:, BO["sutm"]:BO["sutm"] + 3072]
        rts = arb[:, BO["rts"]:BO["rts"] + 64]
        gsb = arb[0:1, BO["gsb"]:BO["gsb"] + 3072]

        xs = arf[:, FO["xs"]:FO["xs"] + 4096]
        x3s = arf[:, FO["x3s"]:FO["x3s"] + 4096]
        uts = arf[:, FO["uts"]:FO["uts"] + 4096]
        dtmp = arf[:, FO["dtmp"]:FO["dtmp"] + 512]
        rtmp1 = arf[:, FO["rtmp1"]:FO["rtmp1"] + 512]
        rtmp2 = arf[:, FO["rtmp2"]:FO["rtmp2"] + 512]
        rtmp3 = arf[:, FO["rtmp3"]:FO["rtmp3"] + 512]
        mrow = arf[0:1, FO["mrow"]:FO["mrow"] + 1536]
        rr = arf[0:2, FO["rr"]:FO["rr"] + 1024]
        recds = arf[:, FO["recds"]:FO["recds"] + 1024]
        rbs = arf[0:64, FO["rbs"]:FO["rbs"] + 1024]
        x1t = arf[:, FO["x1t"]:FO["x1t"] + 512]
        cvf = arf[:, FO["cvf"]:FO["cvf"] + 3072]
        identf = arf[:, FO["identf"]:FO["identf"] + 128]
        rls = arf[0:8, FO["rls"]:FO["rls"] + 512]
        ltt = arf[:, FO["ltt"]:FO["ltt"] + 8]
        lmx = arf[:, FO["lmx"]:FO["lmx"] + 8]
        gtm = arf[:, FO["gtm"]:FO["gtm"] + 8]
        outs = arf[:, FO["outs"]:FO["outs"] + 512]
        gsr = arf[0:1, FO["gsr"]:FO["gsr"] + 3072]
        g8r = arf[0:8, FO["g8r"]:FO["g8r"] + 128]

        def xln_a():
            return arb[:, BO["xln"]:BO["xln"] + 4096]

        scalvec = sbuf("scalvec", (128, 8), F32)
        for _i, _v in enumerate(SCAL_VALS):
            nc.const_aps.register(F32, float(_v), scalvec[:, _i:_i + 1])
        role_s = sbuf("role_s", (1, 1), I32)
        onesb = sbuf("onesb", (128, 128), BF16)
        zerosb = sbuf("zerosb", (128, 128), BF16)
        ones2blk = sbuf("ones2blk", (128, 2), BF16)
        onesLN = sbuf("onesLN", (128, 4), BF16)
        sel2 = sbuf("sel2", (2, 128), BF16)
        gainr = sbuf("gainr", (1, H), F32)
        gainp = sbuf("gainp", (2, 8), F32)
        ln1wb = sbuf("ln1wb", (1, D), BF16)
        ln2wb = sbuf("ln2wb", (1, D), BF16)

        cbO = sbuf("cbO", (128, H), F32)
        cbI = sbuf("cbI", (128, H), F32)
        cvoutb = sbuf("cvoutb", (1, H), BF16)
        qmaxs = sbuf("qmaxs", (2, 8), F32)
        kmaxs = sbuf("kmaxs", (2, 8), F32)
        bncs = sbuf("bncs", (1, 96), F32)
        cvalb = sbuf("cvalb", (1, H), BF16)

        pb = [est.enter_context(nc.psum_tensor(f"pb{i}", [128, 512], F32))
              for i in range(2)]
        pq = [est.enter_context(nc.psum_tensor(f"pq{i}", [128, 1024], F32))
              for i in range(3)]
        for _q in range(3):
            for _h in range(2):
                pb.append(pq[_q][:, _h * 512:(_h + 1) * 512])

        sems = {n: est.enter_context(nc.semaphore(n))
                for n in ["qd", "qg", "qg2", "qgs", "qg2s", "qx", "qs", "qa0",
                          "qa1", "qw1", "qw2", "qm0", "qm1", "qm2", "qm3",
                          "qn0", "qn1", "qn2", "qn3", "pe", "act", "dve",
                          "pool"]}

        block = est.enter_context(nc.Block())

        P0 = Prog()

        def dma(p, eng, out, in_, waits=(), sem="qd", **kw):
            return p.op(eng, lambda e: e.dma_start(out=out, in_=in_, **kw),
                        waits=waits, inc=sem, inc_n=16)

        dma(P0, "sync", scalvec[:, :], din["c_scal"])
        e_role = dma(P0, "sync", role_s[:, :], role_in)
        e_consts = e_role
        for dst, srcn in [(identf, "c_identf"), (onesb, "c_ones"),
                          (zerosb, "c_zeros"),
                          (ones2blk, "c_ones2blk"), (onesLN, "c_onesLN"),
                          (sel2, "c_sel2"), (r2t, "c_r2t"), (tri, "c_tri"),
                          (gainr, "gain_row"), (gainp, "gain_pairs")]:
            e_consts = dma(P0, "sync", dst[:, :], din[srcn])
        e_l1 = dma(P0, "sync", cvf[0:1, 0:D], din["ln1w_row"])
        e_l2 = dma(P0, "sync", cvf[1:2, 0:D] if False else cvf[0:1, 0:D], din["ln1w_row"])
        e_c1 = P0.op("vector", lambda e: e.tensor_copy(ln1wb[:, :], cvf[0:1, 0:D]),
                     waits=[("qd", e_l1)], inc="dve")
        e_l2 = dma(P0, "sync", cvf[0:1, 0:D], din["ln2w_row"],
                   waits=[("dve", e_c1)])
        e_c2 = P0.op("vector", lambda e: e.tensor_copy(ln2wb[:, :], cvf[0:1, 0:D]),
                     waits=[("qd", e_l2)], inc="dve")

        e_cv0 = P0.op("scalar",
                      lambda e: e.activation(bncs[0:1, 0:H], gainr[0:1, :], AF.Abs),
                      waits=[("qd", e_consts)], inc="act")
        e_cv = P0.op("vector",
                     lambda e: e.tensor_scalar_mul(cvoutb[0:1, 0:H],
                                                   bncs[0:1, 0:H], -8.0),
                     waits=[("act", e_cv0)], inc="dve")
        e_cbm = P0.op("tensor",
                      lambda e: e.matmul(pb[7][:, 0:H], onesb[0:1, 0:128],
                                         cvoutb[0:1, 0:H], start=True, stop=True),
                      waits=[("dve", e_cv)], inc="pe")
        e_cbO = P0.op("scalar", lambda e: e.copy(cbO[:, :], pb[7][:, 0:H]),
                      waits=[("pe", e_cbm)], inc="act")

        conv_done = {}
        cv_guard = [None, None, None]
        ji = 0
        for (src, dst) in [(wqT, wqTb), (wkT, wkTb), (wvT, wvTb), (wpT, wpTb),
                           (cattn, cattnb), (cproj, cprojb), (router, routerb),
                           (w1, w1b), (w2, w2b)]:
            big = dst.tensor.name in ("w1b", "w2b")
            sm = "qg2" if big else "qg"
            sms = "qg2s" if big else "qgs"
            rows, cols = src.tensor.shape
            for r0 in range(0, rows, 128):
                rh = min(128, rows - r0)
                for c0 in range(0, cols, 1024):
                    cw = min(1024, cols - c0)
                    h = ji % 3
                    w_ = [cv_guard[h]] if cv_guard[h] else []
                    e_ld = dma(P0, "gpsimd", cvf[0:rh, h * 1024:h * 1024 + cw],
                               src[r0:r0 + rh, c0:c0 + cw],
                               waits=w_ + ([("dve", e_c2)] if ji < 3 else []),
                               sem=sm)
                    e_cp = P0.op("gpsimd",
                                 lambda e, rh=rh, h=h, cw=cw: e.tensor_copy(
                                     cvb[0:rh, h * 1024:h * 1024 + cw],
                                     cvf[0:rh, h * 1024:h * 1024 + cw]),
                                 waits=[(sm, e_ld)], inc="pool")
                    e_st = dma(P0, "gpsimd", dst[r0:r0 + rh, c0:c0 + cw],
                               cvb[0:rh, h * 1024:h * 1024 + cw],
                               waits=[("pool", e_cp)], sem=sm)
                    cv_guard[h] = (sm, e_st)
                    ji += 1
            conv_done[dst.tensor.name] = (sm, P0.now(sm))

        shared_counts = dict(P0.c)

        def build_role(role):
            p = Prog(shared_counts)
            if role == 0:
                own = (0, SPLIT)
                full = (0, SPLIT)
            else:
                own = (SPLIT, S)
                full = (0, S)
            t_full = tiles(*full)
            t_own = tiles(*own)
            t_moe = tiles(own[0], own[1], 384)

            G = {}

            def guard(key):
                return list(G.get(key, []))

            def setg(key, semname, cnt):
                G[key] = [(semname, cnt)]

            def addg(key, semname, cnt):
                G.setdefault(key, []).append((semname, cnt))

            def ln_tile(src_sb, w, dst_sb, wrow_b, out_f32_sb=None, eps=EPS_LN,
                        src_ready=()):
                for c in range(8):
                    e_bf = p.op("vector",
                                lambda e, c=c, w=w: e.tensor_copy(
                                    xbf[:, 0:w], src_sb[:, c * 512:c * 512 + w]),
                                waits=list(src_ready) + guard(("xbf",)), inc="dve")
                    e_sq = p.op("scalar",
                                lambda e, c=c, w=w: e.activation(
                                    xsq[:, 0:w], src_sb[:, c * 512:c * 512 + w],
                                    AF.Square),
                                waits=list(src_ready) + guard(("xsq",)), inc="act")
                    e_m1 = p.op("tensor",
                                lambda e, c=c, w=w: e.matmul(
                                    pb[2][0:1, 0:w], onesb[:, 0:1], xbf[:, 0:w],
                                    start=(c == 0), stop=(c == 7)),
                                waits=[("dve", e_bf)] +
                                      (guard(("bank", 2)) if c == 0 else []),
                                inc="pe")
                    setg(("xbf",), "pe", e_m1)
                    e_m2 = p.op("tensor",
                                lambda e, c=c, w=w: e.matmul(
                                    pb[3][0:1, 0:w], onesb[:, 0:1], xsq[:, 0:w],
                                    start=(c == 0), stop=(c == 7)),
                                waits=[("act", e_sq)] +
                                      (guard(("bank", 3)) if c == 0 else []),
                                inc="pe")
                    setg(("xsq",), "pe", e_m2)
                e_mm = p.now("pe")
                e_m = p.op("vector",
                           lambda e, w=w: e.tensor_scalar_mul(mrow[0:1, 0:w],
                                                              pb[2][0:1, 0:w], 1.0 / D),
                           waits=[("pe", e_mm)] + guard(("mrow",)), inc="dve")
                e_ex = p.op("vector",
                            lambda e, w=w: e.tensor_scalar_mul(mrow[0:1, 512:512 + w],
                                                               pb[3][0:1, 0:w], 1.0 / D),
                            inc="dve")
                setg(("bank", 2), "dve", e_ex)
                setg(("bank", 3), "dve", e_ex)
                e_m2r = p.op("vector",
                             lambda e, w=w: e.tensor_mul(mrow[0:1, 1024:1024 + w],
                                                         mrow[0:1, 0:w], mrow[0:1, 0:w]),
                             waits=[("dve", e_m)], inc="dve")
                e_var = p.op("vector",
                             lambda e, w=w: e.tensor_sub(mrow[0:1, 512:512 + w],
                                                         mrow[0:1, 512:512 + w],
                                                         mrow[0:1, 1024:1024 + w]),
                             waits=[("dve", e_m2r)], inc="dve")
                e_lnv = p.op("scalar",
                             lambda e, w=w, eps=eps: e.activation(
                                 mrow[0:1, 1024:1024 + w], mrow[0:1, 512:512 + w],
                                 AF.Ln, bias=eps),
                             waits=[("dve", e_var)], inc="act")
                e_rst = p.op("scalar",
                             lambda e, w=w: e.activation(mrow[0:1, 512:512 + w],
                                                         mrow[0:1, 1024:1024 + w],
                                                         AF.Exp, scale=-0.5),
                             waits=[("act", e_lnv)], inc="act")
                e_mb = p.op("vector",
                            lambda e, w=w: e.tensor_copy(mrbf[0:1, 0:w], mrow[0:1, 0:w]),
                            waits=[("dve", e_var)] + guard(("mrbf",)), inc="dve")
                e_rb = p.op("vector",
                            lambda e, w=w: e.tensor_copy(mrbf[0:1, 512:512 + w],
                                                         mrow[0:1, 512:512 + w]),
                            waits=[("act", e_rst)], inc="dve")
                setg(("mrow",), "dve", e_rb)
                e_bm = p.op("tensor",
                            lambda e, w=w: e.matmul(pb[3][:, 0:w], onesb[0:1, 0:128],
                                                    mrbf[0:1, 0:w], start=True, stop=True),
                            waits=[("dve", e_mb)] + guard(("bank", 3)), inc="pe")
                last = None
                for c in range(8):
                    wr_ap = (onesb[0:1, 0:128] if wrow_b is None
                             else wrow_b[0:1, c * 128:(c + 1) * 128])
                    dslot = dtmp if c % 2 == 0 else outs
                    dkey = ("dtmp", c % 2)
                    e_br = p.op("tensor",
                                lambda e, c=c, w=w, wr_ap=wr_ap: e.matmul(
                                    pb[4][:, 0:w], wr_ap,
                                    mrbf[0:1, 512:512 + w], start=True, stop=True),
                                waits=[("dve", e_rb)] + guard(("bank", 4)), inc="pe")
                    e_d = p.op("vector",
                               lambda e, c=c, w=w, dslot=dslot: e.tensor_sub(
                                   dslot[:, 0:w], src_sb[:, c * 512:c * 512 + w],
                                   pb[3][:, 0:w]),
                               waits=[("pe", e_bm), ("pe", e_br)] + guard(dkey),
                               inc="dve")
                    e_a = p.op("vector",
                               lambda e, c=c, w=w, dslot=dslot: e.tensor_mul(
                                   dst_sb[:, c * 512:c * 512 + w], dslot[:, 0:w],
                                   pb[4][:, 0:w]),
                               waits=[("dve", e_d)], inc="dve")
                    if out_f32_sb is not None:
                        e_a = p.op("vector",
                                   lambda e, c=c, w=w, dslot=dslot: e.tensor_mul(
                                       out_f32_sb[:, c * 512:c * 512 + w],
                                       dslot[:, 0:w], pb[4][:, 0:w]),
                                   inc="dve")
                    setg(dkey, "dve", e_a)
                    setg(("bank", 4), "dve", e_a)
                    last = e_a
                setg(("bank", 3), "dve", last)
                setg(("mrbf",), "dve", last)
                return last

            def rope_norm(kind, cpair, t0, w, bank, dests, e_qmm):
                e_raw = p.op("scalar",
                             lambda e, w=w, bank=bank: e.copy(qtmp[:, 0:w],
                                                              pb[bank][:, 0:w]),
                             waits=[("pe", e_qmm)] + guard(("qtmp",)), inc="act")
                e_sq = p.op("scalar",
                            lambda e, w=w, bank=bank: e.activation(
                                qsqt[:, 0:w], pb[bank][:, 0:w], AF.Square),
                            waits=[("pe", e_qmm)] + guard(("qsqt",)), inc="act")
                setg(("bank", bank), "act", e_sq)
                e_ss = p.op("tensor",
                            lambda e, w=w: e.matmul(pb[5][0:2, 0:w], ones2blk[:, :],
                                                    qsqt[:, 0:w], start=True, stop=True),
                            waits=[("act", e_sq)] + guard(("bank", 5)), inc="pe")
                setg(("qsqt",), "pe", e_ss)
                e_l = p.op("scalar",
                           lambda e, w=w: e.activation(rr[0:2, 0:w], pb[5][0:2, 0:w],
                                                       AF.Ln, bias=EPS_RMS,
                                                       scale=1.0 / HD),
                           waits=[("pe", e_ss)] + guard(("rr",)), inc="act")
                setg(("bank", 5), "act", e_l)
                e_r = p.op("scalar",
                           lambda e, w=w: e.activation(rr[0:2, 512:512 + w],
                                                       rr[0:2, 0:w], AF.Exp, scale=-0.5),
                           waits=[("act", e_l)], inc="act")
                setg(("rr",), "act", e_r)
                if kind == "q":
                    e_g = p.op("vector",
                               lambda e, w=w, cpair=cpair: e.tensor_scalar_mul(
                                   xbf[0:2, 0:w], rr[0:2, 512:512 + w],
                                   gainp[:, cpair:cpair + 1]),
                               waits=[("act", e_r)] + guard(("xbf",)), inc="dve")
                else:
                    e_g = p.op("vector",
                               lambda e, w=w: e.tensor_copy(xbf[0:2, 0:w],
                                                            rr[0:2, 512:512 + w]),
                               waits=[("act", e_r)] + guard(("xbf",)), inc="dve")
                e_bc = p.op("tensor",
                            lambda e, w=w: e.matmul(pb[6][:, 0:w], sel2[:, :],
                                                    xbf[0:2, 0:w], start=True, stop=True),
                            waits=[("dve", e_g)] + guard(("bank", 6)), inc="pe")
                setg(("xbf",), "pe", e_bc)
                e_rot = p.op("tensor",
                             lambda e, w=w: e.matmul(pb[7][:, 0:w], r2t[:, :],
                                                     qtmp[:, 0:w], start=True, stop=True),
                             waits=[("act", e_raw)] + guard(("bank", 7)), inc="pe")
                e_t1 = p.op("vector",
                            lambda e, w=w: e.tensor_mul(
                                rtmp1[:, 0:w], qtmp[:, 0:w], cstile[:, 0:w]),
                            waits=[("act", e_raw), ("qx", 1)] + guard(("rtmp1",)), inc="dve")
                setg(("qtmp",), "dve", e_t1)
                e_t2 = p.op("vector",
                            lambda e, w=w: e.tensor_mul(
                                rtmp2[:, 0:w], pb[7][:, 0:w], cstile[:, 520:520 + w]),
                            waits=[("pe", e_rot)] + guard(("rtmp2",)), inc="dve")
                setg(("bank", 7), "dve", e_t2)
                e_t3 = p.op("vector",
                            lambda e, w=w: e.tensor_add(rtmp3[:, 0:w], rtmp1[:, 0:w],
                                                        rtmp2[:, 0:w]),
                            waits=[("dve", e_t1), ("dve", e_t2)] + guard(("rtmp3",)),
                            inc="dve")
                setg(("rtmp1",), "dve", e_t3)
                setg(("rtmp2",), "dve", e_t3)
                e_f = None
                for (r0, _, _) in dests:
                    e_f = p.op("vector",
                               lambda e, r0=r0, w=w: e.tensor_mul(
                                   qfin[r0:r0 + 64, 0:w], rtmp3[r0:r0 + 64, 0:w],
                                   pb[6][r0:r0 + 64, 0:w]),
                               waits=[("dve", e_t3), ("pe", e_bc)] + guard(("qfin",)),
                               inc="dve")
                setg(("rtmp3",), "dve", e_f)
                setg(("bank", 6), "dve", e_f)
                e_st = None
                for (r0, dram, dr0) in dests:
                    e_st = dma(p, "sync", dram[dr0:dr0 + 64, t0:t0 + w],
                               qfin[r0:r0 + 64, 0:w], waits=[("dve", e_f)],
                               sem="qs")
                setg(("qfin",), "qs", e_st)

            e_wk = dma(p, "sync", slab[:, 0:2048].rearrange("p (c n) -> p c n", n=256),
                       wkTb.rearrange("(c p) n -> p c n", p=128),
                       waits=[conv_done["wkTb"]], sem="qw1")
            e_wv = dma(p, "sync", slab[:, 2048:4096].rearrange("p (c n) -> p c n", n=256),
                       wvTb.rearrange("(c p) n -> p c n", p=128),
                       waits=[conv_done["wvTb"]], sem="qw1")

            def p2_tile(t0, w):
                e_cs = dma(p, "sync", cstile[:, 0:w], din["c_cos"][:, t0:t0 + w],
                           waits=guard(("cstile",)), sem="qx")
                e_cs = dma(p, "sync", cstile[:, 520:520 + w],
                           din["c_sin"][:, t0:t0 + w], sem="qx")
                e_x = dma(p, "sync",
                          xs.rearrange("p (c n) -> p c n", n=512)[:, :, 0:w],
                          xT.rearrange("(c p) n -> p c n", p=128)[:, :, t0:t0 + w],
                          waits=guard(("xs",)), sem="qx")
                ln_tile(xs, w, xln_a(), None, src_ready=[("qx", e_x)])
                e_ln = p.now("dve")
                setg(("xs",), "dve", e_ln)
                rope_pend = []

                def flush_rope():
                    while rope_pend:
                        args = rope_pend.pop(0)
                        rope_norm(*args)

                for half in range(2):
                    e_wq = dma(p, "sync",
                               slab[:, 4096:8192].rearrange("p (c n) -> p c n", n=512),
                               wqTb.rearrange("(c p) n -> p c n", p=128)[
                                   :, :, half * 512:half * 512 + 512],
                               waits=[conv_done["wqTb"]] + guard(("wqh",)),
                               sem="qw1")
                    for cc in range(4):
                        c = 4 * half + cc
                        bank = c % 2
                        for d_ in range(8):
                            p.op("tensor",
                                 lambda e, cc=cc, d_=d_, bank=bank, w=w: e.matmul(
                                     pb[bank][:, 0:w],
                                     slab[:, 4096 + d_ * 512 + 128 * cc:
                                          4096 + d_ * 512 + 128 * cc + 128],
                                     xln_a()[:, d_ * 512: d_ * 512 + w],
                                     start=(d_ == 0), stop=(d_ == 7)),
                                 waits=([("dve", e_ln), ("qw1", e_wq)] +
                                        guard(("bank", bank)) if d_ == 0 else []),
                                 inc=("pe" if d_ == 7 else None))
                        rope_pend.append(("q", c, t0, w, bank,
                                          [(0, qO_d, 128 * c), (64, qO_d, 128 * c + 64)],
                                          p.now("pe")))
                        if len(rope_pend) > 1:
                            rope_norm(*rope_pend.pop(0))
                    flush_rope()
                    setg(("wqh",), "pe", p.now("pe"))
                for ck in range(2):
                    bank = ck % 2
                    for d_ in range(8):
                        p.op("tensor",
                             lambda e, ck=ck, d_=d_, bank=bank, w=w: e.matmul(
                                 pb[bank][:, 0:w],
                                 slab[:, d_ * 256 + 128 * ck: d_ * 256 + 128 * ck + 128],
                                 xln_a()[:, d_ * 512: d_ * 512 + w],
                                 start=(d_ == 0), stop=(d_ == 7)),
                             waits=([("qw1", e_wk)] + guard(("bank", bank))
                                    if d_ == 0 else []),
                             inc=("pe" if d_ == 7 else None))
                    rope_pend.append(("k", ck, t0, w, bank,
                                      [(0, kO_d, 128 * ck), (64, kO_d, 128 * ck + 64)],
                                      p.now("pe")))
                    if len(rope_pend) > 1:
                        rope_norm(*rope_pend.pop(0))
                flush_rope()
                ntok = (w + 127) // 128
                for s_ in range(ntok):
                    tw = min(128, w - s_ * 128)
                    bank = s_ % 2
                    for d_ in range(8):
                        p.op("tensor",
                             lambda e, s_=s_, d_=d_, bank=bank, tw=tw: e.matmul(
                                 pb[bank][0:tw, 0:256],
                                 xln_a()[:, d_ * 512 + s_ * 128: d_ * 512 + s_ * 128 + tw],
                                 slab[:, 2048 + d_ * 256: 2048 + d_ * 256 + 256],
                                 start=(d_ == 0), stop=(d_ == 7)),
                             waits=([("qw1", e_wv)] + guard(("bank", bank))
                                    if d_ == 0 else []),
                             inc=("pe" if d_ == 7 else None))
                    e_vm = p.now("pe")
                    e_ms = p.op("vector",
                                lambda e, tw=tw: e.memset(vtmp[0:tw, 0:260], 1.0),
                                waits=guard(("vtmp",)), inc="dve")
                    e_vc = p.op("vector",
                                lambda e, tw=tw, bank=bank: e.tensor_copy(
                                    vtmp[0:tw, 0:260].rearrange(
                                        "p (g x) -> p g x", x=65)[:, :, 0:64],
                                    pb[bank][0:tw, 0:256].rearrange(
                                        "p (g x) -> p g x", x=64)),
                                waits=[("dve", e_ms), ("pe", e_vm)], inc="dve")
                    setg(("bank", bank), "dve", e_vc)
                    kc_ = (t0 + s_ * 128) // 128
                    e_vs = None
                    for g2 in range(2):
                        e_vs = dma(p, "sync",
                                   vO_d[128 * g2: 128 * g2 + tw,
                                        kc_ * 130: kc_ * 130 + 130],
                                   vtmp[0:tw, 130 * g2: 130 * g2 + 130],
                                   waits=([("dve", e_vc)] if g2 == 0 else []),
                                   sem="qs")
                    setg(("vtmp",), "qs", e_vs)
                setg(("cstile",), "dve", p.now("dve"))

            if phase_limit == 2:
                for (t0, w) in t_full:
                    p2_tile(t0, w)

            setg(("kbuf", 0), "dve", p.now("dve"))
            setg(("kbuf", 1), "dve", p.now("dve"))

            def attention_block(t_list, qsrc, vsrc, k_load, cb_sb, cb_col, resid_fn,
                                interleave=None):
                def emit_loads(ti, t0, w, j):
                    kext = ((t0 + w + 127) // 128) * 128
                    nkc = kext // 128
                    jb = j % 2
                    sm = f"qa{jb}"
                    e_k = k_load(j, jb, kext)
                    vrow = 128 * ((EVENS[j] // 4) // 2) if vsrc is vO_d else 128 * j
                    e_v = dma(p, "sync",
                              vbuf[:, jb * 4160: jb * 4160 + nkc * 130],
                              vsrc[vrow:vrow + 128, 0:nkc * 130],
                              waits=[("qs", 1)] + guard(("vbuf", jb)), sem=sm)
                    e_q = dma(p, "sync", qbuf[:, jb * 512: jb * 512 + w],
                              qsrc[128 * j:128 * j + 128, t0:t0 + w],
                              waits=[("qs", 1)] + guard(("qbuf", jb)), sem=sm)
                    return (sm, e_k, e_v, e_q)

                pend = {}
                for ti, (t0, w) in enumerate(t_list):
                    if interleave is not None:
                        interleave(ti)
                    kext = ((t0 + w + 127) // 128) * 128
                    nkc = kext // 128
                    pend[0] = emit_loads(ti, t0, w, 0)
                    pend[1] = emit_loads(ti, t0, w, 1)
                    for j in range(8):
                        jb = j % 2
                        sm, e_k, e_v, e_q = pend.pop(j)
                        ca = cb_col(j, 0, ti)
                        cb_ = cb_col(j, 1, ti)
                        nkp = (nkc + 1) // 2
                        dep = {}
                        last_ya = last_yb = None

                        def quad_of(kp, hb):
                            return (2 * kp + hb) % 3

                        for kp in range(nkp + 1):
                            if kp < nkp:
                                ks = [2 * kp] + ([2 * kp + 1]
                                                 if 2 * kp + 1 < nkc else [])
                                qa, qb = quad_of(kp, 0), quad_of(kp, 1)
                                for k in ks:
                                    kh = k % 2
                                    e_sa = p.op("tensor",
                                                lambda e, k=k, kh=kh, qa=qa, jb=jb, w=w: e.matmul(
                                                    pq[qa][:, kh * 512: kh * 512 + w],
                                                    kbuf[0:64, jb * 4096 + 128 * k:
                                                         jb * 4096 + 128 * k + 128],
                                                    qbuf[0:64, jb * 512: jb * 512 + w],
                                                    start=True, stop=True),
                                                waits=[(sm, e_k), (sm, e_q)] +
                                                      (guard(("bank", 2 + 2 * qa)) +
                                                       guard(("bank", 3 + 2 * qa))
                                                       if k == ks[0] else []) +
                                                      (guard(("bank", 0)) if k == 0 else []),
                                                inc="pe")
                                    e_sb = p.op("tensor",
                                                lambda e, k=k, kh=kh, qb=qb, jb=jb, w=w: e.matmul(
                                                    pq[qb][:, kh * 512: kh * 512 + w],
                                                    kbuf[64:128, jb * 4096 + 128 * k:
                                                         jb * 4096 + 128 * k + 128],
                                                    qbuf[64:128, jb * 512: jb * 512 + w],
                                                    start=True, stop=True,
                                                    tile_position=(64, 0)),
                                                waits=(guard(("bank", 2 + 2 * qb)) +
                                                       guard(("bank", 3 + 2 * qb))
                                                       if k == ks[0] else []) +
                                                      (guard(("bank", 1)) if k == 0 else []),
                                                inc="pe")
                                e_s_last = p.now("pe")
                                kw = 512 * (len(ks) - 1) + w
                                ca = cb_col(j, 0, ti)
                                cb_ = cb_col(j, 1, ti)
                                pslot_a = kp % 2
                                pslot_b = 2 + kp % 2
                                e_ea = p.op("scalar",
                                            lambda e, qa=qa, kw=kw, pslot_a=pslot_a, ca=ca: e.activation(
                                                probs[:, pslot_a * 1024: pslot_a * 1024 + kw],
                                                pq[qa][:, 0:kw], AF.Exp, scale=0.125,
                                                bias=cb_sb[:, ca:ca + 1]),
                                            waits=[("pe", e_s_last)] +
                                                  guard(("probs", pslot_a)),
                                            inc="act")
                                setg(("bank", 2 + 2 * qa), "act", e_ea)
                                setg(("bank", 3 + 2 * qa), "act", e_ea)
                                e_eb = p.op("scalar",
                                            lambda e, qb=qb, kw=kw, pslot_b=pslot_b, cb_=cb_: e.activation(
                                                probs[:, pslot_b * 1024: pslot_b * 1024 + kw],
                                                pq[qb][:, 0:kw], AF.Exp, scale=0.125,
                                                bias=cb_sb[:, cb_:cb_ + 1]),
                                            waits=[("pe", e_s_last)] +
                                                  guard(("probs", pslot_b)),
                                            inc="act")
                                setg(("bank", 2 + 2 * qb), "act", e_eb)
                                setg(("bank", 3 + 2 * qb), "act", e_eb)
                                deps_a = ("act", e_ea)
                                deps_b = ("act", e_eb)
                                for k in ks:
                                    dd = t0 - 128 * k
                                    if -w < dd < 128:
                                        pi = (-dd) // 128
                                        kh = k % 2
                                        e_ma = p.op("vector",
                                                    lambda e, pslot_a=pslot_a, kh=kh, pi=pi, w=w: e.tensor_mul(
                                                        probs[:, pslot_a * 1024 + kh * 512:
                                                              pslot_a * 1024 + kh * 512 + w],
                                                        probs[:, pslot_a * 1024 + kh * 512:
                                                              pslot_a * 1024 + kh * 512 + w],
                                                        tri[:, pi * 512: pi * 512 + w]),
                                                    waits=[("act", e_ea)], inc="dve")
                                        e_mb = p.op("vector",
                                                    lambda e, pslot_b=pslot_b, kh=kh, pi=pi, w=w: e.tensor_mul(
                                                        probs[:, pslot_b * 1024 + kh * 512:
                                                              pslot_b * 1024 + kh * 512 + w],
                                                        probs[:, pslot_b * 1024 + kh * 512:
                                                              pslot_b * 1024 + kh * 512 + w],
                                                        tri[:, pi * 512: pi * 512 + w]),
                                                    waits=[("act", e_eb)], inc="dve")
                                        deps_a = ("dve", e_ma)
                                        deps_b = ("dve", e_mb)
                                dep[kp] = (ks, pslot_a, pslot_b, deps_a, deps_b)
                            kpp = kp - 1
                            if kpp >= 0:
                                ksp, psa, psb, da, db = dep.pop(kpp)
                                for k in ksp:
                                    kh = k % 2
                                    w_ya = [da, (sm, e_v)]
                                    if k == 0:
                                        w_ya += guard(("bank", 0))
                                    last_ya = p.op("tensor",
                                                   lambda e, k=k, kh=kh, psa=psa, jb=jb,
                                                          w=w, nkc=nkc: e.matmul(
                                                       pb[0][0:65, 0:w],
                                                       vbuf[:, jb * 4160 + k * 130:
                                                            jb * 4160 + k * 130 + 65],
                                                       probs[:, psa * 1024 + kh * 512:
                                                             psa * 1024 + kh * 512 + w],
                                                       start=(k == 0),
                                                       stop=(k == nkc - 1)),
                                                   waits=w_ya, inc="pe")
                                    w_yb = [db]
                                    if k == 0:
                                        w_yb += guard(("bank", 1))
                                    last_yb = p.op("tensor",
                                                   lambda e, k=k, kh=kh, psb=psb, jb=jb,
                                                          w=w, nkc=nkc: e.matmul(
                                                       pb[1][0:65, 0:w],
                                                       vbuf[:, jb * 4160 + k * 130 + 65:
                                                            jb * 4160 + k * 130 + 130],
                                                       probs[:, psb * 1024 + kh * 512:
                                                             psb * 1024 + kh * 512 + w],
                                                       start=(k == 0),
                                                       stop=(k == nkc - 1)),
                                                   waits=w_yb, inc="pe")
                                setg(("probs", psa), "pe", last_ya)
                                setg(("probs", psb), "pe", last_yb)
                        setg(("vbuf", jb), "pe", last_yb)
                        setg(("qbuf", jb), "pe", last_yb)
                        setg(("kbuf", jb), "pe", last_yb)
                        if j + 2 < 8:
                            pend[j + 2] = emit_loads(ti, t0, w, j + 2)
                        e_yn_last = None
                        for hb in range(2):
                            e_rc = p.op("scalar",
                                        lambda e, hb=hb, w=w: e.activation(
                                            recds[64:65, hb * 512:hb * 512 + w],
                                            pb[hb][64:65, 0:w], AF.Ln),
                                        waits=[("pe", last_ya), ("pe", last_yb)] +
                                              guard(("recds", hb)),
                                        inc="act")
                            e_rcb = p.op("scalar",
                                         lambda e, hb=hb, w=w: e.activation(
                                             recdb[64:65, hb * 512:hb * 512 + w],
                                             recds[64:65, hb * 512:hb * 512 + w],
                                             AF.Exp, scale=-1.0),
                                         waits=[("act", e_rc)] + guard(("recdb", hb)),
                                         inc="act")
                            setg(("recds", hb), "act", e_rcb)
                            e_rbm = p.op("tensor",
                                         lambda e, hb=hb, w=w: e.matmul(
                                             pb[6][0:64, 0:w], onesb[64:65, 0:64],
                                             recdb[64:65, hb * 512:hb * 512 + w],
                                             start=True, stop=True),
                                         waits=[("act", e_rcb)] + guard(("bank", 6)),
                                         inc="pe")
                            setg(("recdb", hb), "pe", e_rbm)
                            e_rbe = p.op("vector",
                                         lambda e, hb=hb, w=w: e.tensor_copy(
                                             rbs[0:64, hb * 512:hb * 512 + w],
                                             pb[6][0:64, 0:w]),
                                         waits=[("pe", e_rbm)] + guard(("rbs", hb)),
                                         inc="dve")
                            setg(("bank", 6), "dve", e_rbe)
                            e_yn = p.op("vector",
                                        lambda e, hb=hb, j=j, w=w: e.tensor_mul(
                                            yts[:, j * 512:j * 512 + w][
                                                64 * hb:64 * hb + 64, :],
                                            pb[hb][0:64, 0:w],
                                            rbs[0:64, hb * 512:hb * 512 + w]),
                                        waits=[("dve", e_rbe)] + guard(("yts", j)),
                                        inc="dve")
                            setg(("rbs", hb), "dve", e_yn)
                            setg(("bank", hb), "dve", e_yn)
                            e_yn_last = e_yn
                        setg(("yts", j), "dve", e_yn_last)
                    resid_fn(ti, t0, w)

            def k_load_outer(j, jb, kext):
                ga = EVENS[j] // 4
                gb = ODDS[j] // 4
                dma(p, "sync", kbuf[0:64, jb * 4096: jb * 4096 + kext],
                    kO_d[64 * ga:64 * ga + 64, 0:kext],
                    waits=[("qs", 1)] + guard(("kbuf", jb)), sem=f"qa{jb}")
                return dma(p, "sync", kbuf[64:128, jb * 4096: jb * 4096 + kext],
                           kO_d[64 * gb:64 * gb + 64, 0:kext], sem=f"qa{jb}")

            def outer_resid(ti, t0, w):
                e_xr = dma(p, "sync",
                           x3s.rearrange("p (c n) -> p c n", n=512)[:, :, 0:w],
                           xT.rearrange("(c p) n -> p c n", p=128)[:, :, t0:t0 + w],
                           waits=guard(("x3s",)), sem="qx")
                yts_done = p.now("dve")
                for c in range(8):
                    bank = 2 + c % 2
                    for jj in range(8):
                        p.op("tensor",
                             lambda e, c=c, jj=jj, bank=bank, w=w: e.matmul(
                                 pb[bank][:, 0:w],
                                 wps[:, jj * 1024 + 128 * c: jj * 1024 + 128 * c + 128],
                                 yts[:, jj * 512:jj * 512 + w],
                                 start=(jj == 0), stop=(jj == 7)),
                             waits=([("qw1", e_wp2), ("dve", yts_done)] +
                                    guard(("bank", bank)) if jj == 0 else []),
                             inc=("pe" if jj == 7 else None))
                    e_pm = p.now("pe")
                    e_r = p.op("vector",
                               lambda e, c=c, bank=bank, w=w: e.scalar_tensor_tensor(
                                   x1t[:, 0:w], pb[bank][:, 0:w], 1.0,
                                   x3s[:, c * 512:c * 512 + w], ALU.mult, ALU.add),
                               waits=[("pe", e_pm), ("qx", e_xr)] + guard(("x1t",)),
                               inc="dve")
                    setg(("bank", bank), "dve", e_r)
                    e_st = dma(p, "sync", x1_d[128 * c: 128 * c + 128, t0:t0 + w],
                               x1t[:, 0:w], waits=[("dve", e_r)], sem="qs")
                    setg(("x1t",), "qs", e_st)
                setg(("x3s",), "dve", p.now("dve"))
                for j in range(8):
                    setg(("yts", j), "pe", p.now("pe"))

            if phase_limit >= 3:
                e_wp2 = dma(p, "sync", wps.rearrange("p (c n) -> p c n", n=1024),
                            wpTb.rearrange("(c p) n -> p c n", p=128),
                            waits=[conv_done["wpTb"]], sem="qw1")
                attention_block(t_full, qO_d, vO_d, k_load_outer, cbO,
                                lambda j, hb, ti: 2 * j + hb, outer_resid,
                                interleave=lambda ti: p2_tile(*t_full[ti]))

            if phase_limit >= 4:
                p.op("vector", lambda e: e.memset(qmaxs[:, :], 0.0), inc="dve")
                p.op("vector", lambda e: e.memset(kmaxs[:, :], 0.0), inc="dve")
                for ti, (t0, w) in enumerate(t_full):
                    e_x1 = dma(p, "sync",
                               xs.rearrange("p (c n) -> p c n", n=512)[:, :, 0:w],
                               x1_d.rearrange("(c p) n -> p c n", p=128)[:, :, t0:t0 + w],
                               waits=[("qs", 1)] + guard(("xs",)), sem="qx")
                    uts_g = []
                    for c in range(8):
                        uts_g += guard(("uts", c))
                    ln_tile(xs, w, xln_a(), None, out_f32_sb=uts,
                            src_ready=[("qx", e_x1)] + uts_g)
                    e_h2 = p.now("dve")
                    setg(("xs",), "dve", e_h2)
                    lo = max(t0, own[0]) - t0
                    hi = min(t0 + w, own[1]) - t0
                    e_h2s = 0
                    if hi > lo:
                        for c in range(8):
                            e_h2s = dma(p, "sync",
                                        h2_d[128 * c:128 * c + 128, t0 + lo:t0 + hi],
                                        uts[:, c * 512 + lo: c * 512 + hi],
                                        waits=[("dve", e_h2)], sem="qs")
                    ln_tile(uts, w, xln_a(), ln1wb, src_ready=[("dve", e_h2)])
                    e_z = p.now("dve")
                    for c in range(8):
                        setg(("uts", c), "dve", e_z)
                        if e_h2s:
                            addg(("uts", c), "qs", e_h2s)
                    q_lo, q_hi = lo, hi
                    for b_ in range(6):
                        is_q = b_ < 2
                        is_k = 2 <= b_ < 4
                        if is_q and q_hi <= q_lo:
                            continue
                        hh = b_ % 2
                        e_sl = dma(p, "sync",
                                   slab.rearrange("p (h c n) -> p h c n", h=2, n=512)[
                                       :, hh, :, :],
                                   cattnb.rearrange("(c p) n -> p c n", p=128)[
                                       :, :, b_ * 512:(b_ + 1) * 512],
                                   waits=[conv_done["cattnb"]] +
                                         guard(("slab", hh)), sem="qw1")
                        if is_q or is_k:
                            o0, o1_ = (q_lo, q_hi) if is_q else (0, w)
                            ww = o1_ - o0
                            for sub in range(4):
                                bank = sub % 2
                                for d_ in range(8):
                                    p.op("tensor",
                                         lambda e, sub=sub, d_=d_, bank=bank, hh=hh,
                                                o0=o0, ww=ww: e.matmul(
                                             pb[bank][:, 0:ww],
                                             slab[:, hh * 4096 + d_ * 512 + 128 * sub:
                                                  hh * 4096 + d_ * 512 + 128 * sub + 128],
                                             xln_a()[:, d_ * 512 + o0: d_ * 512 + o0 + ww],
                                             start=(d_ == 0), stop=(d_ == 7)),
                                         waits=([("dve", e_z), ("qw1", e_sl)] +
                                                guard(("bank", bank)) if d_ == 0 else []),
                                         inc=("pe" if d_ == 7 else None))
                                e_mm = p.now("pe")
                                gd = b_ * 512 + 128 * sub
                                e_ev = p.op("scalar",
                                            lambda e, ww=ww, bank=bank: e.copy(
                                                qtmp[:, 0:ww], pb[bank][:, 0:ww]),
                                            waits=[("pe", e_mm)] + guard(("qtmp",)),
                                            inc="act")
                                e_sq2 = p.op("vector",
                                             lambda e, ww=ww: e.tensor_mul(
                                                 qsqt[:, 0:ww], qtmp[:, 0:ww],
                                                 qtmp[:, 0:ww]),
                                             waits=[("act", e_ev)] + guard(("qsqt",)),
                                             inc="dve")
                                setg(("bank", bank), "act", e_ev)
                                e_ss = p.op("tensor",
                                            lambda e, ww=ww: e.matmul(
                                                pb[5][0:2, 0:ww], ones2blk[:, :],
                                                qsqt[:, 0:ww], start=True, stop=True),
                                            waits=[("dve", e_sq2)] + guard(("bank", 5)),
                                            inc="pe")
                                setg(("qsqt",), "pe", e_ss)
                                ch = (gd % 1024) // 128
                                tgt = qmaxs if is_q else kmaxs
                                e_rm = p.op("vector",
                                            lambda e, tgt=tgt, ch=ch, ww=ww: e.reduce_max(
                                                rr[0:2, 512 + ch:512 + ch + 1]
                                                if False else rr[0:2, 0:1],
                                                pb[5][0:2, 0:ww], axis=AXX),
                                            waits=[("pe", e_ss)] + guard(("rr",)),
                                            inc="dve")
                                setg(("bank", 5), "dve", e_rm)
                                e_rx = p.op("vector",
                                            lambda e, tgt=tgt, ch=ch: e.tensor_max(
                                                tgt[0:2, ch:ch + 1],
                                                tgt[0:2, ch:ch + 1], rr[0:2, 0:1]),
                                            waits=[("dve", e_rm)] + guard(("bmax",)),
                                            inc="dve")
                                setg(("rr",), "dve", e_rx)
                                setg(("bmax",), "dve", e_rx)
                                dstd = qI_d if is_q else kI_d
                                e_qs = dma(p, "sync",
                                           dstd[gd % 1024: gd % 1024 + 128,
                                                t0 + o0:t0 + o0 + ww],
                                           qtmp[:, 0:ww], waits=[("act", e_ev)],
                                           sem="qs")
                                setg(("qtmp",), "qs", e_qs)
                        else:
                            vh = b_ - 4
                            ntok = (w + 127) // 128
                            for s_ in range(ntok):
                                tw = min(128, w - s_ * 128)
                                bank = s_ % 2
                                for d_ in range(8):
                                    p.op("tensor",
                                         lambda e, s_=s_, d_=d_, bank=bank, hh=hh,
                                                tw=tw: e.matmul(
                                             pb[bank][0:tw, 0:512],
                                             xln_a()[:, d_ * 512 + s_ * 128:
                                                     d_ * 512 + s_ * 128 + tw],
                                             slab[:, hh * 4096 + d_ * 512:
                                                  hh * 4096 + d_ * 512 + 512],
                                             start=(d_ == 0), stop=(d_ == 7)),
                                         waits=([("qw1", e_sl), ("dve", e_z)] +
                                                guard(("bank", bank)) if d_ == 0 else []),
                                         inc=("pe" if d_ == 7 else None))
                                e_vm = p.now("pe")
                                e_ms = p.op("vector",
                                            lambda e, tw=tw: e.memset(
                                                vtmp[0:tw, 0:520], 1.0),
                                            waits=guard(("vtmp",)), inc="dve")
                                e_vc = p.op("vector",
                                            lambda e, tw=tw, bank=bank: e.tensor_copy(
                                                vtmp[0:tw, 0:520].rearrange(
                                                    "p (g x) -> p g x", x=65)[:, :, 0:64],
                                                pb[bank][0:tw, 0:512].rearrange(
                                                    "p (g x) -> p g x", x=64)),
                                            waits=[("dve", e_ms), ("pe", e_vm)],
                                            inc="dve")
                                setg(("bank", bank), "dve", e_vc)
                                kc_ = (t0 + s_ * 128) // 128
                                e_vst = None
                                for gi in range(4):
                                    jj_ = 4 * vh + gi
                                    e_vst = dma(p, "sync",
                                                vI_d[128 * jj_: 128 * jj_ + tw,
                                                     kc_ * 130: kc_ * 130 + 130],
                                                vtmp[0:tw, 130 * gi: 130 * gi + 130],
                                                waits=([("dve", e_vc)]
                                                       if gi == 0 else []),
                                                sem="qs")
                                setg(("vtmp",), "qs", e_vst)
                        setg(("slab", hh), "pe", p.now("pe"))
                    setg(("xln",), "pe", p.now("pe"))

                e_b1 = dma(p, "sync", bnc_d[:, 0:8], qmaxs[:, :],
                           waits=[("dve", p.now("dve"))])
                e_b2 = dma(p, "sync", bnc_d[:, 8:16], kmaxs[:, :])
                e_b3 = dma(p, "sync", bncs[0:1, 0:32],
                           bnc_d.rearrange("r c -> (r c)")[None, :],
                           waits=[("qd", e_b2)])
                e_lq = p.op("scalar",
                            lambda e: e.activation(bncs[0:1, 32:64], bncs[0:1, 0:32],
                                                   AF.Ln),
                            waits=[("qd", e_b3)], inc="act")
                e_cs = None
                for h_ in range(H):
                    par, ch = h_ % 2, h_ // 2
                    e_cs = p.op("vector",
                                lambda e, h_=h_, par=par, ch=ch: e.tensor_scalar(
                                    bncs[0:1, 64 + h_:64 + h_ + 1],
                                    bncs[0:1, 32 + par * 16 + ch:32 + par * 16 + ch + 1],
                                    bncs[0:1, 32 + par * 16 + 8 + ch:
                                         32 + par * 16 + 8 + ch + 1],
                                    0.5, ALU.add, ALU.mult),
                                waits=[("act", e_lq)], inc="dve")
                e_ex2 = p.op("scalar",
                             lambda e: e.activation(bncs[0:1, 80:96], bncs[0:1, 64:80],
                                                    AF.Exp),
                             waits=[("dve", e_cs)], inc="act")
                e_cb2 = p.op("vector",
                             lambda e: e.tensor_scalar_mul(cvalb[0:1, :],
                                                           bncs[0:1, 80:96], -0.125),
                             waits=[("act", e_ex2)], inc="dve")
                e_cbm2 = p.op("tensor",
                              lambda e: e.matmul(pb[7][:, 0:H], onesb[0:1, 0:128],
                                                 cvalb[0:1, :], start=True, stop=True),
                              waits=[("dve", e_cb2)] + guard(("bank", 7)), inc="pe")
                e_cbI = p.op("scalar", lambda e: e.copy(cbI[:, :], pb[7][:, 0:H]),
                             waits=[("pe", e_cbm2)], inc="act")
                setg(("bank", 7), "act", e_cbI)

            if phase_limit >= 5:
                e_cp = dma(p, "sync", wps.rearrange("p (c n) -> p c n", n=1024),
                           cprojb.rearrange("(c p) n -> p c n", p=128),
                           waits=[conv_done["cprojb"], ("pe", p.now("pe"))],
                           sem="qw1")

                def k_load_inner(j, jb, kext):
                    return dma(p, "sync", kbuf[:, jb * 4096: jb * 4096 + kext],
                               kI_d[128 * j:128 * j + 128, 0:kext],
                               waits=[("qs", 1)] + guard(("kbuf", jb)),
                               sem=f"qa{jb}")

                def inner_resid(ti, t0, w):
                    e_h2r = dma(p, "sync",
                                x3s.rearrange("p (c n) -> p c n", n=512)[:, :, 0:w],
                                h2_d.rearrange("(c p) n -> p c n", p=128)[:, :, t0:t0 + w],
                                waits=[("qs", 1)] + guard(("x3s",)), sem="qx")
                    e_x1r = dma(p, "sync",
                                xs.rearrange("p (c n) -> p c n", n=512)[:, :, 0:w],
                                x1_d.rearrange("(c p) n -> p c n", p=128)[:, :, t0:t0 + w],
                                waits=[("qs", 1)] + guard(("xs",)), sem="qx")
                    yts_done = p.now("dve")
                    for c in range(8):
                        bank = 2 + c % 2
                        for jj in range(8):
                            p.op("tensor",
                                 lambda e, c=c, jj=jj, bank=bank, w=w: e.matmul(
                                     pb[bank][:, 0:w],
                                     wps[:, jj * 1024 + 128 * c: jj * 1024 + 128 * c + 128],
                                     yts[:, jj * 512:jj * 512 + w],
                                     start=(jj == 0), stop=(jj == 7)),
                                 waits=([("qw1", e_cp), ("dve", yts_done)] +
                                        guard(("bank", bank)) if jj == 0 else []),
                                 inc=("pe" if jj == 7 else None))
                        e_pm = p.now("pe")
                        e_u = p.op("vector",
                                   lambda e, c=c, bank=bank, w=w: e.scalar_tensor_tensor(
                                       uts[:, c * 512:c * 512 + w], pb[bank][:, 0:w],
                                       1.0, x3s[:, c * 512:c * 512 + w],
                                       ALU.mult, ALU.add),
                                   waits=[("pe", e_pm), ("qx", e_h2r)] +
                                         guard(("uts", c)),
                                   inc="dve")
                        e_su = p.op("vector",
                                    lambda e, c=c, w=w: e.tensor_add(
                                        cvb[:, 0:w], uts[:, c * 512:c * 512 + w],
                                        xs[:, c * 512:c * 512 + w]),
                                    waits=[("dve", e_u), ("qx", e_x1r)] +
                                          guard(("x1t",)),
                                    inc="dve")
                        setg(("bank", bank), "dve", e_u)
                        e_sust = dma(p, "sync", su_d[128 * c:128 * c + 128, t0:t0 + w],
                                     cvb[:, 0:w], waits=[("dve", e_su)], sem="qs")
                        setg(("x1t",), "qs", e_sust)
                        setg(("uts", c), "dve", e_su)
                    setg(("xs",), "dve", p.now("dve"))
                    for j in range(8):
                        setg(("yts", j), "pe", p.now("pe"))
                    ln_tile(uts, w, xln_a(), ln2wb,
                            src_ready=[("dve", p.now("dve"))] + guard(("xln",)))
                    e_ul = p.now("dve")
                    e_us = None
                    for c in range(8):
                        e_us = dma(p, "sync", uln_d[128 * c:128 * c + 128, t0:t0 + w],
                                   xln_a()[:, c * 512:c * 512 + w],
                                   waits=[("dve", e_ul)], sem="qs")
                    setg(("xln",), "qs", e_us)
                    setg(("x3s",), "dve", p.now("dve"))
                    for c in range(8):
                        setg(("uts", c), "dve", e_ul)

                attention_block(t_own, qI_d, vI_d, k_load_inner, cbI,
                                lambda j, hb, ti: 2 * j + hb, inner_resid)

            bwaits = [(sn, p.now(sn)) for sn in
                      ["qd", "qg", "qg2", "qx", "qs", "qa0", "qa1", "qw1",
                       "qw2", "pe", "act", "dve", "pool"]]
            for engn in ENGS:
                p.op(engn, lambda e: e.nop(), waits=bwaits)
            G.clear()

            if phase_limit >= 6:
                e_rt = dma(p, "sync", rts.rearrange("p (c n) -> p c n", n=8),
                           routerb.rearrange("(c p) n -> p c n", p=128),
                           waits=[conv_done["routerb"]], sem="qw1")
                for ti, (t0, w) in enumerate(t_own):
                    e_ur = dma(p, "sync",
                               xln_a().rearrange("p (c n) -> p c n", n=512)[:, :, 0:w],
                               uln_d.rearrange("(c p) n -> p c n", p=128)[:, :, t0:t0 + w],
                               waits=[("qs", 1)] + guard(("xln",)), sem="qx")
                    for d_ in range(8):
                        p.op("tensor",
                             lambda e, d_=d_, w=w: e.matmul(
                                 pb[0][0:E, 0:w], rts[:, d_ * 8:(d_ + 1) * 8],
                                 xln_a()[:, d_ * 512:d_ * 512 + w],
                                 start=(d_ == 0), stop=(d_ == 7)),
                             waits=([("qx", e_ur), ("qw1", e_rt)] + guard(("bank", 0))
                                    if d_ == 0 else []),
                             inc=("pe" if d_ == 7 else None))
                    e_rmm = p.now("pe")
                    e_rl = p.op("scalar",
                                lambda e, w=w: e.copy(rls[0:E, 0:w], pb[0][0:E, 0:w]),
                                waits=[("pe", e_rmm)] + guard(("rls",)), inc="act")
                    setg(("bank", 0), "act", e_rl)
                    ntok = (w + 127) // 128
                    e_gst = None
                    for s_ in range(ntok):
                        tw = min(128, w - s_ * 128)
                        e_tp = p.op("tensor",
                                    lambda e, s_=s_, tw=tw: e.transpose(
                                        pb[1][0:tw, 0:E],
                                        rls[0:E, s_ * 128: s_ * 128 + tw],
                                        identf[0:E, 0:E]),
                                    waits=[("act", e_rl)] + guard(("bank", 1)),
                                    inc="pe")
                        e_lt = p.op("vector",
                                    lambda e, tw=tw: e.tensor_copy(ltt[0:tw, :],
                                                                   pb[1][0:tw, 0:E]),
                                    waits=[("pe", e_tp)] + guard(("ltt",)), inc="dve")
                        setg(("bank", 1), "dve", e_lt)
                        e_m1 = p.op("vector",
                                    lambda e, tw=tw: e.reduce_max(lmx[0:tw, 0:1],
                                                                  ltt[0:tw, :], axis=AXX),
                                    waits=[("dve", e_lt)], inc="dve")
                        e_k1 = p.op("vector",
                                    lambda e, tw=tw: e.tensor_scalar(
                                        gtm[0:tw, :], ltt[0:tw, :], lmx[0:tw, 0:1],
                                        None, ALU.is_equal),
                                    waits=[("dve", e_m1)] + guard(("gtm",)), inc="dve")
                        e_lm = p.op("vector",
                                    lambda e, tw=tw: e.scalar_tensor_tensor(
                                        ltt[0:tw, :], gtm[0:tw, :], NEG_BIG,
                                        ltt[0:tw, :], ALU.mult, ALU.add),
                                    waits=[("dve", e_k1)], inc="dve")
                        e_m2 = p.op("vector",
                                    lambda e, tw=tw: e.reduce_max(lmx[0:tw, 1:2],
                                                                  ltt[0:tw, :], axis=AXX),
                                    waits=[("dve", e_lm)], inc="dve")
                        e_df = p.op("vector",
                                    lambda e, tw=tw: e.tensor_sub(lmx[0:tw, 2:3],
                                                                  lmx[0:tw, 0:1],
                                                                  lmx[0:tw, 1:2]),
                                    waits=[("dve", e_m2)], inc="dve")
                        e_s1 = p.op("scalar",
                                    lambda e, tw=tw: e.activation(lmx[0:tw, 3:4],
                                                                  lmx[0:tw, 2:3],
                                                                  AF.Sigmoid),
                                    waits=[("dve", e_df)], inc="act")
                        e_s2 = p.op("vector",
                                    lambda e, tw=tw: e.tensor_scalar(
                                        lmx[0:tw, 4:5], lmx[0:tw, 3:4], -1.0, 1.0,
                                        ALU.mult, ALU.add),
                                    waits=[("act", e_s1)], inc="dve")
                        e_k2 = p.op("vector",
                                    lambda e, tw=tw: e.tensor_scalar(
                                        ltt[0:tw, :], ltt[0:tw, :], lmx[0:tw, 1:2],
                                        None, ALU.is_equal),
                                    waits=[("dve", e_s2)], inc="dve")
                        e_g1 = p.op("vector",
                                    lambda e, tw=tw: e.tensor_scalar_mul(
                                        gtm[0:tw, :], gtm[0:tw, :], lmx[0:tw, 3:4]),
                                    waits=[("dve", e_k2)], inc="dve")
                        e_g2 = p.op("vector",
                                    lambda e, tw=tw: e.tensor_scalar_mul(
                                        ltt[0:tw, :], ltt[0:tw, :], lmx[0:tw, 4:5]),
                                    waits=[("dve", e_g1)], inc="dve")
                        e_gf = p.op("vector",
                                    lambda e, tw=tw: e.tensor_add(gtm[0:tw, :],
                                                                  gtm[0:tw, :],
                                                                  ltt[0:tw, :]),
                                    waits=[("dve", e_g2)], inc="dve")
                        setg(("ltt",), "dve", e_gf)
                        e_gt = p.op("tensor",
                                    lambda e, tw=tw: e.transpose(
                                        pb[2][0:E, 0:tw], gtm[0:tw, :],
                                        identf[0:tw, 0:tw]),
                                    waits=[("dve", e_gf)] + guard(("bank", 2)),
                                    inc="pe")
                        e_g8 = p.op("vector",
                                    lambda e, tw=tw: e.tensor_copy(g8r[0:E, 0:tw],
                                                                   pb[2][0:E, 0:tw]),
                                    waits=[("pe", e_gt)] + guard(("g8r",)), inc="dve")
                        setg(("bank", 2), "dve", e_g8)
                        setg(("gtm",), "pe", e_gt)
                        e_gst = dma(p, "sync",
                                    g_d[:, t0 + s_ * 128: t0 + s_ * 128 + tw],
                                    g8r[0:E, 0:tw], waits=[("dve", e_g8)], sem="qs")
                        setg(("g8r",), "qs", e_gst)
                    setg(("rls",), "pe", p.now("pe"))
                    setg(("xln",), "pe", p.now("pe"))

            if phase_limit >= 7:
                for ti, (t0, w) in enumerate(t_moe):
                    ntc = (w + 127) // 128
                    e_ur = dma(p, "sync",
                               xln_a().rearrange("p (c n) -> p c n", n=512)[:, :, 0:w],
                               uln_d.rearrange("(c p) n -> p c n", p=128)[:, :, t0:t0 + w],
                               waits=guard(("xln",)) + [("qs", 1)], sem="qx")
                    e_gr = None
                    for ee in range(E):
                        e_gr = dma(p, "sync", gsr[0:1, ee * 384: ee * 384 + w],
                                   g_d[ee:ee + 1, t0:t0 + w],
                                   waits=(guard(("gsr",)) + [("qs", 1)]
                                          if ee == 0 else []), sem="qx")
                    e_su_r = None
                    for tc in range(ntc):
                        tw = min(128, w - tc * 128)
                        e_su_r = p.op("sync",
                                      lambda e, tc=tc, tw=tw, t0=t0: e.dma_start_transpose(
                                          sutm[0:tw, tc * 1024:(tc + 1) * 1024],
                                          su_d[:, t0 + tc * 128: t0 + tc * 128 + tw]),
                                      waits=[("qs", 1)] + guard(("sutm",)),
                                      inc="qx", inc_n=16)
                    e_gb = p.op("vector",
                                lambda e: e.tensor_copy(gsb[0:1, :], gsr[0:1, :]),
                                waits=[("qx", e_gr)] + guard(("gsb",)), inc="dve")
                    setg(("gsr",), "dve", e_gb)
                    e_ge_all = None
                    for ee in range(E):
                        e_gm = p.op("tensor",
                                    lambda e, ee=ee, w=w: e.matmul(
                                        pb[6][:, 0:w], onesb[0:1, 0:128],
                                        gsb[0:1, ee * 384: ee * 384 + w],
                                        start=True, stop=True),
                                    waits=[("dve", e_gb)] + guard(("bank", 6)),
                                    inc="pe")
                        e_ge = p.op("scalar",
                                    lambda e, ee=ee, w=w: e.copy(
                                        ges[:, ee * 384:ee * 384 + w], pb[6][:, 0:w]),
                                    waits=[("pe", e_gm)] + guard(("ges", ee)),
                                    inc="act")
                        setg(("bank", 6), "act", e_ge)
                        e_ge_all = e_ge
                    setg(("gsb",), "act", e_ge_all)
                    for bq in range(6):
                        p.op("tensor",
                             lambda e, bq=bq: e.matmul(
                                 pb[bq][:, 0:512], zerosb[:, :], xln_a()[:, 0:512],
                                 start=True, stop=False),
                             waits=[("qx", e_ur)] + guard(("bank", bq)),
                             inc="pe")
                    last_w2 = None
                    hs_ev = {}
                    w1_ev = {}
                    w2_ev = {}

                    def slab_load(sl):
                        hh = sl % 4
                        ee, f_ = sl // 16, sl % 16
                        w1_ev[sl] = dma(p, "sync",
                                        w1s[:, hh * 1024:(hh + 1) * 1024].rearrange(
                                            "p (c n) -> p c n", n=128),
                                        w1b.rearrange("(c p) n -> p c n", p=128)[
                                            :, :, ee * FF + f_ * 128:
                                            ee * FF + f_ * 128 + 128],
                                        waits=[conv_done["w1b"]] +
                                              guard(("w1s", hh)), sem=f"qm{hh}")
                        w2_ev[sl] = dma(p, "sync",
                                        w2s[:, hh * 1024:(hh + 1) * 1024],
                                        w2b[sl * 128:(sl + 1) * 128, :],
                                        waits=[conv_done["w2b"]] +
                                              guard(("w2s", hh)), sem=f"qn{hh}")

                    for pre in range(3):
                        slab_load(pre)
                    for sl in range(129):
                        if sl < 128:
                            ee, f_ = sl // 16, sl % 16
                            hh = sl % 4
                            e_w1 = w1_ev.pop(sl)
                            bank = 6 + sl % 2
                            for d_ in range(8):
                                p.op("tensor",
                                     lambda e, d_=d_, hh=hh, bank=bank, w=w: e.matmul(
                                         pb[bank][:, 0:w],
                                         w1s[:, hh * 1024 + d_ * 128:
                                             hh * 1024 + d_ * 128 + 128],
                                         xln_a()[:, d_ * 512: d_ * 512 + w],
                                         start=(d_ == 0), stop=(d_ == 7)),
                                     waits=([(f"qm{hh}", e_w1), ("qx", e_ur)] +
                                            guard(("bank", bank)) if d_ == 0 else []),
                                     inc=("pe" if d_ == 7 else None))
                            e_hm = p.now("pe")
                            setg(("w1s", hh), "pe", e_hm)
                            e_gl = p.op("scalar",
                                        lambda e, bank=bank, w=w: e.activation(
                                            qtmp[:, 0:w], pb[bank][:, 0:w], AF.Gelu),
                                        waits=[("pe", e_hm), ("act", e_ge_all)] +
                                              guard(("qtmp",)),
                                        inc="act")
                            setg(("bank", bank), "act", e_gl)
                            hslot = sl % 32
                            e_hs = p.op("vector",
                                        lambda e, hslot=hslot, ee=ee, w=w: e.tensor_mul(
                                            hsc[:, hslot * 384: hslot * 384 + w],
                                            qtmp[:, 0:w],
                                            ges[:, ee * 384: ee * 384 + w]),
                                        waits=[("act", e_gl)] + guard(("hsc", hslot)),
                                        inc="dve")
                            setg(("qtmp",), "dve", e_hs)
                            hs_ev[sl] = e_hs
                        sp = sl - 1
                        if sp >= 0:
                            hhp = sp % 4
                            hslotp = sp % 32
                            e_hsp = hs_ev.pop(sp)
                            e_w2p = w2_ev.pop(sp)
                            for tc in range(ntc):
                                tw = min(128, w - tc * 128)
                                for dh in range(2):
                                    bk = 2 * tc + dh
                                    ws_ = []
                                    if tc == 0 and dh == 0:
                                        ws_ = [(f"qn{hhp}", e_w2p), ("dve", e_hsp)]
                                    last_w2 = p.op("tensor",
                                                   lambda e, sp=sp, tc=tc, dh=dh, bk=bk,
                                                          hhp=hhp, hslotp=hslotp,
                                                          tw=tw: e.matmul(
                                                       pb[bk][0:tw, 0:512],
                                                       hsc[:, hslotp * 384 + 128 * tc:
                                                           hslotp * 384 + 128 * tc + tw],
                                                       w2s[:, hhp * 1024 + 512 * dh:
                                                           hhp * 1024 + 512 * dh + 512],
                                                       start=False,
                                                       stop=(sp == 127)),
                                                   waits=ws_,
                                                   inc=("pe" if (tc == ntc - 1 and
                                                                 dh == 1) else None))
                            setg(("w2s", hhp), "pe", p.now("pe"))
                            setg(("hsc", hslotp), "pe", p.now("pe"))
                        if sl + 3 < 128:
                            slab_load(sl + 3)
                    e_w2d = p.now("pe")
                    for ee in range(E):
                        setg(("ges", ee), "pe", e_w2d)
                    for tc in range(ntc):
                        tw = min(128, w - tc * 128)
                        for dh in range(2):
                            bk = 2 * tc + dh
                            e_o = p.op("vector",
                                       lambda e, tc=tc, dh=dh, bk=bk, tw=tw: e.scalar_tensor_tensor(
                                           outs[0:tw, 0:512], pb[bk][0:tw, 0:512],
                                           1.0,
                                           sutm[0:tw, tc * 1024 + 512 * dh:
                                                tc * 1024 + 512 * dh + 512],
                                           ALU.mult, ALU.add),
                                       waits=[("pe", e_w2d), ("qx", e_su_r)] +
                                             guard(("outs",)),
                                       inc="dve")
                            oc0 = t0 - own[0]
                            e_os = dma(p, "sync",
                                       out_tm[oc0 + tc * 128: oc0 + tc * 128 + tw,
                                              512 * dh: 512 * dh + 512],
                                       outs[0:tw, 0:512], waits=[("dve", e_o)],
                                       sem="qs")
                            setg(("outs",), "qs", e_os)
                    setg(("sutm",), "dve", p.now("dve"))
                    for bq in range(6):
                        setg(("bank", bq), "dve", p.now("dve"))
                    setg(("xln",), "pe", p.now("pe"))

            for name, shape in debug_taps:
                src = {"dbg_x1": x1_d, "dbg_h2": h2_d, "dbg_su": su_d,
                       "dbg_uln": uln_d, "dbg_g": g_d, "dbg_qO": qO_d,
                       "dbg_kO": kO_d, "dbg_qI": qI_d, "dbg_kI": kI_d,
                       "dbg_vO": vO_d, "dbg_vI": vI_d}.get(name)
                if src is None:
                    continue
                rows, cols = shape
                if cols == S:
                    cols = full[1]
                if rows == S:
                    rows = full[1]
                isbf = src.tensor.dtype == BF16
                for r0 in range(0, rows, 128):
                    rh = min(128, rows - r0)
                    for c0 in range(0, cols, 512):
                        cw = min(512, cols - c0)
                        if isbf:
                            e_l = dma(p, "sync", cvb[0:rh, 0:cw],
                                      src[r0:r0 + rh, c0:c0 + cw],
                                      waits=[("qs", 1)] + guard(("cv",)))
                            e_c = p.op("vector",
                                       lambda e, rh=rh, cw=cw: e.tensor_copy(
                                           cvf[0:rh, 0:cw], cvb[0:rh, 0:cw]),
                                       waits=[("qd", e_l)], inc="dve")
                            e_s = dma(p, "sync", dbg[name][r0:r0 + rh, c0:c0 + cw],
                                      cvf[0:rh, 0:cw], waits=[("dve", e_c)])
                        else:
                            e_l = dma(p, "sync", cvf[0:rh, 0:cw],
                                      src[r0:r0 + rh, c0:c0 + cw],
                                      waits=[("qs", 1)] + guard(("cv",)))
                            e_s = dma(p, "sync", dbg[name][r0:r0 + rh, c0:c0 + cw],
                                      cvf[0:rh, 0:cw], waits=[("qd", e_l)])
                        setg(("cv",), "qd", e_s)

            p.op("sync", lambda e: e.nop(),
                 waits=[(sn, p.now(sn)) for sn in Prog.DMA_SEMS])
            return p

        PA = build_role(0)
        PB = build_role(1)

        def emit_engine(name):
            def body(eng):
                for f in P0.q[name]:
                    f(eng, sems)
                eng.wait_ge(sems["qd"], e_role)
                with eng.register(f"role_{name}") as reg:
                    eng.reg_load(reg, role_s[0:1, 0:1])
                    with eng.If_eq(reg, 0):
                        for f in PA.q[name]:
                            f(eng, sems)
                    with eng.Else():
                        for f in PB.q[name]:
                            f(eng, sems)
            return body

        for name in ENGS:
            getattr(block, name)(emit_engine(name))

    return nc



_BUILD_CACHE = {}


def _get_nc(key="main", **kw):
    if key not in _BUILD_CACHE:
        _BUILD_CACHE[key] = build_program(**kw)
    return _BUILD_CACHE[key]


def _perm_slot():
    perm = np.zeros(D, dtype=np.int64)
    for j in range(8):
        for b_, h_ in ((0, EVENS[j]), (1, ODDS[j])):
            for d_ in range(HD):
                perm[128 * j + 64 * b_ + d_] = 64 * h_ + d_
    return perm


def make_in_maps(inputs, consts=None):
    x = np.ascontiguousarray(np.asarray(inputs["L0_attn_x"], dtype=np.float32))
    perm = _perm_slot()
    wqT = np.ascontiguousarray(np.asarray(inputs["L0_attn_Wq"], np.float32).T[:, perm])
    wkT = np.ascontiguousarray(np.asarray(inputs["L0_attn_Wk"], np.float32).T)
    wvT = np.ascontiguousarray(np.asarray(inputs["L0_attn_Wv"], np.float32).T)
    wpT = np.ascontiguousarray(np.asarray(inputs["L0_attn_Wproj"], np.float32).T[perm, :])
    cattn = np.ascontiguousarray(np.asarray(inputs["L0_ffn_c_attn_w"], np.float32))
    cproj = np.ascontiguousarray(np.asarray(inputs["L0_ffn_c_proj_w"], np.float32))
    router = np.ascontiguousarray(np.asarray(inputs["L0_ffn_router_w"], np.float32))
    w1 = np.ascontiguousarray(np.asarray(inputs["L0_ffn_w1"], np.float32))
    w2 = np.ascontiguousarray(np.asarray(inputs["L0_ffn_w2"], np.float32))
    gains = np.asarray(inputs["L0_attn_q_gain"], np.float32)
    gain_row = np.ascontiguousarray(gains[np.array(SLOT_HEADS)][None, :])
    gain_pairs = np.ascontiguousarray(
        np.stack([gains[np.array(EVENS)], gains[np.array(ODDS)]], axis=0))
    ln1w = np.ascontiguousarray(np.asarray(inputs["L0_ffn_ln1_w"], np.float32)[None, :])
    ln2w = np.ascontiguousarray(np.asarray(inputs["L0_ffn_ln2_w"], np.float32)[None, :])
    if consts is None:
        consts = _host_consts()
    in_maps = []
    for core in range(8):
        b_ = core // 2
        m = dict(
            xT=np.ascontiguousarray(x[b_].T),
            role=np.array([[core % 2]], dtype=np.int32),
            wqT=wqT, wkT=wkT, wvT=wvT, wpT=wpT, cattn=cattn, cproj=cproj,
            router=router, w1=w1, w2=w2, gain_row=gain_row,
            gain_pairs=gain_pairs, ln1w_row=ln1w, ln2w_row=ln2w,
        )
        m.update(consts)
        in_maps.append(m)
    return in_maps


def kernel(**inputs):
    nc = _get_nc("main")
    in_maps = make_in_maps(inputs)
    res = run_bass_kernel_spmd(nc, in_maps, core_ids=list(range(8)))
    out = np.empty((4, S, D), dtype=np.float32)
    for core in range(8):
        b_ = core // 2
        o = res.results[core]["out_tm"]
        if core % 2 == 0:
            out[b_, 0:SPLIT, :] = o
        else:
            out[b_, SPLIT:S, :] = o[0:S - SPLIT, :]
    return out
